# version 07
# baseline (speedup 1.0000x reference)
import math
import numpy as np
from contextlib import ExitStack

import concourse.bass as bass
import concourse.mybir as mybir
from concourse.bass_utils import run_bass_kernel_spmd

F32 = mybir.dt.float32
BF16 = mybir.dt.bfloat16
I32 = mybir.dt.int32
AF = mybir.ActivationFunctionType
ALU = mybir.AluOpType
AX = mybir.AxisListType

B, T, D = 4, 2048, 2048
NH, HD = 16, 128
DFF = 8192
KC = D // 128
FC = DFF // 128
NOWN = 1024
NG, GH = 4, 4

INV_SQRT_HD = float(np.float32(1.0 / math.sqrt(HD)))
LN_SQRT_HD = float(np.float32(math.log(math.sqrt(HD))))
LN_127 = float(np.float32(math.log(127.0)))
LN_2_SQHD = float(np.float32(math.log(2.0) - math.log(math.sqrt(HD))))


class Sem:
    __slots__ = ("h", "n")

    def __init__(self, h):
        self.h = h
        self.n = 0


class Emitter:

    def __init__(self, block):
        self.block = block
        self.buf = {e: [] for e in ("tensor", "vector", "scalar", "gpsimd", "sync")}
        self.water = {e: {} for e in self.buf}
        self.prog = {}

    def _push(self, eng, fn, waits, upd):
        agg = {}
        for s, v in waits:
            if s is None:
                continue
            assert v <= s.n, (eng, v, s.n, "wait on not-yet-planned sem value")
            if agg.get(id(s), (None, -1))[1] < v:
                agg[id(s)] = (s, v)
        wl = []
        wm = self.water[eng]
        for s, v in agg.values():
            if wm.get(id(s), -1) < v:
                wm[id(s)] = v
                wl.append((s.h, v))

        def closure(e, wl=wl, fn=fn, upd=upd):
            for h, v in wl:
                e.wait_ge(h, v)
            inst = fn(e)
            if upd is not None:
                inst.then_inc(upd[0], upd[1])

        self.buf[eng].append(closure)

    def chained(self, eng, fn, waits=()):
        ps = self.prog[eng]
        waits = [w for w in waits if w is not None]
        if ps.n > 0:
            self.water[eng].pop(id(ps), None)
            waits.append((ps, ps.n))
        ps.n += 1
        self._push(eng, fn, waits, (ps.h, 1))
        return (ps, ps.n)

    def plain(self, eng, fn, waits=(), inc=None):
        waits = [w for w in waits if w is not None]
        upd = None
        ret = None
        if inc is not None:
            s, k = inc
            s.n += k
            upd = (s.h, k)
            ret = (s, s.n)
        self._push(eng, fn, waits, upd)
        return ret

    def flush(self):
        for eng, ops in self.buf.items():
            if not ops:
                continue
            body = list(ops)
            ops.clear()

            def run(e, body=body):
                for f in body:
                    f(e)

            getattr(self.block, eng)(run)


def build_nc(detect_races=True, debug=False):
    nc = bass.Bass("TRN2", target_bir_lowering=False,
                   detect_race_conditions=detect_races)

    xt = nc.dram_tensor("xt", [D, T], F32, kind="ExternalInput").ap()
    wqt = nc.dram_tensor("wqt", [KC, 128, D], F32, kind="ExternalInput").ap()
    wkt = nc.dram_tensor("wkt", [KC, 128, D], F32, kind="ExternalInput").ap()
    wvt = nc.dram_tensor("wvt", [2 * NG, 128, KC * 256], F32, kind="ExternalInput").ap()
    wot = nc.dram_tensor("wot", [KC, 128, D], F32, kind="ExternalInput").ap()
    w1t = nc.dram_tensor("w1t", [FC, 128, D], F32, kind="ExternalInput").ap()
    w2t = nc.dram_tensor("w2t", [FC, 128, D], F32, kind="ExternalInput").ap()
    w3t = nc.dram_tensor("w3t", [KC, 128, DFF], F32, kind="ExternalInput").ap()
    wmt = nc.dram_tensor("wmt", [96, 128, D], F32, kind="ExternalInput").ap()
    cct = nc.dram_tensor("cct", [128, KC], F32, kind="ExternalInput").ap()
    rct = nc.dram_tensor("rct", [64, T], F32, kind="ExternalInput").ap()
    rst = nc.dram_tensor("rst", [64, T], F32, kind="ExternalInput").ap()
    outT = nc.dram_tensor("outT", [D, NOWN], F32, kind="ExternalOutput").ap()
    x2s = nc.dram_tensor("x2s", [D, NOWN], F32, kind="Internal").ap()
    if debug:
        dbg_hf = nc.dram_tensor("dbg_hf", [D, NOWN], F32, kind="ExternalOutput").ap()
        dbg_xq = nc.dram_tensor("dbg_xq", [D, NOWN], F32, kind="ExternalOutput").ap()
        dbg_thc = nc.dram_tensor("dbg_thc", [128, 12], F32, kind="ExternalOutput").ap()
        dbg_rows = nc.dram_tensor("dbg_rows", [128, 2 * NOWN], F32, kind="ExternalOutput").ap()
        dbg_hff = nc.dram_tensor("dbg_hff", [DFF, NOWN], F32, kind="ExternalOutput").ap()
        dbg_hq = nc.dram_tensor("dbg_hq", [DFF, NOWN], F32, kind="ExternalOutput").ap()
        dbg_rows3 = nc.dram_tensor("dbg_rows3", [128, 2 * NOWN], F32, kind="ExternalOutput").ap()
        dbg_nb = nc.dram_tensor("dbg_nb", [128, 3 * NH], F32, kind="ExternalOutput").ap()

    st = ExitStack()
    with st:
        sems = {}
        for name in ("vp", "sp", "gp", "tp"):
            sems[name] = Sem(st.enter_context(nc.semaphore(name)))
        VP, SP, GP, TP = (sems[k] for k in ("vp", "sp", "gp", "tp"))

        all_dma_sems = []

        def dsem(name):
            sm = Sem(st.enter_context(nc.semaphore(name)))
            all_dma_sems.append(sm)
            return sm

        blk = st.enter_context(nc.Block())
        E = Emitter(blk)
        E.prog = {"vector": VP, "scalar": SP, "gpsimd": GP}

        vop = lambda fn, waits=(): E.chained("vector", fn, waits)
        sop = lambda fn, waits=(): E.chained("scalar", fn, waits)
        gop = lambda fn, waits=(): E.chained("gpsimd", fn, waits)
        gdma = lambda sm, fn, waits=(): E.plain("gpsimd", fn, waits, inc=(sm, 16))
        ydma = lambda sm, fn, waits=(): E.plain("sync", fn, waits, inc=(sm, 16))

        s_cc, s_rc, s_rs = dsem("s_cc"), dsem("s_rc"), dsem("s_rs")
        s_x = [dsem("s_x0"), dsem("s_x1")]
        s_wm = [dsem("s_wm0"), dsem("s_wm1")]
        s_wq = [dsem("s_wq0"), dsem("s_wq1")]
        s_wv = dsem("s_wv")
        s_rope = dsem("s_rope")
        s_hv = dsem("s_hv")
        s_wo = [dsem("s_wo0"), dsem("s_wo1")]
        s_xo = [dsem("s_xo0"), dsem("s_xo1")]
        s_x2w = [dsem("s_x2w0"), dsem("s_x2w1")]
        s_x2r = [dsem("s_x2r0"), dsem("s_x2r1")]
        s_st = [dsem("s_st0"), dsem("s_st1"), dsem("s_st2"), dsem("s_st3")]
        s_w3 = [dsem("s_w30"), dsem("s_w31")]
        s_out = [dsem("s_out0"), dsem("s_out1")]
        s_dbg = dsem("s_dbg")

        def ddump(dst_ap, src_ap, waits=()):
            if not debug:
                return None
            return gdma(s_dbg, lambda g, dst_ap=dst_ap, src_ap=src_ap:
                        g.dma_start(out=dst_ap, in_=src_ap),
                        waits=list(waits) + [(s_dbg, s_dbg.n)])

        def mm(fn, waits=(), last=False):
            return E.plain("tensor", fn, waits, inc=(TP, 1) if last else None)

        ps = [st.enter_context(nc.psum_tensor(f"ps{i}", [128, 1024], F32))
              for i in range(4)]
        psfree = [None, None, None, None]

        def pswait(i):
            return [psfree[i]] if psfree[i] is not None else []

        def sb(stack, name, shape, dtype, side=None):
            kw = {"side": side} if side else {}
            return stack.enter_context(nc.sbuf_tensor(name, shape, dtype, **kw))

        ones128 = sb(st, "ones128", [128, 128], BF16)
        ones1f = sb(st, "ones1f", [1, 128], F32)
        ccolf = sb(st, "ccolf", [128, KC], F32)
        scol = sb(st, "scol", [128, KC], BF16)
        modfm = sb(st, "modfm", [128, 96], F32)
        capl1 = sb(st, "capl1", [128, KC], F32)
        cbpl1 = sb(st, "cbpl1", [128, KC], F32)
        ga2 = sb(st, "ga2", [128, KC], F32)
        gb2 = sb(st, "gb2", [128, KC], F32)
        gscr = sb(st, "gscr", [128, 3 * KC], F32)
        qmax = sb(st, "qmax", [128, NH], F32)
        kmax = sb(st, "kmax", [128, NH], F32)
        negB = sb(st, "negB", [128, NH], F32)
        bcol = sb(st, "bcol", [128, 8], F32)
        thc = sb(st, "thc", [128, 12], F32)
        constb = sb(st, "constb", [128, 7], F32)

        ones_d = vop(lambda v: v.memset(ones128[:], 1.0))
        ones1_d = vop(lambda v: v.memset(ones1f[:], 1.0))
        vop(lambda v: v.memset(constb[:, 0:1], 1e-5))
        vop(lambda v: v.memset(constb[:, 1:2], 1e-6))
        vop(lambda v: v.memset(constb[:, 2:3], -LN_SQRT_HD))
        vop(lambda v: v.memset(constb[:, 3:4], LN_127))
        vop(lambda v: v.memset(constb[:, 4:5], -0.5))
        vop(lambda v: v.memset(constb[:, 5:6], 0.5))
        cb_d = vop(lambda v: v.memset(constb[:, 6:7], LN_2_SQHD))
        ld_cc = gdma(s_cc, lambda g: g.dma_start(out=ccolf[:], in_=cct[:, :]))
        ld_rc = gdma(s_rc, lambda g: g.dma_start(out=cbT[:], in_=rct[:, :]))
        ld_rs = gdma(s_rs, lambda g: g.dma_start(out=sbT[:], in_=rst[:, :]))
        sc_done = sop(lambda s: s.activation(scol[:], ccolf[:], AF.Silu),
                      waits=[ld_cc])

        def halve(cur_fn, bufs, op, final_out, waits, n=NOWN):
            b0 = lambda lo, c: bufs[0][lo:lo + c, 0:n]
            b1 = lambda lo, c: bufs[1][lo:lo + c, 0:n]
            regions = [cur_fn, b0, b1]
            done = list(waits)
            p, lvl = 64, 0
            while p >= 1:
                cur = regions[(2 * lvl) % 3]
                tmp = regions[(2 * lvl + 1) % 3]
                nxt = regions[(2 * lvl + 2) % 3]
                d = gdma(s_hv, lambda g, cur=cur, tmp=tmp, p=p: g.dma_start(
                    out=tmp(0, p), in_=cur(p, p)), waits=done)
                if p > 1:
                    t = vop(lambda v, cur=cur, tmp=tmp, nxt=nxt, p=p:
                            v.tensor_tensor(nxt(0, p), cur(0, p), tmp(0, p),
                                            op), waits=[d])
                else:
                    t = vop(lambda v, cur=cur, tmp=tmp: v.tensor_tensor(
                        final_out, cur(0, 1), tmp(0, 1), op), waits=[d])
                done = [t]
                lvl += 1
                p //= 2
            return done[0]

        stA = ExitStack()
        hbuf = sb(stA, "hbuf", [128, KC * T], BF16)
        ybuf = sb(stA, "ybuf", [128, NH * NOWN], BF16)
        cbT = sb(stA, "cbT", [64, T], BF16)
        sbT = sb(stA, "sbT", [64, T], BF16)

        stB = ExitStack()
        xstr = sb(stB, "xstr", [128, 2 * T], BF16)
        xsq = sb(stB, "xsq", [128, 2 * T], BF16)
        mst = sb(stB, "mst", [128, T], F32)
        rstd = sb(stB, "rstd", [128, T], F32)
        uscr = sb(stB, "uscr", [128, T], F32)
        u2scr = sb(stB, "u2scr", [128, T], F32)
        wms = sb(stB, "wms", [128, 2 * D], BF16)

        xld = [None, None]
        sqd = [None, None]
        mm_x = [None, None]
        for kc in range(KC):
            pr = kc % 2
            xld[pr] = gdma(s_x[pr], lambda g, kc=kc, pr=pr: g.dma_start(
                out=xstr[:, pr * T:pr * T + T],
                in_=xt[kc * 128:(kc + 1) * 128, :]),
                waits=[mm_x[pr], sqd[pr]])
            sqd[pr] = vop(lambda v, pr=pr: v.tensor_tensor(
                xsq[:, pr * T:pr * T + T], xstr[:, pr * T:pr * T + T],
                xstr[:, pr * T:pr * T + T], ALU.mult), waits=[xld[pr]])
            for nb in range(4):
                pt, sl = ps[nb // 2], slice((nb % 2) * 512, (nb % 2) * 512 + 512)
                mm(lambda t, kc=kc, pr=pr, pt=pt, sl=sl, nb=nb: t.matmul(
                    pt[:, sl], ones128[:, :],
                    xstr[:, pr * T + nb * 512: pr * T + nb * 512 + 512],
                    start=(kc == 0), stop=(kc == KC - 1)),
                    waits=[xld[pr], ones_d] + (pswait(nb // 2) if kc == 0 else []))
            for nb in range(4):
                pt, sl = ps[2 + nb // 2], slice((nb % 2) * 512, (nb % 2) * 512 + 512)
                mm_x[pr] = mm(lambda t, kc=kc, pr=pr, pt=pt, sl=sl, nb=nb: t.matmul(
                    pt[:, sl], ones128[:, :],
                    xsq[:, pr * T + nb * 512: pr * T + nb * 512 + 512],
                    start=(kc == 0), stop=(kc == KC - 1)),
                    waits=[sqd[pr]] + (pswait(2 + nb // 2) if kc == 0 else []),
                    last=True)
        stats_done = mm_x[(KC - 1) % 2]

        for hh in range(2):
            vop(lambda v, hh=hh: v.tensor_scalar(
                mst[:, hh * 1024:hh * 1024 + 1024], ps[hh][:, :], 1.0 / D, None,
                ALU.mult), waits=[stats_done])
            vop(lambda v, hh=hh: v.tensor_tensor(
                uscr[:, hh * 1024:hh * 1024 + 1024],
                mst[:, hh * 1024:hh * 1024 + 1024],
                mst[:, hh * 1024:hh * 1024 + 1024], ALU.mult))
            vop(lambda v, hh=hh: v.scalar_tensor_tensor(
                u2scr[:, hh * 1024:hh * 1024 + 1024], ps[2 + hh][:, :], 1.0 / D,
                uscr[:, hh * 1024:hh * 1024 + 1024], ALU.mult, ALU.subtract))
        var_done = (VP, VP.n)
        lnv = sop(lambda s: s.activation(uscr[:, :], u2scr[:, :], AF.Ln,
                                         bias=constb[:, 0:1]),
                  waits=[var_done, cb_d])
        rstd_done = sop(lambda s: s.activation(rstd[:, :], uscr[:, :], AF.Exp,
                                               scale=-0.5))
        psfree[0] = psfree[1] = var_done
        psfree[2] = psfree[3] = var_done

        wld = [None, None]
        mm_wm = [None, None]
        first_mod = [True]
        for dc in range(96):
            pr = dc % 2
            wld[pr] = gdma(s_wm[pr], lambda g, dc=dc, pr=pr: g.dma_start(
                out=wms[:, pr * D:pr * D + D], in_=wmt[dc]),
                waits=[sc_done, mm_wm[pr]])
            for kc in range(KC):
                w2 = [wld[pr]] if kc == 0 else []
                if kc == 0 and first_mod[0]:
                    w2 += [sc_done] + pswait(2)
                    first_mod[0] = False
                mm_wm[pr] = mm(lambda t, dc=dc, pr=pr, kc=kc: t.matmul(
                    ps[2][:, dc:dc + 1],
                    wms[:, pr * D + kc * 128: pr * D + kc * 128 + 128],
                    scol[:, kc:kc + 1],
                    start=(kc == 0), stop=(kc == KC - 1)),
                    waits=w2, last=(kc == KC - 1))
        mod_mm = mm_wm[1]
        mod_ev = vop(lambda v: v.tensor_copy(modfm[:], ps[2][:, 0:96]),
                     waits=[mod_mm])
        psfree[2] = mod_ev
        capl_d = vop(lambda v: v.tensor_scalar(capl1[:], modfm[:, 16:32], 1.0,
                                               None, ALU.add))
        cbpl_d = vop(lambda v: v.tensor_scalar(cbpl1[:], modfm[:, 64:80], 1.0,
                                               None, ALU.add))
        gate_d = []
        for dst, c0 in ((ga2, 32), (gb2, 80)):
            vop(lambda v, c0=c0: v.tensor_single_scalar(
                gscr[:, 0:KC], modfm[:, c0:c0 + KC], 0.0, ALU.is_ge))
            vop(lambda v: v.tensor_scalar(
                gscr[:, KC:2 * KC], gscr[:, 0:KC], 2.0, -1.0, ALU.mult, ALU.add))
            ab = sop(lambda s, c0=c0: s.activation(
                gscr[:, 0:KC], modfm[:, c0:c0 + KC], AF.Abs),
                waits=[mod_ev, (VP, VP.n)])
            vop(lambda v: v.tensor_single_scalar(
                gscr[:, 2 * KC:3 * KC], gscr[:, 0:KC], 0.25, ALU.max),
                waits=[ab])
            gate_d.append(vop(lambda v, dst=dst: v.tensor_tensor(
                dst[:], gscr[:, KC:2 * KC], gscr[:, 2 * KC:3 * KC], ALU.mult)))
        ga_done, gb_done = gate_d

        h_ld = [None, None]
        h_done = [None, None]
        for kc in range(KC):
            pr = kc % 2
            h_ld[pr] = gdma(s_x[pr], lambda g, kc=kc, pr=pr: g.dma_start(
                out=xstr[:, pr * T:pr * T + T],
                in_=xt[kc * 128:(kc + 1) * 128, :]),
                waits=[mm_x[pr], sqd[pr], h_done[pr]])
            vop(lambda v, pr=pr: v.tensor_tensor(
                uscr[:, :], xstr[:, pr * T:pr * T + T], mst[:, :], ALU.subtract),
                waits=[h_ld[pr], rstd_done, capl_d, mod_ev])
            vop(lambda v: v.tensor_tensor(u2scr[:, :], uscr[:, :], rstd[:, :],
                                          ALU.mult))
            h_done[pr] = vop(lambda v, kc=kc: v.tensor_scalar(
                hbuf[:, kc * T:(kc + 1) * T], u2scr[:, :],
                capl1[:, kc:kc + 1], modfm[:, kc:kc + 1], ALU.mult, ALU.add))
        h_all = h_done[(KC - 1) % 2]

        E.flush()
        stB.close()

        stP = ExitStack()
        stgP = sb(stP, "stgP", [128, 4 * 512], F32, side="right")
        wsum = sb(stP, "wsum", [128, 800], F32, side="right")
        dumpP = sb(stP, "dumpP", [128, 512], BF16, side="right")

        pre_steps = []
        pre_slot_free = [None, None, None, None]
        pre_red_last = [None]

        def _pre_chunk(src_ap, col, slot, eng):
            def emit(src_ap=src_ap, col=col, slot=slot, eng=eng):
                ld = gdma(s_st[slot], lambda g: g.dma_start(
                    out=stgP[:, slot * 512:slot * 512 + 512], in_=src_ap),
                    waits=[pre_slot_free[slot]])
                if eng == "v":
                    red = vop(lambda v: v.reduce_sum(
                        wsum[:, col:col + 1],
                        stgP[:, slot * 512:slot * 512 + 512], AX.X,
                        apply_absolute_value=True), waits=[ld])
                else:
                    red = sop(lambda s: s.activation(
                        dumpP[:, :], stgP[:, slot * 512:slot * 512 + 512],
                        AF.Abs, accum_out=wsum[:, col:col + 1]), waits=[ld])
                pre_slot_free[slot] = red
                pre_red_last[0] = red
            return emit

        ci = 0
        for oc in range(FC):
            for qq in range(4):
                pre_steps.append(_pre_chunk(
                    w1t[oc][:, qq * 512:(qq + 1) * 512], 0 + 4 * oc + qq,
                    ci % 4, "v" if ci % 2 == 0 else "s"))
                ci += 1
                pre_steps.append(_pre_chunk(
                    w2t[oc][:, qq * 512:(qq + 1) * 512], 256 + 4 * oc + qq,
                    ci % 4, "v" if ci % 2 == 0 else "s"))
                ci += 1
        for tt3 in range(KC):
            for qq in range(16):
                pre_steps.append(_pre_chunk(
                    w3t[tt3][:, qq * 512:(qq + 1) * 512], 512 + 16 * tt3 + qq,
                    ci % 4, "v" if ci % 2 == 0 else "s"))
                ci += 1
        pre_pos = [0]

        def pump_pre(n):
            k = 0
            while k < n and pre_pos[0] < len(pre_steps):
                pre_steps[pre_pos[0]]()
                pre_pos[0] += 1
                k += 1

        stC = ExitStack()
        Qs = sb(stC, "Qs", [128, NH * NOWN], BF16)
        Kg = sb(stC, "Kg", [128, 2 * T], BF16)
        Vg = sb(stC, "Vg", [128, KC * 512], BF16)
        PT = sb(stC, "PT", [128, 2 * NOWN], BF16)
        wstg = sb(stC, "wstg", [128, 2 * D], BF16)
        wstgv = sb(stC, "wstgv", [128, KC * 256], BF16)
        qf = sb(stC, "qf", [64, NOWN], F32)
        lnscr = sb(stC, "lnscr", [128, NOWN], F32)
        rcp = sb(stC, "rcp", [128, NOWN], F32)

        state = {"wstg": None, "qf": None, "ropetmp": None, "pt": None,
                 "vstg": None, "kg": None, "vg": None}

        def rope(dst_ap, psrc, tok0, ntok, waits):
            ev = sop(lambda s: s.copy(qf[:, 0:ntok], psrc),
                     waits=list(waits) + [state["qf"]])
            d1 = ydma(s_rope, lambda sy: sy.dma_start(out=rcp[0:32, 0:ntok],
                                              in_=qf[32:64, 0:ntok]),
                      waits=[ev, state["ropetmp"]])
            d2 = ydma(s_rope, lambda sy: sy.dma_start(out=rcp[32:64, 0:ntok],
                                              in_=qf[0:32, 0:ntok]),
                      waits=[ev, state["ropetmp"]])
            a = vop(lambda v: v.tensor_tensor(
                lnscr[0:64, 0:ntok], qf[:, 0:ntok],
                cbT[:, tok0:tok0 + ntok], ALU.mult),
                waits=[ev, ld_rc, d1, d2, state["ropetmp"]])
            b = vop(lambda v: v.tensor_tensor(
                qf[:, 0:ntok], rcp[0:64, 0:ntok],
                sbT[:, tok0:tok0 + ntok], ALU.mult), waits=[ld_rs])
            add = vop(lambda v: v.tensor_tensor(
                dst_ap, lnscr[0:64, 0:ntok], qf[:, 0:ntok], ALU.add))
            state["qf"] = add
            state["ropetmp"] = add
            return add, ev

        q_done = [None, None]
        wfree = [None, None]
        for oc in range(NH):
            pr = oc % 2
            ld = gdma(s_wq[pr], lambda g, oc=oc, pr=pr: g.dma_start(
                out=wstg[:, pr * D:pr * D + D], in_=wqt[oc]),
                waits=[h_all, wfree[pr]])
            lastmm = None
            for kc in range(KC):
                for nb in range(2):
                    lastmm = mm(lambda t, oc=oc, kc=kc, nb=nb, pr=pr: t.matmul(
                        ps[pr][:, nb * 512:nb * 512 + 512],
                        wstg[:, pr * D + kc * 128: pr * D + kc * 128 + 128],
                        hbuf[:, kc * T + nb * 512: kc * T + nb * 512 + 512],
                        start=(kc == 0), stop=(kc == KC - 1)),
                        waits=([ld, h_all] + pswait(pr))
                        if (kc == 0 and nb == 0) else [],
                        last=(kc == KC - 1 and nb == 1))
            wfree[pr] = lastmm
            ev2 = sop(lambda s, oc=oc, pr=pr: s.copy(
                Qs[64:128, oc * NOWN:(oc + 1) * NOWN], ps[pr][64:128, :]),
                waits=[lastmm])
            state["qev2"] = ev2
            rp, q_ev = rope(Qs[0:64, oc * NOWN:(oc + 1) * NOWN],
                            ps[pr][0:64, :], 0, NOWN, [lastmm])
            q_done[pr] = rp
            psfree[pr] = q_ev
            sq = vop(lambda v, oc=oc: v.tensor_tensor(
                PT[:, 0:NOWN], Qs[:, oc * NOWN:(oc + 1) * NOWN],
                Qs[:, oc * NOWN:(oc + 1) * NOWN], ALU.mult),
                waits=[rp, ev2, state["pt"]])
            mqn = None
            for nb in range(2):
                mqn = mm(lambda t, nb=nb: t.matmul(
                    ps[2][:, nb * 512:nb * 512 + 512], ones128[:, :],
                    PT[:, nb * 512:nb * 512 + 512], start=True, stop=True),
                    waits=([sq] + pswait(2)) if nb == 0 else [], last=(nb == 1))
            red = vop(lambda v, oc=oc: v.reduce_max(
                qmax[:, oc:oc + 1], ps[2][:, :], AX.X), waits=[mqn])
            state["pt"] = red
            psfree[2] = red
            pump_pre(8)

        for g in range(NG):
            v_ev = None
            for sub in range(2):
                vld = gdma(s_wv, lambda g_, g2=g, sub=sub: g_.dma_start(
                    out=wstgv[:], in_=wvt[2 * g2 + sub]),
                    waits=[h_all, state["vstg"]])
                for tc in range(KC):
                    pr = tc % 2
                    lastmm = None
                    for kc in range(KC):
                        lastmm = mm(lambda t, tc=tc, kc=kc, pr=pr: t.matmul(
                            ps[pr][:, 0:256],
                            hbuf[:, kc * T + tc * 128: kc * T + tc * 128 + 128],
                            wstgv[:, kc * 256:(kc + 1) * 256],
                            start=(kc == 0), stop=(kc == KC - 1)),
                            waits=([vld] + pswait(pr) +
                                   ([state["vg"]] if (tc == 0 and sub == 0)
                                    else []))
                            if kc == 0 else [], last=(kc == KC - 1))
                    v_ev = sop(lambda s, tc=tc, pr=pr, sub=sub: s.copy(
                        Vg[:, tc * 512 + sub * 256:tc * 512 + sub * 256 + 256],
                        ps[pr][:, 0:256]), waits=[lastmm])
                    psfree[pr] = v_ev
                state["vstg"] = v_ev
            vg_done = v_ev

            for pair in range(2):
                kg_ready = [[], []]
                nb_edge = [None, None]
                for j2 in range(2):
                    h = g * GH + pair * 2 + j2
                    jg = pair * 2 + j2
                    pp = j2 * 2
                    kpr = h % 2
                    ld = gdma(s_wq[kpr], lambda g_, h=h, kpr=kpr: g_.dma_start(
                        out=wstg[:, kpr * D:kpr * D + D], in_=wkt[h]),
                        waits=[wfree[kpr]])
                    lastmm = None
                    for kc in range(KC):
                        for nb in range(4):
                            pt = ps[pp + nb // 2]
                            sl = slice((nb % 2) * 512, (nb % 2) * 512 + 512)
                            lastmm = mm(lambda t, kc=kc, nb=nb, pt=pt, sl=sl,
                                        kpr=kpr: t.matmul(
                                            pt[:, sl],
                                            wstg[:, kpr * D + kc * 128:
                                                 kpr * D + kc * 128 + 128],
                                            hbuf[:, kc * T + nb * 512:
                                                 kc * T + nb * 512 + 512],
                                            start=(kc == 0),
                                            stop=(kc == KC - 1)),
                                        waits=([ld] + pswait(pp) +
                                               pswait(pp + 1) + [state["kg"]])
                                        if (kc == 0 and nb == 0) else [],
                                        last=(kc == KC - 1 and nb == 3))
                    wfree[kpr] = lastmm
                    ev2a = sop(lambda s, j2=j2, pp=pp: s.copy(
                        Kg[64:128, j2 * T:j2 * T + 1024], ps[pp][64:128, :]),
                        waits=[lastmm, state["kg"]])
                    ev2b = sop(lambda s, j2=j2, pp=pp: s.copy(
                        Kg[64:128, j2 * T + 1024:j2 * T + 2048],
                        ps[pp + 1][64:128, :]))
                    r1, k_ev1 = rope(Kg[0:64, j2 * T:j2 * T + 1024],
                                     ps[pp][0:64, :],
                                     0, 1024, [lastmm, state["kg"]])
                    r2, k_ev2 = rope(Kg[0:64, j2 * T + 1024:j2 * T + 2048],
                                     ps[pp + 1][0:64, :], 1024, 1024, [])
                    psfree[pp] = k_ev1
                    psfree[pp + 1] = k_ev2
                    kg_ready[j2] = [r1, r2, ev2a, ev2b]
                    sq1 = vop(lambda v, j2=j2: v.tensor_tensor(
                        PT[:, 0:1024], Kg[:, j2 * T:j2 * T + 1024],
                        Kg[:, j2 * T:j2 * T + 1024], ALU.mult),
                        waits=[r1, ev2a, state["pt"]])
                    sq2 = vop(lambda v, j2=j2: v.tensor_tensor(
                        PT[:, 1024:2048], Kg[:, j2 * T + 1024:j2 * T + 2048],
                        Kg[:, j2 * T + 1024:j2 * T + 2048], ALU.mult),
                        waits=[r2, ev2b])
                    tg = 2 - pp
                    mk = None
                    for nb in range(4):
                        pt = ps[tg + nb // 2]
                        sl = slice((nb % 2) * 512, (nb % 2) * 512 + 512)
                        mk = mm(lambda t, nb=nb, pt=pt, sl=sl: t.matmul(
                            pt[:, sl], ones128[:, :],
                            PT[:, nb * 512:nb * 512 + 512],
                            start=True, stop=True),
                            waits=([sq1, sq2] + pswait(tg) + pswait(tg + 1))
                            if nb == 0 else [], last=(nb == 3))
                    state["ptmk"] = mk
                    rk1 = vop(lambda v, tg=tg: v.reduce_max(
                        bcol[:, 0:1], ps[tg][:, :], AX.X), waits=[mk])
                    rk2 = vop(lambda v, tg=tg: v.reduce_max(
                        bcol[:, 1:2], ps[tg + 1][:, :], AX.X))
                    rk3 = vop(lambda v, h=h: v.tensor_tensor(
                        kmax[:, h:h + 1], bcol[:, 0:1], bcol[:, 1:2],
                        ALU.max))
                    psfree[tg] = rk1
                    psfree[tg + 1] = rk2
                    state["pt"] = rk3
                    u = vop(lambda v, h=h: v.tensor_tensor(
                        bcol[:, 2:3], qmax[:, h:h + 1], kmax[:, h:h + 1],
                        ALU.mult))
                    l1 = sop(lambda s: s.activation(bcol[:, 3:4],
                                                    bcol[:, 2:3], AF.Ln),
                             waits=[u])
                    l2 = sop(lambda s: s.activation(bcol[:, 4:5],
                                                    bcol[:, 3:4], AF.Exp,
                                                    scale=0.5,
                                                    bias=constb[:, 2:3]),
                             waits=[cb_d])
                    nb_edge[j2] = vop(lambda v, h=h: v.tensor_scalar(
                        negB[:, h:h + 1], bcol[:, 4:5], -1.0, -1.0,
                        ALU.mult, ALU.add), waits=[l2])
                    pump_pre(24)

                for j2 in range(2):
                    h = g * GH + pair * 2 + j2
                    jg = pair * 2 + j2
                    head_w = kg_ready[j2] + [q_done[0], q_done[1],
                                             state["qev2"]]
                    exp_w0 = [nb_edge[j2], state["ptmk"], state["pt"]]
                    prev_av = None
                    prev_sig = None
                    exd = [[None, None], [None, None]]

                    def st_mm(kc):
                        pr = kc % 2
                        w = (head_w if kc == 0 else []) + pswait(pr)
                        for nb in range(2):
                            smm = mm(lambda t, h=h, j2=j2, kc=kc, nb=nb, pr=pr:
                                     t.matmul(
                                         ps[pr][:, nb * 512:nb * 512 + 512],
                                         Kg[:, j2 * T + kc * 128:
                                            j2 * T + kc * 128 + 128],
                                         Qs[:, h * NOWN + nb * 512:
                                            h * NOWN + nb * 512 + 512],
                                         start=True, stop=True),
                                     waits=w if nb == 0 else [],
                                     last=True)
                            ex = sop(lambda s, kc=kc, pr=pr, h=h, nb=nb:
                                     s.activation(
                                         PT[:, pr * NOWN + nb * 512:
                                            pr * NOWN + nb * 512 + 512],
                                         ps[pr][:, nb * 512:nb * 512 + 512],
                                         AF.Exp, bias=negB[:, h:h + 1],
                                         scale=INV_SQRT_HD),
                                     waits=[smm] +
                                     (exp_w0 if (kc == 0 and nb == 0) else []))
                            exd[pr][nb] = ex
                        psfree[pr] = exd[pr][1]

                    def av_sig(kc):
                        nonlocal prev_av, prev_sig
                        pr = kc % 2
                        for nb in range(2):
                            prev_av = mm(
                                lambda t, kc=kc, jg=jg, nb=nb, pr=pr: t.matmul(
                                    ps[2][:, nb * 512:nb * 512 + 512],
                                    Vg[:, kc * 512 + jg * 128:
                                       kc * 512 + jg * 128 + 128],
                                    PT[:, pr * NOWN + nb * 512:
                                       pr * NOWN + nb * 512 + 512],
                                    start=(kc == 0), stop=(kc == KC - 1)),
                                waits=([exd[pr][nb], vg_done] + pswait(2))
                                if (kc == 0 and nb == 0) else [exd[pr][nb]],
                                last=(kc == KC - 1 and nb == 1))
                            prev_sig = mm(
                                lambda t, kc=kc, nb=nb, pr=pr: t.matmul(
                                    ps[3][:, nb * 512:nb * 512 + 512],
                                    ones128[:, :],
                                    PT[:, pr * NOWN + nb * 512:
                                       pr * NOWN + nb * 512 + 512],
                                    start=(kc == 0), stop=(kc == KC - 1)),
                                waits=pswait(3)
                                if (kc == 0 and nb == 0) else [],
                                last=(kc == KC - 1 and nb == 1))

                    st_mm(0)
                    for kc in range(KC):
                        if kc + 1 < KC:
                            st_mm(kc + 1)
                        av_sig(kc)
                    ln1 = sop(lambda s: s.activation(lnscr[:, :], ps[3][:, :],
                                                     AF.Ln),
                              waits=[prev_sig, state["ropetmp"]])
                    ln2 = sop(lambda s: s.activation(rcp[:, :], lnscr[:, :],
                                                     AF.Exp, scale=-1.0))
                    yv = vop(lambda v, h=h: v.tensor_tensor(
                        ybuf[:, h * NOWN:(h + 1) * NOWN], ps[2][:, :],
                        rcp[:, :], ALU.mult), waits=[ln2, prev_av])
                    psfree[2] = yv
                    psfree[3] = ln1
                    state["ropetmp"] = yv
                    state["kg"] = yv
                    state["vg"] = yv
                    pump_pre(24)
        y_all = state["vg"]

        if debug:
            ddump(dbg_nb[:, 0:NH], negB[:, :], waits=[(VP, VP.n), (SP, SP.n)])
            ddump(dbg_nb[:, NH:2 * NH], qmax[:, :], waits=[])
            ddump(dbg_nb[:, 2 * NH:3 * NH], kmax[:, :], waits=[])
        pump_pre(10 ** 9)

        def wscale(col0, ncols, nelem, c_sw, c_th, c_nth, swmul, waits):
            red = vop(lambda v: v.reduce_sum(
                bcol[:, 5:6], wsum[:, col0:col0 + ncols], AX.X), waits=waits)
            sm = halve(lambda lo, c: bcol[lo:lo + c, 5:6],
                       (bcol[:, 6:7], bcol[:, 7:8]),
                       ALU.add, bcol[0:1, 0:1], [red], n=1)
            vop(lambda v, nelem=nelem: v.tensor_scalar(
                bcol[0:1, 1:2], bcol[0:1, 0:1], 1.0 / nelem, None,
                ALU.mult), waits=[sm])
            u2 = vop(lambda v: v.tensor_single_scalar(
                bcol[0:1, 2:3], bcol[0:1, 1:2], 1e-5, ALU.max))
            m1 = mm(lambda t: t.matmul(
                ps[3][:, 0:1], ones1f[:, :],
                bcol[0:1, 2:3], start=True, stop=True),
                waits=[u2, ones1_d] + pswait(3), last=True)
            vop(lambda v, c_sw=c_sw, swmul=swmul: v.tensor_scalar(
                thc[:, c_sw:c_sw + 1], ps[3][:, 0:1], swmul, None, ALU.mult),
                waits=[m1])
            vop(lambda v, c_th=c_th: v.tensor_scalar(
                thc[:, c_th:c_th + 1], ps[3][:, 0:1], 0.5, None, ALU.mult))
            c3 = vop(lambda v, c_nth=c_nth: v.tensor_scalar(
                thc[:, c_nth:c_nth + 1], ps[3][:, 0:1], -0.5, None, ALU.mult))
            psfree[3] = c3
            return c3

        th1_d = wscale(0, 256, DFF * D, 0, 1, 2, -0.5,
                       [pre_red_last[0]] + [x for x in pre_slot_free])
        th2_d = wscale(256, 256, DFF * D, 3, 4, 5, -0.5, [])
        th3_d = wscale(512, 256, D * DFF, 6, 7, 8, -0.5, [])
        if debug:
            ddump(dbg_thc[:, 0:9], thc[:, 0:9], waits=[th3_d])

        E.flush()
        stC.close()
        stP.close()

        stD = ExitStack()
        wstgo = sb(stD, "wstgo", [128, 2 * D], BF16)
        xown = sb(stD, "xown", [128, 2 * NOWN], F32)
        x2stg = sb(stD, "x2stg", [128, 2 * NOWN], F32)

        oc_st = [None, None]
        ev_d = [None, None]
        wo_mm = [None, None]
        for oc in range(KC):
            pr = oc % 2
            ld = gdma(s_wo[pr], lambda g, oc=oc, pr=pr: g.dma_start(
                out=wstgo[:, pr * D:pr * D + D], in_=wot[oc]),
                waits=[y_all, wo_mm[pr]])
            xo = ydma(s_xo[pr], lambda sy, oc=oc, pr=pr: sy.dma_start(
                out=xown[:, pr * NOWN:pr * NOWN + NOWN],
                in_=xt[oc * 128:(oc + 1) * 128, 0:NOWN]),
                waits=[ev_d[pr]] + ([y_all] if oc < 2 else []))
            lastmm = None
            for kc in range(KC):
                for nb in range(2):
                    lastmm = mm(lambda t, oc=oc, kc=kc, nb=nb, pr=pr: t.matmul(
                        ps[pr][:, nb * 512:nb * 512 + 512],
                        wstgo[:, pr * D + kc * 128: pr * D + kc * 128 + 128],
                        ybuf[:, kc * NOWN + nb * 512: kc * NOWN + nb * 512 + 512],
                        start=(kc == 0), stop=(kc == KC - 1)),
                        waits=([ld, y_all] + pswait(pr))
                        if (kc == 0 and nb == 0) else [],
                        last=(kc == KC - 1 and nb == 1))
            wo_mm[pr] = lastmm
            ev_d[pr] = vop(lambda v, oc=oc, pr=pr: v.scalar_tensor_tensor(
                x2stg[:, pr * NOWN:pr * NOWN + NOWN], ps[pr][:, :],
                ga2[:, oc:oc + 1], xown[:, pr * NOWN:pr * NOWN + NOWN],
                ALU.mult, ALU.add), waits=[lastmm, xo, ga_done, oc_st[pr]])
            psfree[pr] = ev_d[pr]
            oc_st[pr] = ydma(s_x2w[pr], lambda sy, oc=oc, pr=pr: sy.dma_start(
                out=x2s[oc * 128:(oc + 1) * 128, :],
                in_=x2stg[:, pr * NOWN:pr * NOWN + NOWN]), waits=[ev_d[pr]])
        x2_written = [oc_st[0], oc_st[1]]

        E.flush()
        stD.close()
        stA.close()

        stE = ExitStack()
        xq = sb(stE, "xq", [128, KC * NOWN], BF16)

        stF = ExitStack()
        hf = sb(stF, "hf", [128, KC * NOWN], F32)
        x2c = sb(stF, "x2c", [128, 2 * NOWN], F32)
        cbf = sb(stF, "cbf", [128, 2 * NOWN], BF16)
        csq = sb(stF, "csq", [128, 2 * NOWN], BF16)
        m2b = sb(stF, "m2b", [128, NOWN], F32)
        rstd2b = sb(stF, "rstd2b", [128, NOWN], F32)
        zscr = sb(stF, "zscr", [128, NOWN], F32)
        z2scr = sb(stF, "z2scr", [128, NOWN], F32)
        zint = sb(stF, "zint", [128, NOWN], I32)
        mrun = sb(stF, "mrun", [128, 4 * NOWN], F32)
        mha = sb(stF, "mha", [64, NOWN], F32)
        mhb = sb(stF, "mhb", [64, NOWN], F32)
        rows = sb(stF, "rows", [1, 4 * NOWN], F32)
        txb = sb(stF, "txb", [128, NOWN], F32)
        sxb = sb(stF, "sxb", [128, NOWN], F32)

        def row(i):
            return rows[0:1, i * NOWN:(i + 1) * NOWN]

        ld2 = [None, None]
        cast_d = [None, None]
        mm2a = [None, None]
        mm2b = [None, None]
        for kc in range(KC):
            pr = kc % 2
            ld2[pr] = ydma(s_x2r[pr], lambda sy, kc=kc, pr=pr: sy.dma_start(
                out=x2c[:, pr * NOWN:pr * NOWN + NOWN],
                in_=x2s[kc * 128:(kc + 1) * 128, :]),
                waits=x2_written + [cast_d[pr]])
            c1 = vop(lambda v, pr=pr: v.tensor_copy(
                cbf[:, pr * NOWN:pr * NOWN + NOWN],
                x2c[:, pr * NOWN:pr * NOWN + NOWN]), waits=[ld2[pr], mm2a[pr]])
            cast_d[pr] = c1
            c2 = vop(lambda v, pr=pr: v.tensor_tensor(
                csq[:, pr * NOWN:pr * NOWN + NOWN],
                cbf[:, pr * NOWN:pr * NOWN + NOWN],
                cbf[:, pr * NOWN:pr * NOWN + NOWN], ALU.mult),
                waits=[mm2b[pr]])
            for nb in range(2):
                mm2a[pr] = mm(lambda t, kc=kc, pr=pr, nb=nb: t.matmul(
                    ps[0][:, nb * 512:nb * 512 + 512], ones128[:, :],
                    cbf[:, pr * NOWN + nb * 512: pr * NOWN + nb * 512 + 512],
                    start=(kc == 0), stop=(kc == KC - 1)),
                    waits=([c1] + pswait(0)) if nb == 0 else [],
                    last=(nb == 1))
            for nb in range(2):
                mm2b[pr] = mm(lambda t, kc=kc, pr=pr, nb=nb: t.matmul(
                    ps[1][:, nb * 512:nb * 512 + 512], ones128[:, :],
                    csq[:, pr * NOWN + nb * 512: pr * NOWN + nb * 512 + 512],
                    start=(kc == 0), stop=(kc == KC - 1)),
                    waits=([c2] + pswait(1)) if nb == 0 else [],
                    last=(nb == 1))
        st2a = mm2a[(KC - 1) % 2]
        st2b = mm2b[(KC - 1) % 2]
        vop(lambda v: v.tensor_scalar(m2b[:], ps[0][:, :], 1.0 / D, None,
                                      ALU.mult), waits=[st2a])
        vop(lambda v: v.tensor_tensor(zscr[:], m2b[:], m2b[:], ALU.mult))
        var2 = vop(lambda v: v.scalar_tensor_tensor(
            z2scr[:], ps[1][:, :], 1.0 / D, zscr[:], ALU.mult, ALU.subtract),
            waits=[st2b])
        sop(lambda s: s.activation(zscr[:], z2scr[:], AF.Ln,
                                   bias=constb[:, 0:1]), waits=[var2, cb_d])
        rstd2_d = sop(lambda s: s.activation(rstd2b[:], zscr[:], AF.Exp,
                                             scale=-0.5))
        psfree[0] = var2
        psfree[1] = var2

        hf_d = [None, None]
        mr_d = None
        mmh = [None, None]
        for kc in range(KC):
            pr = kc % 2
            ld2[pr] = ydma(s_x2r[pr], lambda sy, kc=kc, pr=pr: sy.dma_start(
                out=x2c[:, pr * NOWN:pr * NOWN + NOWN],
                in_=x2s[kc * 128:(kc + 1) * 128, :]),
                waits=[mm2a[pr], mm2b[pr], cast_d[pr], hf_d[pr]])
            u1 = vop(lambda v, pr=pr: v.tensor_tensor(
                zscr[:], x2c[:, pr * NOWN:pr * NOWN + NOWN], m2b[:],
                ALU.subtract),
                waits=[ld2[pr], rstd2_d, cbpl_d, (SP, SP.n)])
            vop(lambda v: v.tensor_tensor(z2scr[:], zscr[:], rstd2b[:],
                                          ALU.mult))
            hf_d[pr] = vop(lambda v, kc=kc: v.tensor_scalar(
                hf[:, kc * NOWN:(kc + 1) * NOWN], z2scr[:],
                cbpl1[:, kc:kc + 1], modfm[:, 48 + kc:49 + kc],
                ALU.mult, ALU.add))
            if kc == 0:
                vop(lambda v: v.tensor_copy(mrun[:, 0:NOWN], hf[:, 0:NOWN]))
                mr_d = vop(lambda v: v.tensor_copy(mrun[:, 2 * NOWN:3 * NOWN],
                                                   hf[:, 0:NOWN]))
            else:
                src = ((kc - 1) % 2) * NOWN
                dst = (kc % 2) * NOWN
                vop(lambda v, kc=kc, src=src, dst=dst: v.tensor_tensor(
                    mrun[:, dst:dst + NOWN], mrun[:, src:src + NOWN],
                    hf[:, kc * NOWN:(kc + 1) * NOWN], ALU.max))
                mr_d = vop(lambda v, kc=kc, src=src, dst=dst: v.tensor_tensor(
                    mrun[:, 2 * NOWN + dst:2 * NOWN + dst + NOWN],
                    mrun[:, 2 * NOWN + src:2 * NOWN + src + NOWN],
                    hf[:, kc * NOWN:(kc + 1) * NOWN], ALU.min))
            sqh = vop(lambda v, kc=kc, pr=pr: v.tensor_tensor(
                csq[:, pr * NOWN:pr * NOWN + NOWN],
                hf[:, kc * NOWN:(kc + 1) * NOWN],
                hf[:, kc * NOWN:(kc + 1) * NOWN], ALU.mult),
                waits=[mmh[pr]])
            for nb in range(2):
                mmh[pr] = mm(lambda t, kc=kc, pr=pr, nb=nb: t.matmul(
                    ps[2][:, nb * 512:nb * 512 + 512], ones128[:, :],
                    csq[:, pr * NOWN + nb * 512: pr * NOWN + nb * 512 + 512],
                    start=(kc == 0), stop=(kc == KC - 1)),
                    waits=([sqh] + pswait(2)) if nb == 0 else [],
                    last=(nb == 1))
        sumhf2 = mmh[(KC - 1) % 2]

        mr_off = ((KC - 1) % 2) * NOWN
        mrow_d = halve(lambda lo, c: mrun[lo:lo + c, mr_off:mr_off + NOWN],
                       (mha, mhb), ALU.max, row(0), [mr_d])

        r1_ = sop(lambda s: s.activation(row(1), ps[2][0:1, :], AF.Ln,
                                         scale=1.0 / D, bias=constb[0:1, 1:2]),
                  waits=[sumhf2, cb_d])
        r2_ = sop(lambda s: s.activation(row(2), row(1), AF.Exp, scale=-0.5))
        psfree[2] = r1_
        t1_ = vop(lambda v: v.tensor_single_scalar(row(1), row(0), 1e-30,
                                                   ALU.max),
                  waits=[mrow_d, r2_])
        t2_ = sop(lambda s: s.activation(row(3), row(1), AF.Ln), waits=[t1_])
        tx_d = sop(lambda s: s.activation(row(1), row(3), AF.Exp, scale=-1.0,
                                          bias=constb[0:1, 3:4]))
        s1_ = vop(lambda v: v.tensor_tensor(row(3), row(2), row(0), ALU.mult),
                  waits=[r2_, tx_d])
        s2_ = vop(lambda v: v.tensor_single_scalar(row(0), row(3), 1e-5,
                                                   ALU.max))
        sx_d = vop(lambda v: v.tensor_single_scalar(row(3), row(0),
                                                    1.0 / 127.0, ALU.mult))

        bc_d = []
        for rsrc, dst, dep in ((row(1), txb, tx_d), (row(3), sxb, sx_d)):
            m2_ = None
            for nb in range(2):
                m2_ = mm(lambda t, rsrc=rsrc, nb=nb: t.matmul(
                    ps[3][:, nb * 512:nb * 512 + 512], ones1f[:, :],
                    rsrc[0:1, nb * 512:nb * 512 + 512], start=True, stop=True),
                    waits=([dep, ones1_d] + pswait(3)) if nb == 0 else [],
                    last=(nb == 1))
            bcv = vop(lambda v, dst=dst: v.tensor_copy(dst[:], ps[3][:, :]),
                      waits=[m2_])
            psfree[3] = bcv
            bc_d.append(bcv)
        bc_tx, bc_sx = bc_d

        xq_d = None
        for kc in range(KC):
            vop(lambda v, kc=kc: v.tensor_tensor(
                zscr[:], hf[:, kc * NOWN:(kc + 1) * NOWN], txb[:], ALU.mult),
                waits=[bc_tx, bc_sx])
            vop(lambda v: v.tensor_copy(zint[:], zscr[:]))
            vop(lambda v: v.tensor_tensor(z2scr[:], zscr[:], zint[:],
                                          ALU.subtract))
            vop(lambda v: v.tensor_single_scalar(m2b[:], z2scr[:], 0.5,
                                                 ALU.is_ge))
            vop(lambda v: v.tensor_tensor(rstd2b[:], zint[:], m2b[:], ALU.add))
            vop(lambda v: v.scalar_tensor_tensor(
                zscr[:], z2scr[:], -0.5, rstd2b[:], ALU.is_le, ALU.subtract))
            xq_d = vop(lambda v, kc=kc: v.tensor_tensor(
                xq[:, kc * NOWN:(kc + 1) * NOWN], zscr[:], sxb[:], ALU.mult))
        xq_all = xq_d
        if debug:
            for kc in range(KC):
                ddump(dbg_hf[kc * 128:(kc + 1) * 128, :],
                      hf[:, kc * NOWN:(kc + 1) * NOWN], waits=[xq_all])
                ddump(dbg_xq[kc * 128:(kc + 1) * 128, :],
                      xq[:, kc * NOWN:(kc + 1) * NOWN], waits=[xq_all])
            ddump(dbg_rows[:, 0:NOWN], txb[:, :], waits=[xq_all])
            ddump(dbg_rows[:, NOWN:2 * NOWN], sxb[:, :], waits=[xq_all])

        E.flush()
        stF.close()

        stR = ExitStack()
        hff = sb(stR, "hff", [128, FC * NOWN], BF16, side="right")
        thb = sb(stR, "thb", [128, NOWN], F32, side="right")
        sxh3b = sb(stR, "sxh3b", [128, NOWN], F32, side="right")

        stG0 = ExitStack()
        mrun3 = sb(stG0, "mrun3", [128, 2 * NOWN], BF16)

        stG = ExitStack()
        stg = sb(stG, "stg", [128, 3 * NOWN], F32)
        tern = sb(stG, "tern", [128, 2 * D], BF16)
        sA = sb(stG, "sA", [128, 2 * NOWN], BF16)
        sB = sb(stG, "sB", [128, 2 * NOWN], BF16)
        slscr = sb(stG, "slscr", [128, NOWN], BF16)

        e3_start = [(VP, VP.n)]
        if debug:
            e3_start.append((s_dbg, s_dbg.n))
        slot_free = [list(e3_start) for _ in range(3)]
        e3_hs = [0]

        mm_b1 = [None]
        mm_b1h = [None]
        mm_b2 = [None]
        mm_b2h = [None]
        sil_d = [None]
        stt_d = [None]
        hsq_d = [None]
        mr3_d = [None]
        sq_mm = [None]
        q1h = [None, None]
        q2h = [None, None]

        comb_d = [None, None]

        def e3_load_quant(oc):
            for wsel, base, thcol, nthcol, mmh, qh, wt in (
                    (0, 0, 1, 2, (mm_b1h, mm_b1), q1h, w1t),
                    (1, D, 4, 5, (mm_b2h, mm_b2), q2h, w2t)):
                for hh2 in range(2):
                    hs = e3_hs[0]
                    e3_hs[0] = (hs + 1) % 3
                    pa = hh2 * NOWN
                    ssl = slice(hs * NOWN, hs * NOWN + NOWN)
                    tsl = slice(base + hh2 * NOWN, base + hh2 * NOWN + NOWN)
                    ldw = gdma(s_st[hs], lambda g, oc=oc, ssl=ssl, wt=wt,
                               hh2=hh2: g.dma_start(
                        out=stg[:, ssl],
                        in_=wt[oc][:, hh2 * NOWN:hh2 * NOWN + NOWN]),
                        waits=slot_free[hs] + [th1_d, th2_d, th3_d])
                    s1 = sop(lambda s, ssl=ssl, pa=pa, nthcol=nthcol:
                             s.activation(
                                 sA[:, pa:pa + NOWN], stg[:, ssl], AF.Sign,
                                 bias=thc[:, nthcol:nthcol + 1]),
                             waits=[ldw, comb_d[hh2]])
                    s2 = sop(lambda s, ssl=ssl, pa=pa, thcol=thcol:
                             s.activation(
                                 sB[:, pa:pa + NOWN], stg[:, ssl], AF.Sign,
                                 bias=thc[:, thcol:thcol + 1]))
                    q = vop(lambda v, tsl=tsl, pa=pa: v.tensor_tensor(
                        tern[:, tsl], sA[:, pa:pa + NOWN],
                        sB[:, pa:pa + NOWN], ALU.add),
                        waits=[s2, mmh[hh2][0]])
                    qh[hh2] = q
                    comb_d[hh2] = q
                    slot_free[hs] = [q]

        def e3_sumh2(oc):
            for nb in range(2):
                sq_mm[0] = mm(lambda t, oc=oc, nb=nb: t.matmul(
                    ps[2][:, nb * 512:nb * 512 + 512], ones128[:, :],
                    slscr[:, nb * 512:nb * 512 + 512],
                    start=(oc == 0), stop=(oc == FC - 1)),
                    waits=([hsq_d[0]] + pswait(2)) if nb == 0 else [],
                    last=(nb == 1))

        e3_load_quant(0)
        for oc in range(FC):
            qq1 = (q1h[0], q1h[1])
            qq2 = (q2h[0], q2h[1])
            for kc in range(KC):
                for nb in range(2):
                    e = mm(lambda t, kc=kc, nb=nb: t.matmul(
                        ps[0][:, nb * 512:nb * 512 + 512],
                        tern[:, kc * 128:(kc + 1) * 128],
                        xq[:, kc * NOWN + nb * 512: kc * NOWN + nb * 512 + 512],
                        start=(kc == 0), stop=(kc == KC - 1)),
                        waits=([qq1[0], xq_all] + pswait(0))
                        if (kc == 0 and nb == 0) else
                        ([qq1[1]] if (kc == KC // 2 and nb == 0) else []),
                        last=(nb == 1 and kc in (KC // 2 - 1, KC - 1)))
                    if kc == KC // 2 - 1 and nb == 1:
                        mm_b1h[0] = e
            mm_b1[0] = e
            for kc in range(KC):
                for nb in range(2):
                    e = mm(lambda t, kc=kc, nb=nb: t.matmul(
                        ps[1][:, nb * 512:nb * 512 + 512],
                        tern[:, D + kc * 128: D + kc * 128 + 128],
                        xq[:, kc * NOWN + nb * 512: kc * NOWN + nb * 512 + 512],
                        start=(kc == 0), stop=(kc == KC - 1)),
                        waits=([qq2[0]] + pswait(1))
                        if (kc == 0 and nb == 0) else
                        ([qq2[1]] if (kc == KC // 2 and nb == 0) else []),
                        last=(nb == 1 and kc in (KC // 2 - 1, KC - 1)))
                    if kc == KC // 2 - 1 and nb == 1:
                        mm_b2h[0] = e
            mm_b2[0] = e
            if oc > 0:
                e3_sumh2(oc - 1)
            if oc + 1 < FC:
                e3_load_quant(oc + 1)
            sil_d[0] = sop(lambda s: s.activation(slscr[:, :], ps[0][:, :],
                                                  AF.Silu, scale=thc[:, 0:1]),
                           waits=[mm_b1[0], th1_d, sq_mm[0], mr3_d[0]])
            stt_d[0] = vop(lambda v, oc=oc: v.scalar_tensor_tensor(
                hff[:, oc * NOWN:(oc + 1) * NOWN], ps[1][:, :], thc[:, 3:4],
                slscr[:, :], ALU.mult, ALU.mult), waits=[mm_b2[0], sil_d[0]])
            psfree[0] = sil_d[0]
            psfree[1] = stt_d[0]
            if debug:
                ddump(dbg_hff[oc * 128:(oc + 1) * 128, :],
                      hff[:, oc * NOWN:(oc + 1) * NOWN], waits=[stt_d[0]])
            hsq_d[0] = vop(lambda v, oc=oc: v.tensor_tensor(
                slscr[:, :], hff[:, oc * NOWN:(oc + 1) * NOWN],
                hff[:, oc * NOWN:(oc + 1) * NOWN], ALU.mult),
                waits=[sq_mm[0], sil_d[0], stt_d[0]])
            if oc == 0:
                mr3_d[0] = vop(lambda v: v.tensor_copy(mrun3[:, 0:NOWN],
                                                       slscr[:, :]))
            else:
                src = ((oc - 1) % 2) * NOWN
                dst = (oc % 2) * NOWN
                mr3_d[0] = vop(lambda v, src=src, dst=dst: v.tensor_tensor(
                    mrun3[:, dst:dst + NOWN], mrun3[:, src:src + NOWN],
                    slscr[:, :], ALU.max))
        e3_sumh2(FC - 1)
        sq_mm = sq_mm[0]
        mr3_d = mr3_d[0]

        E.flush()
        stG.close()

        stGt = ExitStack()
        mha3 = sb(stGt, "mha3", [64, NOWN], F32)
        mhb3 = sb(stGt, "mhb3", [64, NOWN], F32)
        rows3 = sb(stGt, "rows3", [1, 4 * NOWN], F32)

        def row3(i):
            return rows3[0:1, i * NOWN:(i + 1) * NOWN]

        mr3_off = ((FC - 1) % 2) * NOWN
        mrow3_d = halve(lambda lo, c: mrun3[lo:lo + c, mr3_off:mr3_off + NOWN],
                        (mha3, mhb3), ALU.max, row3(0), [mr3_d])
        sop(lambda s: s.activation(row3(1), ps[2][0:1, :], AF.Ln,
                                   scale=1.0 / DFF, bias=constb[0:1, 1:2]),
            waits=[sq_mm])
        rh2 = sop(lambda s: s.activation(row3(2), row3(1), AF.Exp, scale=-0.5))
        psfree[2] = (SP, SP.n)
        h1_ = vop(lambda v: v.tensor_single_scalar(row3(1), row3(0), 1e-30,
                                                   ALU.max),
                  waits=[mrow3_d, rh2])
        h2_ = sop(lambda s: s.activation(row3(3), row3(1), AF.Ln), waits=[h1_])
        th_d = sop(lambda s: s.activation(row3(1), row3(3), AF.Exp, scale=-0.5,
                                          bias=constb[0:1, 3:4]))
        m_d = sop(lambda s: s.activation(row3(0), row3(3), AF.Exp, scale=0.5))
        g1_ = vop(lambda v: v.tensor_tensor(row3(3), row3(2), row3(0),
                                            ALU.mult), waits=[rh2, th_d, m_d])
        g2_ = vop(lambda v: v.tensor_single_scalar(row3(0), row3(3), 1e-5,
                                                   ALU.max))
        g3_ = vop(lambda v: v.tensor_scalar(row3(3), row3(0), thc[0:1, 6:7],
                                            None, ALU.mult), waits=[th3_d])
        sxh_d = vop(lambda v: v.tensor_single_scalar(row3(0), row3(3),
                                                     1.0 / 127.0, ALU.mult))
        for rsrc, dst, dep in ((row3(1), thb, th_d), (row3(0), sxh3b, sxh_d)):
            m2_ = None
            for nb in range(2):
                m2_ = mm(lambda t, rsrc=rsrc, nb=nb: t.matmul(
                    ps[3][:, nb * 512:nb * 512 + 512], ones1f[:, :],
                    rsrc[0:1, nb * 512:nb * 512 + 512], start=True, stop=True),
                    waits=([dep, sxh_d] + pswait(3)) if nb == 0 else [],
                    last=(nb == 1))
            bcv = vop(lambda v, dst=dst: v.tensor_copy(dst[:], ps[3][:, :]),
                      waits=[m2_])
            psfree[3] = bcv
        bc_all = psfree[3]

        E.flush()
        stGt.close()
        stG0.close()
        stE.close()

        stH = ExitStack()
        zscr4 = sb(stH, "zscr4", [128, NOWN], F32)
        z2scr4 = sb(stH, "z2scr4", [128, 2 * NOWN], F32)
        zint4 = sb(stH, "zint4", [128, 2 * NOWN], I32)
        w3stg = sb(stH, "w3stg", [128, 2 * 1024], F32)
        tern3 = sb(stH, "tern3", [128, 2 * 4096], BF16)
        sA4 = sb(stH, "sA4", [128, 2048], BF16)
        sB4 = sb(stH, "sB4", [128, 2048], BF16)
        sA4w = sb(stH, "sA4w", [128, 1024], BF16)
        sB4w = sb(stH, "sB4w", [128, 1024], BF16)
        x2c4 = sb(stH, "x2c4", [128, NOWN], F32)
        outst = sb(stH, "outst", [128, NOWN], F32)

        hq_d = None
        hq_w0 = [bc_all] + ([(s_dbg, s_dbg.n)] if debug else [])
        sh_q = [None] * FC

        def hq_front(oc):
            pr = oc % 2
            vop(lambda v, oc=oc: v.tensor_tensor(
                zscr4[:], hff[:, oc * NOWN:(oc + 1) * NOWN], thb[:],
                ALU.mult), waits=(hq_w0 if oc == 0 else []) +
                ([sh_q[oc - 2]] if oc >= 2 else []))
            vop(lambda v, pr=pr: v.tensor_copy(
                zint4[:, pr * NOWN:pr * NOWN + NOWN], zscr4[:]))
            rr = vop(lambda v, pr=pr: v.tensor_tensor(
                z2scr4[:, pr * NOWN:pr * NOWN + NOWN], zscr4[:],
                zint4[:, pr * NOWN:pr * NOWN + NOWN], ALU.subtract))
            sh1 = sop(lambda s, pr=pr: s.activation(
                sA4[:, pr * NOWN:pr * NOWN + NOWN],
                z2scr4[:, pr * NOWN:pr * NOWN + NOWN],
                AF.Sign, bias=constb[:, 4:5]), waits=[rr, cb_d])
            sh_q[oc] = sop(lambda s, pr=pr: s.activation(
                sB4[:, pr * NOWN:pr * NOWN + NOWN],
                z2scr4[:, pr * NOWN:pr * NOWN + NOWN],
                AF.Sign, bias=constb[:, 5:6]))

        hq_done = [None] * FC

        def hq_back(oc):
            nonlocal hq_d
            pr = oc % 2
            vop(lambda v, pr=pr: v.tensor_tensor(
                outst[:, :], sA4[:, pr * NOWN:pr * NOWN + NOWN],
                sB4[:, pr * NOWN:pr * NOWN + NOWN], ALU.add),
                waits=[sh_q[oc]])
            hq_d = vop(lambda v, oc=oc, pr=pr: v.scalar_tensor_tensor(
                hff[:, oc * NOWN:(oc + 1) * NOWN], outst[:, :], -0.5,
                zint4[:, pr * NOWN:pr * NOWN + NOWN],
                ALU.mult, ALU.subtract))
            hq_done[oc] = hq_d

        hq_front(0)
        for oc in range(FC):
            if oc + 1 < FC:
                hq_front(oc + 1)
            hq_back(oc)
        hq_all = hq_d
        if debug:
            for oc in range(FC):
                ddump(dbg_hq[oc * 128:(oc + 1) * 128, :],
                      hff[:, oc * NOWN:(oc + 1) * NOWN], waits=[hq_all])
            ddump(dbg_rows3[:, 0:NOWN], thb[:, :], waits=[hq_all])
            ddump(dbg_rows3[:, NOWN:2 * NOWN], sxh3b[:, :], waits=[hq_all])

        mm_oh = [None, None]
        mm_o = [None, None]
        ev4 = [None, None]
        st4 = [None]
        tq3 = [None, None]
        ld3_free = [None, None]

        s4_d = [None]

        def e4_quant_half(oc, hhf):
            for qq in range(4):
                sl = qq % 2
                ld = gdma(s_w3[sl], lambda g, oc=oc, hhf=hhf, qq=qq, sl=sl:
                          g.dma_start(
                    out=w3stg[:, sl * 1024:sl * 1024 + 1024],
                    in_=w3t[oc][:, hhf * 4096 + qq * 1024:
                                hhf * 4096 + qq * 1024 + 1024]),
                    waits=[th3_d, ld3_free[sl], bc_all])
                sw1_ = sop(lambda s, sl=sl: s.activation(
                    sA4w[:, :], w3stg[:, sl * 1024:sl * 1024 + 1024],
                    AF.Sign, bias=thc[:, 8:9]), waits=[ld, s4_d[0]])
                sw2_ = sop(lambda s, sl=sl: s.activation(
                    sB4w[:, :], w3stg[:, sl * 1024:sl * 1024 + 1024],
                    AF.Sign, bias=thc[:, 7:8]))
                tq3[hhf] = vop(lambda v, hhf=hhf, qq=qq: v.tensor_tensor(
                    tern3[:, hhf * 4096 + qq * 1024:
                          hhf * 4096 + qq * 1024 + 1024],
                    sA4w[:, :], sB4w[:, :], ALU.add),
                    waits=[sw2_, mm_oh[hhf]])
                s4_d[0] = tq3[hhf]
                ld3_free[sl] = tq3[hhf]

        e4_quant_half(0, 0)
        e4_quant_half(0, 1)
        for oc in range(KC):
            pr = oc % 2
            x2ld = ydma(s_x2r[0], lambda sy, oc=oc: sy.dma_start(
                out=x2c4[:, :],
                in_=x2s[oc * 128:(oc + 1) * 128, :]),
                waits=[ev4[0], ev4[1]] + ([(VP, VP.n)] if oc < 2 else []))
            lastmm = None
            for hhf in range(2):
                tq_now = tq3[hhf]
                for kc2 in range(FC // 2):
                    kc = hhf * (FC // 2) + kc2
                    for nb in range(2):
                        lastmm = mm(lambda t, kc=kc, nb=nb, pr=pr, hhf=hhf:
                                    t.matmul(
                                        ps[pr][:, nb * 512:nb * 512 + 512],
                                        tern3[:, hhf * 4096 + kc2 * 128:
                                              hhf * 4096 + kc2 * 128 + 128]
                                        if False else
                                        tern3[:, (kc % (FC // 2)) * 128 +
                                              hhf * 4096:
                                              (kc % (FC // 2)) * 128 +
                                              hhf * 4096 + 128],
                                        hff[:, kc * NOWN + nb * 512:
                                            kc * NOWN + nb * 512 + 512],
                                        start=(kc == 0), stop=(kc == FC - 1)),
                                    waits=(([tq_now] + pswait(pr))
                                           if (kc2 == 0 and nb == 0) else []) +
                                    ([hq_done[kc]]
                                     if (oc == 0 and nb == 0) else []),
                                    last=(kc2 == FC // 2 - 1 and nb == 1))
                mm_oh[hhf] = lastmm
                if hhf == 0 and oc + 1 < KC:
                    e4_quant_half(oc + 1, 0)
            if oc + 1 < KC:
                e4_quant_half(oc + 1, 1)
            mm_o[pr] = lastmm
            u_d = vop(lambda v, oc=oc, pr=pr: v.scalar_tensor_tensor(
                zscr4[:], ps[pr][:, :], gb2[:, oc:oc + 1], sxh3b[:],
                ALU.mult, ALU.mult), waits=[lastmm, gb_done, x2ld, st4[0]])
            ev4[pr] = vop(lambda v, oc=oc: v.tensor_tensor(
                outst[:, :], zscr4[:], x2c4[:, :], ALU.add))
            psfree[pr] = ev4[pr]
            st4[0] = ydma(s_out[0], lambda sy, oc=oc: sy.dma_start(
                out=outT[oc * 128:(oc + 1) * 128, :],
                in_=outst[:, :]), waits=[ev4[pr]])

        def final_waits(sy):
            for sm in all_dma_sems:
                if sm.n > 0:
                    sy.wait_ge(sm.h, sm.n)
        E.plain("sync", final_waits)
        E.flush()
        stH.close()
        stR.close()

    return nc


_cache = {}


def _tile_oc(w, noc):
    dout, din = w.shape
    kc = din // 128
    a = w.reshape(noc, 128, kc, 128).transpose(0, 3, 2, 1)
    return np.ascontiguousarray(a.reshape(noc, 128, din))


def _rope_tables(own0):
    inv = 1.0 / (10000.0 ** (np.arange(0, HD, 2, dtype=np.float64) / HD))
    t = np.arange(T, dtype=np.float64)
    ang = np.outer(inv, t)
    c = np.cos(ang)
    s = np.sin(ang)
    s2 = np.concatenate([-s[0:32], s[32:64]], axis=0)
    perm = np.concatenate([np.arange(own0, T), np.arange(0, own0)])
    return (np.ascontiguousarray(c[:, perm]).astype(np.float32),
            np.ascontiguousarray(s2[:, perm]).astype(np.float32))


def prepare_in_maps(L0_attn_x, L0_attn_wq, L0_attn_wk, L0_attn_wv, L0_attn_wo,
                    L0_ffn_w1, L0_ffn_w2, L0_ffn_w3, L0_c, L0_wmod):
    wq = np.asarray(L0_attn_wq, np.float32)
    wk = np.asarray(L0_attn_wk, np.float32)
    wv = np.asarray(L0_attn_wv, np.float32)
    wo = np.asarray(L0_attn_wo, np.float32)
    w1 = np.asarray(L0_ffn_w1, np.float32)
    w2 = np.asarray(L0_ffn_w2, np.float32)
    w3 = np.asarray(L0_ffn_w3, np.float32)
    wm = np.asarray(L0_wmod, np.float32)
    x = np.asarray(L0_attn_x, np.float32)
    c = np.asarray(L0_c, np.float32)

    shared = {
        "wqt": _tile_oc(wq, KC),
        "wkt": _tile_oc(wk, KC),
        "wot": _tile_oc(wo, KC),
        "w1t": _tile_oc(w1, FC),
        "w2t": _tile_oc(w2, FC),
        "w3t": _tile_oc(w3, KC),
    }
    wvr = wv.reshape(2 * NG, 256, KC, 128).transpose(0, 3, 2, 1)
    shared["wvt"] = np.ascontiguousarray(wvr.reshape(2 * NG, 128, KC * 256))
    wmr = wm.reshape(KC, 128, 96, 128).transpose(2, 1, 0, 3)
    shared["wmt"] = np.ascontiguousarray(wmr.reshape(96, 128, D))

    in_maps = []
    for core in range(8):
        b, p = core // 2, core % 2
        own0 = p * NOWN
        xtb = x[b].T
        xrot = np.ascontiguousarray(
            np.concatenate([xtb[:, own0:], xtb[:, :own0]], axis=1))
        rc, rs = _rope_tables(own0)
        m = dict(shared)
        m["xt"] = xrot
        m["cct"] = np.ascontiguousarray(c[b].reshape(KC, 128).T)
        m["rct"] = rc
        m["rst"] = rs
        in_maps.append(m)
    return in_maps


def kernel(**inputs):
    if "nc" not in _cache:
        _cache["nc"] = build_nc(detect_races=True)
    nc = _cache["nc"]
    in_maps = prepare_in_maps(**inputs)
    res = run_bass_kernel_spmd(nc, in_maps, core_ids=list(range(8)))
    out = np.empty((B, T, D), np.float32)
    for core in range(8):
        b, p = core // 2, core % 2
        out[b, p * NOWN:(p + 1) * NOWN, :] = res.results[core]["outT"].T
    return out


# version 08
# speedup vs baseline: 1.0813x; 1.0813x over previous
import math
import numpy as np
from contextlib import ExitStack

import concourse.bass as bass
import concourse.mybir as mybir
from concourse.bass_utils import run_bass_kernel_spmd

F32 = mybir.dt.float32
BF16 = mybir.dt.bfloat16
I32 = mybir.dt.int32
AF = mybir.ActivationFunctionType
ALU = mybir.AluOpType
AX = mybir.AxisListType

B, T, D = 4, 2048, 2048
NH, HD = 16, 128
DFF = 8192
KC = D // 128
FC = DFF // 128
NOWN = 1024
NG, GH = 4, 4

INV_SQRT_HD = float(np.float32(1.0 / math.sqrt(HD)))
LN_SQRT_HD = float(np.float32(math.log(math.sqrt(HD))))
LN_127 = float(np.float32(math.log(127.0)))
LN_2_SQHD = float(np.float32(math.log(2.0) - math.log(math.sqrt(HD))))


class Sem:
    __slots__ = ("h", "n")

    def __init__(self, h):
        self.h = h
        self.n = 0


class Emitter:

    def __init__(self, block):
        self.block = block
        self.buf = {e: [] for e in ("tensor", "vector", "scalar", "gpsimd", "sync")}
        self.water = {e: {} for e in self.buf}
        self.prog = {}

    def _push(self, eng, fn, waits, upd):
        agg = {}
        for s, v in waits:
            if s is None:
                continue
            assert v <= s.n, (eng, v, s.n, "wait on not-yet-planned sem value")
            if agg.get(id(s), (None, -1))[1] < v:
                agg[id(s)] = (s, v)
        wl = []
        wm = self.water[eng]
        for s, v in agg.values():
            if wm.get(id(s), -1) < v:
                wm[id(s)] = v
                wl.append((s.h, v))

        def closure(e, wl=wl, fn=fn, upd=upd):
            for h, v in wl:
                e.wait_ge(h, v)
            inst = fn(e)
            if upd is not None:
                inst.then_inc(upd[0], upd[1])

        self.buf[eng].append(closure)

    def chained(self, eng, fn, waits=()):
        ps = self.prog[eng]
        waits = [w for w in waits if w is not None]
        if ps.n > 0:
            self.water[eng].pop(id(ps), None)
            waits.append((ps, ps.n))
        ps.n += 1
        self._push(eng, fn, waits, (ps.h, 1))
        return (ps, ps.n)

    def plain(self, eng, fn, waits=(), inc=None):
        waits = [w for w in waits if w is not None]
        upd = None
        ret = None
        if inc is not None:
            s, k = inc
            s.n += k
            upd = (s.h, k)
            ret = (s, s.n)
        self._push(eng, fn, waits, upd)
        return ret

    def flush(self):
        for eng, ops in self.buf.items():
            if not ops:
                continue
            body = list(ops)
            ops.clear()

            def run(e, body=body):
                for f in body:
                    f(e)

            getattr(self.block, eng)(run)


def build_nc(detect_races=True, debug=False):
    nc = bass.Bass("TRN2", target_bir_lowering=False,
                   detect_race_conditions=detect_races)

    xt = nc.dram_tensor("xt", [D, T], F32, kind="ExternalInput").ap()
    wqt = nc.dram_tensor("wqt", [KC, 128, D], F32, kind="ExternalInput").ap()
    wkt = nc.dram_tensor("wkt", [KC, 128, D], F32, kind="ExternalInput").ap()
    wvt = nc.dram_tensor("wvt", [2 * NG, 128, KC * 256], F32, kind="ExternalInput").ap()
    wot = nc.dram_tensor("wot", [KC, 128, D], F32, kind="ExternalInput").ap()
    w1t = nc.dram_tensor("w1t", [FC, 128, D], F32, kind="ExternalInput").ap()
    w2t = nc.dram_tensor("w2t", [FC, 128, D], F32, kind="ExternalInput").ap()
    w3t = nc.dram_tensor("w3t", [KC, 128, DFF], F32, kind="ExternalInput").ap()
    wmt = nc.dram_tensor("wmt", [96, 128, D], F32, kind="ExternalInput").ap()
    cct = nc.dram_tensor("cct", [128, KC], F32, kind="ExternalInput").ap()
    rct = nc.dram_tensor("rct", [64, T], F32, kind="ExternalInput").ap()
    rst = nc.dram_tensor("rst", [64, T], F32, kind="ExternalInput").ap()
    outT = nc.dram_tensor("outT", [D, NOWN], F32, kind="ExternalOutput").ap()
    x2s = nc.dram_tensor("x2s", [D, NOWN], F32, kind="Internal").ap()
    if debug:
        dbg_hf = nc.dram_tensor("dbg_hf", [D, NOWN], F32, kind="ExternalOutput").ap()
        dbg_xq = nc.dram_tensor("dbg_xq", [D, NOWN], F32, kind="ExternalOutput").ap()
        dbg_thc = nc.dram_tensor("dbg_thc", [128, 12], F32, kind="ExternalOutput").ap()
        dbg_rows = nc.dram_tensor("dbg_rows", [128, 2 * NOWN], F32, kind="ExternalOutput").ap()
        dbg_hff = nc.dram_tensor("dbg_hff", [DFF, NOWN], F32, kind="ExternalOutput").ap()
        dbg_hq = nc.dram_tensor("dbg_hq", [DFF, NOWN], F32, kind="ExternalOutput").ap()
        dbg_rows3 = nc.dram_tensor("dbg_rows3", [128, 2 * NOWN], F32, kind="ExternalOutput").ap()
        dbg_nb = nc.dram_tensor("dbg_nb", [128, 3 * NH], F32, kind="ExternalOutput").ap()

    st = ExitStack()
    with st:
        sems = {}
        for name in ("vp", "sp", "gp", "tp"):
            sems[name] = Sem(st.enter_context(nc.semaphore(name)))
        VP, SP, GP, TP = (sems[k] for k in ("vp", "sp", "gp", "tp"))

        all_dma_sems = []

        def dsem(name):
            sm = Sem(st.enter_context(nc.semaphore(name)))
            all_dma_sems.append(sm)
            return sm

        blk = st.enter_context(nc.Block())
        E = Emitter(blk)
        E.prog = {"vector": VP, "scalar": SP, "gpsimd": GP}

        vop = lambda fn, waits=(): E.chained("vector", fn, waits)
        sop = lambda fn, waits=(): E.chained("scalar", fn, waits)
        gop = lambda fn, waits=(): E.chained("gpsimd", fn, waits)
        gdma = lambda sm, fn, waits=(): E.plain("gpsimd", fn, waits, inc=(sm, 16))
        ydma = lambda sm, fn, waits=(): E.plain("sync", fn, waits, inc=(sm, 16))

        s_cc, s_rc, s_rs = dsem("s_cc"), dsem("s_rc"), dsem("s_rs")
        s_x = [dsem("s_x0"), dsem("s_x1")]
        s_wm = [dsem("s_wm0"), dsem("s_wm1")]
        s_wq = [dsem("s_wq0"), dsem("s_wq1")]
        s_wv = dsem("s_wv")
        s_rope = dsem("s_rope")
        s_hv = dsem("s_hv")
        s_wo = [dsem("s_wo0"), dsem("s_wo1")]
        s_xo = [dsem("s_xo0"), dsem("s_xo1")]
        s_x2w = [dsem("s_x2w0"), dsem("s_x2w1")]
        s_x2r = [dsem("s_x2r0"), dsem("s_x2r1")]
        s_st = [dsem("s_st0"), dsem("s_st1"), dsem("s_st2"), dsem("s_st3")]
        s_w3 = [dsem("s_w30"), dsem("s_w31")]
        s_out = [dsem("s_out0"), dsem("s_out1")]
        s_dbg = dsem("s_dbg")

        def ddump(dst_ap, src_ap, waits=()):
            if not debug:
                return None
            return gdma(s_dbg, lambda g, dst_ap=dst_ap, src_ap=src_ap:
                        g.dma_start(out=dst_ap, in_=src_ap),
                        waits=list(waits) + [(s_dbg, s_dbg.n)])

        def mm(fn, waits=(), last=False):
            return E.plain("tensor", fn, waits, inc=(TP, 1) if last else None)

        ps = [st.enter_context(nc.psum_tensor(f"ps{i}", [128, 1024], F32))
              for i in range(4)]
        psfree = [None, None, None, None]

        def pswait(i):
            return [psfree[i]] if psfree[i] is not None else []

        def sb(stack, name, shape, dtype, side=None):
            kw = {"side": side} if side else {}
            return stack.enter_context(nc.sbuf_tensor(name, shape, dtype, **kw))

        ones128 = sb(st, "ones128", [128, 128], BF16)
        ones1f = sb(st, "ones1f", [1, 128], F32)
        ccolf = sb(st, "ccolf", [128, KC], F32)
        scol = sb(st, "scol", [128, KC], BF16)
        modfm = sb(st, "modfm", [128, 96], F32)
        capl1 = sb(st, "capl1", [128, KC], F32)
        cbpl1 = sb(st, "cbpl1", [128, KC], F32)
        ga2 = sb(st, "ga2", [128, KC], F32)
        gb2 = sb(st, "gb2", [128, KC], F32)
        gscr = sb(st, "gscr", [128, 3 * KC], F32)
        qmax = sb(st, "qmax", [128, NH], F32)
        kmax = sb(st, "kmax", [128, NH], F32)
        negB = sb(st, "negB", [128, NH], F32)
        bcol = sb(st, "bcol", [128, 8], F32)
        thc = sb(st, "thc", [128, 12], F32)
        constb = sb(st, "constb", [128, 7], F32)

        ones_d = vop(lambda v: v.memset(ones128[:], 1.0))
        ones1_d = vop(lambda v: v.memset(ones1f[:], 1.0))
        vop(lambda v: v.memset(constb[:, 0:1], 1e-5))
        vop(lambda v: v.memset(constb[:, 1:2], 1e-6))
        vop(lambda v: v.memset(constb[:, 2:3], -LN_SQRT_HD))
        vop(lambda v: v.memset(constb[:, 3:4], LN_127))
        vop(lambda v: v.memset(constb[:, 4:5], -0.5))
        vop(lambda v: v.memset(constb[:, 5:6], 0.5))
        cb_d = vop(lambda v: v.memset(constb[:, 6:7], LN_2_SQHD))
        ld_cc = gdma(s_cc, lambda g: g.dma_start(out=ccolf[:], in_=cct[:, :]))
        ld_rc = gdma(s_rc, lambda g: g.dma_start(out=cbT[:], in_=rct[:, :]))
        ld_rs = gdma(s_rs, lambda g: g.dma_start(out=sbT[:], in_=rst[:, :]))
        sc_done = sop(lambda s: s.activation(scol[:], ccolf[:], AF.Silu),
                      waits=[ld_cc])

        def halve(cur_fn, bufs, op, final_out, waits, n=NOWN):
            b0 = lambda lo, c: bufs[0][lo:lo + c, 0:n]
            b1 = lambda lo, c: bufs[1][lo:lo + c, 0:n]
            regions = [cur_fn, b0, b1]
            done = list(waits)
            p, lvl = 64, 0
            while p >= 1:
                cur = regions[(2 * lvl) % 3]
                tmp = regions[(2 * lvl + 1) % 3]
                nxt = regions[(2 * lvl + 2) % 3]
                d = gdma(s_hv, lambda g, cur=cur, tmp=tmp, p=p: g.dma_start(
                    out=tmp(0, p), in_=cur(p, p)), waits=done)
                if p > 1:
                    t = vop(lambda v, cur=cur, tmp=tmp, nxt=nxt, p=p:
                            v.tensor_tensor(nxt(0, p), cur(0, p), tmp(0, p),
                                            op), waits=[d])
                else:
                    t = vop(lambda v, cur=cur, tmp=tmp: v.tensor_tensor(
                        final_out, cur(0, 1), tmp(0, 1), op), waits=[d])
                done = [t]
                lvl += 1
                p //= 2
            return done[0]

        stA = ExitStack()
        hbuf = sb(stA, "hbuf", [128, KC * T], BF16)
        ybuf = sb(stA, "ybuf", [128, NH * NOWN], BF16)
        cbT = sb(stA, "cbT", [64, T], BF16)
        sbT = sb(stA, "sbT", [64, T], BF16)

        stB = ExitStack()
        xstr = sb(stB, "xstr", [128, 2 * T], BF16)
        xsq = sb(stB, "xsq", [128, 2 * T], BF16)
        mst = sb(stB, "mst", [128, T], F32)
        rstd = sb(stB, "rstd", [128, T], F32)
        uscr = sb(stB, "uscr", [128, T], F32)
        u2scr = sb(stB, "u2scr", [128, T], F32)
        wms = sb(stB, "wms", [128, 2 * D], BF16)

        xld = [None, None]
        sqd = [None, None]
        mm_x = [None, None]
        for kc in range(KC):
            pr = kc % 2
            xld[pr] = gdma(s_x[pr], lambda g, kc=kc, pr=pr: g.dma_start(
                out=xstr[:, pr * T:pr * T + T],
                in_=xt[kc * 128:(kc + 1) * 128, :]),
                waits=[mm_x[pr], sqd[pr]])
            sqd[pr] = vop(lambda v, pr=pr: v.tensor_tensor(
                xsq[:, pr * T:pr * T + T], xstr[:, pr * T:pr * T + T],
                xstr[:, pr * T:pr * T + T], ALU.mult), waits=[xld[pr]])
            for nb in range(4):
                pt, sl = ps[nb // 2], slice((nb % 2) * 512, (nb % 2) * 512 + 512)
                mm(lambda t, kc=kc, pr=pr, pt=pt, sl=sl, nb=nb: t.matmul(
                    pt[:, sl], ones128[:, :],
                    xstr[:, pr * T + nb * 512: pr * T + nb * 512 + 512],
                    start=(kc == 0), stop=(kc == KC - 1)),
                    waits=[xld[pr], ones_d] + (pswait(nb // 2) if kc == 0 else []))
            for nb in range(4):
                pt, sl = ps[2 + nb // 2], slice((nb % 2) * 512, (nb % 2) * 512 + 512)
                mm_x[pr] = mm(lambda t, kc=kc, pr=pr, pt=pt, sl=sl, nb=nb: t.matmul(
                    pt[:, sl], ones128[:, :],
                    xsq[:, pr * T + nb * 512: pr * T + nb * 512 + 512],
                    start=(kc == 0), stop=(kc == KC - 1)),
                    waits=[sqd[pr]] + (pswait(2 + nb // 2) if kc == 0 else []),
                    last=True)
        stats_done = mm_x[(KC - 1) % 2]

        for hh in range(2):
            vop(lambda v, hh=hh: v.tensor_scalar(
                mst[:, hh * 1024:hh * 1024 + 1024], ps[hh][:, :], 1.0 / D, None,
                ALU.mult), waits=[stats_done])
            vop(lambda v, hh=hh: v.tensor_tensor(
                uscr[:, hh * 1024:hh * 1024 + 1024],
                mst[:, hh * 1024:hh * 1024 + 1024],
                mst[:, hh * 1024:hh * 1024 + 1024], ALU.mult))
            vop(lambda v, hh=hh: v.scalar_tensor_tensor(
                u2scr[:, hh * 1024:hh * 1024 + 1024], ps[2 + hh][:, :], 1.0 / D,
                uscr[:, hh * 1024:hh * 1024 + 1024], ALU.mult, ALU.subtract))
        var_done = (VP, VP.n)
        lnv = sop(lambda s: s.activation(uscr[:, :], u2scr[:, :], AF.Ln,
                                         bias=constb[:, 0:1]),
                  waits=[var_done, cb_d])
        rstd_done = sop(lambda s: s.activation(rstd[:, :], uscr[:, :], AF.Exp,
                                               scale=-0.5))
        psfree[0] = psfree[1] = var_done
        psfree[2] = psfree[3] = var_done

        wld = [None, None]
        mm_wm = [None, None]
        first_mod = [True]
        for dc in range(96):
            pr = dc % 2
            wld[pr] = gdma(s_wm[pr], lambda g, dc=dc, pr=pr: g.dma_start(
                out=wms[:, pr * D:pr * D + D], in_=wmt[dc]),
                waits=[sc_done, mm_wm[pr]])
            for kc in range(KC):
                w2 = [wld[pr]] if kc == 0 else []
                if kc == 0 and first_mod[0]:
                    w2 += [sc_done] + pswait(2)
                    first_mod[0] = False
                mm_wm[pr] = mm(lambda t, dc=dc, pr=pr, kc=kc: t.matmul(
                    ps[2][:, dc:dc + 1],
                    wms[:, pr * D + kc * 128: pr * D + kc * 128 + 128],
                    scol[:, kc:kc + 1],
                    start=(kc == 0), stop=(kc == KC - 1)),
                    waits=w2, last=(kc == KC - 1))
        mod_mm = mm_wm[1]
        mod_ev = vop(lambda v: v.tensor_copy(modfm[:], ps[2][:, 0:96]),
                     waits=[mod_mm])
        psfree[2] = mod_ev
        capl_d = vop(lambda v: v.tensor_scalar(capl1[:], modfm[:, 16:32], 1.0,
                                               None, ALU.add))
        cbpl_d = vop(lambda v: v.tensor_scalar(cbpl1[:], modfm[:, 64:80], 1.0,
                                               None, ALU.add))
        gate_d = []
        for dst, c0 in ((ga2, 32), (gb2, 80)):
            vop(lambda v, c0=c0: v.tensor_single_scalar(
                gscr[:, 0:KC], modfm[:, c0:c0 + KC], 0.0, ALU.is_ge))
            vop(lambda v: v.tensor_scalar(
                gscr[:, KC:2 * KC], gscr[:, 0:KC], 2.0, -1.0, ALU.mult, ALU.add))
            ab = sop(lambda s, c0=c0: s.activation(
                gscr[:, 0:KC], modfm[:, c0:c0 + KC], AF.Abs),
                waits=[mod_ev, (VP, VP.n)])
            vop(lambda v: v.tensor_single_scalar(
                gscr[:, 2 * KC:3 * KC], gscr[:, 0:KC], 0.25, ALU.max),
                waits=[ab])
            gate_d.append(vop(lambda v, dst=dst: v.tensor_tensor(
                dst[:], gscr[:, KC:2 * KC], gscr[:, 2 * KC:3 * KC], ALU.mult)))
        ga_done, gb_done = gate_d

        h_ld = [None, None]
        h_done = [None, None]
        for kc in range(KC):
            pr = kc % 2
            h_ld[pr] = gdma(s_x[pr], lambda g, kc=kc, pr=pr: g.dma_start(
                out=xstr[:, pr * T:pr * T + T],
                in_=xt[kc * 128:(kc + 1) * 128, :]),
                waits=[mm_x[pr], sqd[pr], h_done[pr]])
            vop(lambda v, pr=pr: v.tensor_tensor(
                uscr[:, :], xstr[:, pr * T:pr * T + T], mst[:, :], ALU.subtract),
                waits=[h_ld[pr], rstd_done, capl_d, mod_ev])
            vop(lambda v: v.tensor_tensor(u2scr[:, :], uscr[:, :], rstd[:, :],
                                          ALU.mult))
            h_done[pr] = vop(lambda v, kc=kc: v.tensor_scalar(
                hbuf[:, kc * T:(kc + 1) * T], u2scr[:, :],
                capl1[:, kc:kc + 1], modfm[:, kc:kc + 1], ALU.mult, ALU.add))
        h_all = h_done[(KC - 1) % 2]

        E.flush()
        stB.close()

        stP = ExitStack()
        stgP = sb(stP, "stgP", [128, 2 * 512], BF16, side="right")
        wsum = sb(stP, "wsum", [128, 800], F32, side="right")

        pre_steps = []
        pre_slot_free = [None, None]
        pre_red_last = [None]

        def _pre_chunk(src_ap, col, slot, eng):
            def emit(src_ap=src_ap, col=col, slot=slot):
                ld = gdma(s_st[slot], lambda g: g.dma_start(
                    out=stgP[:, slot * 512:slot * 512 + 512], in_=src_ap),
                    waits=[pre_slot_free[slot]])
                red = vop(lambda v: v.reduce_sum(
                    wsum[:, col:col + 1],
                    stgP[:, slot * 512:slot * 512 + 512], AX.X,
                    apply_absolute_value=True), waits=[ld])
                pre_slot_free[slot] = red
                pre_red_last[0] = red
            return emit

        ci = 0
        for oc in range(FC):
            for qq in range(4):
                pre_steps.append(_pre_chunk(
                    w1t[oc][:, qq * 512:(qq + 1) * 512], 0 + 4 * oc + qq,
                    ci % 2, "v"))
                ci += 1
                pre_steps.append(_pre_chunk(
                    w2t[oc][:, qq * 512:(qq + 1) * 512], 256 + 4 * oc + qq,
                    ci % 2, "v"))
                ci += 1
        for tt3 in range(KC):
            for qq in range(16):
                pre_steps.append(_pre_chunk(
                    w3t[tt3][:, qq * 512:(qq + 1) * 512], 512 + 16 * tt3 + qq,
                    ci % 2, "v"))
                ci += 1
        pre_pos = [0]

        def pump_pre(n):
            k = 0
            while k < n and pre_pos[0] < len(pre_steps):
                pre_steps[pre_pos[0]]()
                pre_pos[0] += 1
                k += 1

        stC = ExitStack()
        Qs = sb(stC, "Qs", [128, NH * NOWN], BF16)
        Kg = sb(stC, "Kg", [128, 2 * T], BF16)
        Vg = sb(stC, "Vg", [128, KC * 512], BF16)
        PT = sb(stC, "PT", [128, 2 * NOWN], BF16)
        wstg = sb(stC, "wstg", [128, 2 * D], BF16)
        wstgv = sb(stC, "wstgv", [128, KC * 256], BF16)
        qf = sb(stC, "qf", [64, NOWN], F32)
        lnscr = sb(stC, "lnscr", [128, NOWN], F32)
        rcp = sb(stC, "rcp", [128, NOWN], F32)
        ra = sb(stC, "ra", [64, NOWN], F32)
        rb = sb(stC, "rb", [64, NOWN], F32)

        state = {"wstg": None, "qf": None, "ropetmp": None, "pt": None,
                 "vstg": None, "kg0": None, "kg1": None, "vg_mm": None,
                 "yv": None}

        def rope(dst_ap, psrc, tok0, ntok, waits):
            ev = sop(lambda s: s.copy(qf[:, 0:ntok], psrc),
                     waits=list(waits) + [state["qf"]])
            d1 = ydma(s_rope, lambda sy: sy.dma_start(out=rb[0:32, 0:ntok],
                                              in_=qf[32:64, 0:ntok]),
                      waits=[ev, state["ropetmp"]])
            d2 = ydma(s_rope, lambda sy: sy.dma_start(out=rb[32:64, 0:ntok],
                                              in_=qf[0:32, 0:ntok]),
                      waits=[ev, state["ropetmp"]])
            a = vop(lambda v: v.tensor_tensor(
                ra[:, 0:ntok], qf[:, 0:ntok],
                cbT[:, tok0:tok0 + ntok], ALU.mult),
                waits=[ev, ld_rc, d1, d2, state["ropetmp"]])
            b = vop(lambda v: v.tensor_tensor(
                qf[:, 0:ntok], rb[:, 0:ntok],
                sbT[:, tok0:tok0 + ntok], ALU.mult), waits=[ld_rs])
            add = vop(lambda v: v.tensor_tensor(
                dst_ap, ra[:, 0:ntok], qf[:, 0:ntok], ALU.add))
            state["qf"] = add
            state["ropetmp"] = add
            return add, ev

        q_done = [None, None]
        wfree = [None, None]
        for oc in range(NH):
            pr = oc % 2
            ld = gdma(s_wq[pr], lambda g, oc=oc, pr=pr: g.dma_start(
                out=wstg[:, pr * D:pr * D + D], in_=wqt[oc]),
                waits=[h_all, wfree[pr]])
            lastmm = None
            for kc in range(KC):
                for nb in range(2):
                    lastmm = mm(lambda t, oc=oc, kc=kc, nb=nb, pr=pr: t.matmul(
                        ps[pr][:, nb * 512:nb * 512 + 512],
                        wstg[:, pr * D + kc * 128: pr * D + kc * 128 + 128],
                        hbuf[:, kc * T + nb * 512: kc * T + nb * 512 + 512],
                        start=(kc == 0), stop=(kc == KC - 1)),
                        waits=([ld, h_all] + pswait(pr))
                        if (kc == 0 and nb == 0) else [],
                        last=(kc == KC - 1 and nb == 1))
            wfree[pr] = lastmm
            ev2 = sop(lambda s, oc=oc, pr=pr: s.copy(
                Qs[64:128, oc * NOWN:(oc + 1) * NOWN], ps[pr][64:128, :]),
                waits=[lastmm])
            state["qev2"] = ev2
            rp, q_ev = rope(Qs[0:64, oc * NOWN:(oc + 1) * NOWN],
                            ps[pr][0:64, :], 0, NOWN, [lastmm])
            q_done[pr] = rp
            psfree[pr] = q_ev
            sq = vop(lambda v, oc=oc: v.tensor_tensor(
                PT[:, 0:NOWN], Qs[:, oc * NOWN:(oc + 1) * NOWN],
                Qs[:, oc * NOWN:(oc + 1) * NOWN], ALU.mult),
                waits=[rp, ev2, state["pt"]])
            mqn = None
            for nb in range(2):
                mqn = mm(lambda t, nb=nb: t.matmul(
                    ps[2][:, nb * 512:nb * 512 + 512], ones128[:, :],
                    PT[:, nb * 512:nb * 512 + 512], start=True, stop=True),
                    waits=([sq] + pswait(2)) if nb == 0 else [], last=(nb == 1))
            red = vop(lambda v, oc=oc: v.reduce_max(
                qmax[:, oc:oc + 1], ps[2][:, :], AX.X), waits=[mqn])
            state["pt"] = red
            psfree[2] = red
            pump_pre(8)

        for g in range(NG):
            v_ev = None
            for sub in range(2):
                vld = gdma(s_wv, lambda g_, g2=g, sub=sub: g_.dma_start(
                    out=wstgv[:], in_=wvt[2 * g2 + sub]),
                    waits=[h_all, state["vstg"]])
                for tc in range(KC):
                    pr = tc % 2
                    lastmm = None
                    for kc in range(KC):
                        lastmm = mm(lambda t, tc=tc, kc=kc, pr=pr: t.matmul(
                            ps[pr][:, 0:256],
                            hbuf[:, kc * T + tc * 128: kc * T + tc * 128 + 128],
                            wstgv[:, kc * 256:(kc + 1) * 256],
                            start=(kc == 0), stop=(kc == KC - 1)),
                            waits=([vld] + pswait(pr) +
                                   ([state["vg_mm"]] if (tc == 0 and sub == 0)
                                    else []))
                            if kc == 0 else [], last=(kc == KC - 1))
                    v_ev = sop(lambda s, tc=tc, pr=pr, sub=sub: s.copy(
                        Vg[:, tc * 512 + sub * 256:tc * 512 + sub * 256 + 256],
                        ps[pr][:, 0:256]), waits=[lastmm])
                    psfree[pr] = v_ev
                state["vstg"] = v_ev
            vg_done = v_ev

            for pair in range(2):
                kg_ready = [[], []]
                nb_edge = [None, None]
                for j2 in range(2):
                    h = g * GH + pair * 2 + j2
                    jg = pair * 2 + j2
                    pp = j2 * 2
                    kpr = h % 2
                    ld = gdma(s_wq[kpr], lambda g_, h=h, kpr=kpr: g_.dma_start(
                        out=wstg[:, kpr * D:kpr * D + D], in_=wkt[h]),
                        waits=[wfree[kpr]])
                    lastmm = None
                    for kc in range(KC):
                        for nb in range(4):
                            pt = ps[pp + nb // 2]
                            sl = slice((nb % 2) * 512, (nb % 2) * 512 + 512)
                            lastmm = mm(lambda t, kc=kc, nb=nb, pt=pt, sl=sl,
                                        kpr=kpr: t.matmul(
                                            pt[:, sl],
                                            wstg[:, kpr * D + kc * 128:
                                                 kpr * D + kc * 128 + 128],
                                            hbuf[:, kc * T + nb * 512:
                                                 kc * T + nb * 512 + 512],
                                            start=(kc == 0),
                                            stop=(kc == KC - 1)),
                                        waits=([ld] + pswait(pp) +
                                               pswait(pp + 1) +
                                               [state["kg%d" % j2]])
                                        if (kc == 0 and nb == 0) else [],
                                        last=(kc == KC - 1 and nb == 3))
                    wfree[kpr] = lastmm
                    ev2a = sop(lambda s, j2=j2, pp=pp: s.copy(
                        Kg[64:128, j2 * T:j2 * T + 1024], ps[pp][64:128, :]),
                        waits=[lastmm, state["kg%d" % j2]])
                    ev2b = sop(lambda s, j2=j2, pp=pp: s.copy(
                        Kg[64:128, j2 * T + 1024:j2 * T + 2048],
                        ps[pp + 1][64:128, :]))
                    r1, k_ev1 = rope(Kg[0:64, j2 * T:j2 * T + 1024],
                                     ps[pp][0:64, :],
                                     0, 1024, [lastmm, state["kg%d" % j2]])
                    r2, k_ev2 = rope(Kg[0:64, j2 * T + 1024:j2 * T + 2048],
                                     ps[pp + 1][0:64, :], 1024, 1024, [])
                    psfree[pp] = k_ev1
                    psfree[pp + 1] = k_ev2
                    kg_ready[j2] = [r1, r2, ev2a, ev2b]
                    sq1 = vop(lambda v, j2=j2: v.tensor_tensor(
                        PT[:, 0:1024], Kg[:, j2 * T:j2 * T + 1024],
                        Kg[:, j2 * T:j2 * T + 1024], ALU.mult),
                        waits=[r1, ev2a, state["pt"]])
                    sq2 = vop(lambda v, j2=j2: v.tensor_tensor(
                        PT[:, 1024:2048], Kg[:, j2 * T + 1024:j2 * T + 2048],
                        Kg[:, j2 * T + 1024:j2 * T + 2048], ALU.mult),
                        waits=[r2, ev2b])
                    tg = 2 - pp
                    mk = None
                    for nb in range(4):
                        pt = ps[tg + nb // 2]
                        sl = slice((nb % 2) * 512, (nb % 2) * 512 + 512)
                        mk = mm(lambda t, nb=nb, pt=pt, sl=sl: t.matmul(
                            pt[:, sl], ones128[:, :],
                            PT[:, nb * 512:nb * 512 + 512],
                            start=True, stop=True),
                            waits=([sq1, sq2] + pswait(tg) + pswait(tg + 1))
                            if nb == 0 else [], last=(nb == 3))
                    state["ptmk"] = mk
                    rk1 = vop(lambda v, tg=tg: v.reduce_max(
                        bcol[:, 0:1], ps[tg][:, :], AX.X), waits=[mk])
                    rk2 = vop(lambda v, tg=tg: v.reduce_max(
                        bcol[:, 1:2], ps[tg + 1][:, :], AX.X))
                    rk3 = vop(lambda v, h=h: v.tensor_tensor(
                        kmax[:, h:h + 1], bcol[:, 0:1], bcol[:, 1:2],
                        ALU.max))
                    psfree[tg] = rk1
                    psfree[tg + 1] = rk2
                    state["pt"] = rk3
                    u = vop(lambda v, h=h: v.tensor_tensor(
                        bcol[:, 2:3], qmax[:, h:h + 1], kmax[:, h:h + 1],
                        ALU.mult))
                    l1 = sop(lambda s: s.activation(bcol[:, 3:4],
                                                    bcol[:, 2:3], AF.Ln),
                             waits=[u])
                    l2 = sop(lambda s: s.activation(bcol[:, 4:5],
                                                    bcol[:, 3:4], AF.Exp,
                                                    scale=0.5,
                                                    bias=constb[:, 2:3]),
                             waits=[cb_d])
                    nb_edge[j2] = vop(lambda v, h=h: v.tensor_scalar(
                        negB[:, h:h + 1], bcol[:, 4:5], -1.0, -1.0,
                        ALU.mult, ALU.add), waits=[l2])
                    pump_pre(24)

                for j2 in range(2):
                    h = g * GH + pair * 2 + j2
                    jg = pair * 2 + j2
                    head_w = kg_ready[j2] + [q_done[0], q_done[1],
                                             state["qev2"]]
                    exp_w0 = [nb_edge[j2], state["ptmk"], state["pt"]]
                    prev_av = None
                    prev_sig = None
                    exd = [[None, None], [None, None]]
                    head_smm = [None]

                    def st_mm(kc):
                        pr = kc % 2
                        w = (head_w if kc == 0 else []) + pswait(pr)
                        for nb in range(2):
                            smm = mm(lambda t, h=h, j2=j2, kc=kc, nb=nb, pr=pr:
                                     t.matmul(
                                         ps[pr][:, nb * 512:nb * 512 + 512],
                                         Kg[:, j2 * T + kc * 128:
                                            j2 * T + kc * 128 + 128],
                                         Qs[:, h * NOWN + nb * 512:
                                            h * NOWN + nb * 512 + 512],
                                         start=True, stop=True),
                                     waits=w if nb == 0 else [],
                                     last=True)
                            head_smm[0] = smm
                            ex = sop(lambda s, kc=kc, pr=pr, h=h, nb=nb:
                                     s.activation(
                                         PT[:, pr * NOWN + nb * 512:
                                            pr * NOWN + nb * 512 + 512],
                                         ps[pr][:, nb * 512:nb * 512 + 512],
                                         AF.Exp, bias=negB[:, h:h + 1],
                                         scale=INV_SQRT_HD),
                                     waits=[smm] +
                                     (exp_w0 if (kc == 0 and nb == 0) else []))
                            exd[pr][nb] = ex
                        psfree[pr] = exd[pr][1]

                    def av_sig(kc):
                        nonlocal prev_av, prev_sig
                        pr = kc % 2
                        for nb in range(2):
                            prev_av = mm(
                                lambda t, kc=kc, jg=jg, nb=nb, pr=pr: t.matmul(
                                    ps[2][:, nb * 512:nb * 512 + 512],
                                    Vg[:, kc * 512 + jg * 128:
                                       kc * 512 + jg * 128 + 128],
                                    PT[:, pr * NOWN + nb * 512:
                                       pr * NOWN + nb * 512 + 512],
                                    start=(kc == 0), stop=(kc == KC - 1)),
                                waits=([exd[pr][nb], vg_done] + pswait(2))
                                if (kc == 0 and nb == 0) else [exd[pr][nb]],
                                last=(kc == KC - 1 and nb == 1))
                            prev_sig = mm(
                                lambda t, kc=kc, nb=nb, pr=pr: t.matmul(
                                    ps[3][:, nb * 512:nb * 512 + 512],
                                    ones128[:, :],
                                    PT[:, pr * NOWN + nb * 512:
                                       pr * NOWN + nb * 512 + 512],
                                    start=(kc == 0), stop=(kc == KC - 1)),
                                waits=pswait(3)
                                if (kc == 0 and nb == 0) else [],
                                last=(kc == KC - 1 and nb == 1))

                    st_mm(0)
                    for kc in range(KC):
                        if kc + 1 < KC:
                            st_mm(kc + 1)
                        av_sig(kc)
                    ln1 = sop(lambda s: s.activation(lnscr[:, :], ps[3][:, :],
                                                     AF.Ln),
                              waits=[prev_sig])
                    ln2 = sop(lambda s: s.activation(rcp[:, :], lnscr[:, :],
                                                     AF.Exp, scale=-1.0),
                              waits=[state["yv"]])
                    yv = vop(lambda v, h=h: v.tensor_tensor(
                        ybuf[:, h * NOWN:(h + 1) * NOWN], ps[2][:, :],
                        rcp[:, :], ALU.mult), waits=[ln2, prev_av])
                    psfree[2] = yv
                    psfree[3] = ln1
                    state["yv"] = yv
                    state["kg%d" % j2] = head_smm[0]
                    state["vg_mm"] = prev_av
                    pump_pre(24)
        y_all = state["yv"]

        if debug:
            ddump(dbg_nb[:, 0:NH], negB[:, :], waits=[(VP, VP.n), (SP, SP.n)])
            ddump(dbg_nb[:, NH:2 * NH], qmax[:, :], waits=[])
            ddump(dbg_nb[:, 2 * NH:3 * NH], kmax[:, :], waits=[])
        pump_pre(10 ** 9)

        def wscale(col0, ncols, nelem, c_sw, c_th, c_nth, swmul, waits):
            red = vop(lambda v: v.reduce_sum(
                bcol[:, 5:6], wsum[:, col0:col0 + ncols], AX.X), waits=waits)
            sm = halve(lambda lo, c: bcol[lo:lo + c, 5:6],
                       (bcol[:, 6:7], bcol[:, 7:8]),
                       ALU.add, bcol[0:1, 0:1], [red], n=1)
            vop(lambda v, nelem=nelem: v.tensor_scalar(
                bcol[0:1, 1:2], bcol[0:1, 0:1], 1.0 / nelem, None,
                ALU.mult), waits=[sm])
            u2 = vop(lambda v: v.tensor_single_scalar(
                bcol[0:1, 2:3], bcol[0:1, 1:2], 1e-5, ALU.max))
            m1 = mm(lambda t: t.matmul(
                ps[3][:, 0:1], ones1f[:, :],
                bcol[0:1, 2:3], start=True, stop=True),
                waits=[u2, ones1_d] + pswait(3), last=True)
            vop(lambda v, c_sw=c_sw, swmul=swmul: v.tensor_scalar(
                thc[:, c_sw:c_sw + 1], ps[3][:, 0:1], swmul, None, ALU.mult),
                waits=[m1])
            vop(lambda v, c_th=c_th: v.tensor_scalar(
                thc[:, c_th:c_th + 1], ps[3][:, 0:1], 0.5, None, ALU.mult))
            c3 = vop(lambda v, c_nth=c_nth: v.tensor_scalar(
                thc[:, c_nth:c_nth + 1], ps[3][:, 0:1], -0.5, None, ALU.mult))
            psfree[3] = c3
            return c3

        th1_d = wscale(0, 256, DFF * D, 0, 1, 2, -0.5,
                       [pre_red_last[0]] + [x for x in pre_slot_free])
        th2_d = wscale(256, 256, DFF * D, 3, 4, 5, -0.5, [])
        th3_d = wscale(512, 256, D * DFF, 6, 7, 8, -0.5, [])
        if debug:
            ddump(dbg_thc[:, 0:9], thc[:, 0:9], waits=[th3_d])

        E.flush()
        stC.close()
        stP.close()

        stD = ExitStack()
        wstgo = sb(stD, "wstgo", [128, 2 * D], BF16)
        xown = sb(stD, "xown", [128, 2 * NOWN], F32)
        x2stg = sb(stD, "x2stg", [128, 2 * NOWN], F32)

        oc_st = [None, None]
        ev_d = [None, None]
        wo_mm = [None, None]
        for oc in range(KC):
            pr = oc % 2
            ld = gdma(s_wo[pr], lambda g, oc=oc, pr=pr: g.dma_start(
                out=wstgo[:, pr * D:pr * D + D], in_=wot[oc]),
                waits=[y_all, wo_mm[pr]])
            xo = ydma(s_xo[pr], lambda sy, oc=oc, pr=pr: sy.dma_start(
                out=xown[:, pr * NOWN:pr * NOWN + NOWN],
                in_=xt[oc * 128:(oc + 1) * 128, 0:NOWN]),
                waits=[ev_d[pr]] + ([y_all] if oc < 2 else []))
            lastmm = None
            for kc in range(KC):
                for nb in range(2):
                    lastmm = mm(lambda t, oc=oc, kc=kc, nb=nb, pr=pr: t.matmul(
                        ps[pr][:, nb * 512:nb * 512 + 512],
                        wstgo[:, pr * D + kc * 128: pr * D + kc * 128 + 128],
                        ybuf[:, kc * NOWN + nb * 512: kc * NOWN + nb * 512 + 512],
                        start=(kc == 0), stop=(kc == KC - 1)),
                        waits=([ld, y_all] + pswait(pr))
                        if (kc == 0 and nb == 0) else [],
                        last=(kc == KC - 1 and nb == 1))
            wo_mm[pr] = lastmm
            ev_d[pr] = vop(lambda v, oc=oc, pr=pr: v.scalar_tensor_tensor(
                x2stg[:, pr * NOWN:pr * NOWN + NOWN], ps[pr][:, :],
                ga2[:, oc:oc + 1], xown[:, pr * NOWN:pr * NOWN + NOWN],
                ALU.mult, ALU.add), waits=[lastmm, xo, ga_done, oc_st[pr]])
            psfree[pr] = ev_d[pr]
            oc_st[pr] = ydma(s_x2w[pr], lambda sy, oc=oc, pr=pr: sy.dma_start(
                out=x2s[oc * 128:(oc + 1) * 128, :],
                in_=x2stg[:, pr * NOWN:pr * NOWN + NOWN]), waits=[ev_d[pr]])
        x2_written = [oc_st[0], oc_st[1]]

        E.flush()
        stD.close()
        stA.close()

        stE = ExitStack()
        xq = sb(stE, "xq", [128, KC * NOWN], BF16)

        stF = ExitStack()
        hf = sb(stF, "hf", [128, KC * NOWN], F32)
        x2c = sb(stF, "x2c", [128, 2 * NOWN], F32)
        cbf = sb(stF, "cbf", [128, 2 * NOWN], BF16)
        csq = sb(stF, "csq", [128, 2 * NOWN], BF16)
        m2b = sb(stF, "m2b", [128, NOWN], F32)
        rstd2b = sb(stF, "rstd2b", [128, NOWN], F32)
        zscr = sb(stF, "zscr", [128, NOWN], F32)
        z2scr = sb(stF, "z2scr", [128, NOWN], F32)
        zint = sb(stF, "zint", [128, NOWN], I32)
        mrun = sb(stF, "mrun", [128, 4 * NOWN], F32)
        mha = sb(stF, "mha", [64, NOWN], F32)
        mhb = sb(stF, "mhb", [64, NOWN], F32)
        rows = sb(stF, "rows", [1, 4 * NOWN], F32)
        txb = sb(stF, "txb", [128, NOWN], F32)
        sxb = sb(stF, "sxb", [128, NOWN], F32)

        def row(i):
            return rows[0:1, i * NOWN:(i + 1) * NOWN]

        ld2 = [None, None]
        cast_d = [None, None]
        mm2a = [None, None]
        mm2b = [None, None]
        for kc in range(KC):
            pr = kc % 2
            ld2[pr] = ydma(s_x2r[pr], lambda sy, kc=kc, pr=pr: sy.dma_start(
                out=x2c[:, pr * NOWN:pr * NOWN + NOWN],
                in_=x2s[kc * 128:(kc + 1) * 128, :]),
                waits=x2_written + [cast_d[pr]])
            c1 = vop(lambda v, pr=pr: v.tensor_copy(
                cbf[:, pr * NOWN:pr * NOWN + NOWN],
                x2c[:, pr * NOWN:pr * NOWN + NOWN]), waits=[ld2[pr], mm2a[pr]])
            cast_d[pr] = c1
            c2 = vop(lambda v, pr=pr: v.tensor_tensor(
                csq[:, pr * NOWN:pr * NOWN + NOWN],
                cbf[:, pr * NOWN:pr * NOWN + NOWN],
                cbf[:, pr * NOWN:pr * NOWN + NOWN], ALU.mult),
                waits=[mm2b[pr]])
            for nb in range(2):
                mm2a[pr] = mm(lambda t, kc=kc, pr=pr, nb=nb: t.matmul(
                    ps[0][:, nb * 512:nb * 512 + 512], ones128[:, :],
                    cbf[:, pr * NOWN + nb * 512: pr * NOWN + nb * 512 + 512],
                    start=(kc == 0), stop=(kc == KC - 1)),
                    waits=([c1] + pswait(0)) if nb == 0 else [],
                    last=(nb == 1))
            for nb in range(2):
                mm2b[pr] = mm(lambda t, kc=kc, pr=pr, nb=nb: t.matmul(
                    ps[1][:, nb * 512:nb * 512 + 512], ones128[:, :],
                    csq[:, pr * NOWN + nb * 512: pr * NOWN + nb * 512 + 512],
                    start=(kc == 0), stop=(kc == KC - 1)),
                    waits=([c2] + pswait(1)) if nb == 0 else [],
                    last=(nb == 1))
        st2a = mm2a[(KC - 1) % 2]
        st2b = mm2b[(KC - 1) % 2]
        vop(lambda v: v.tensor_scalar(m2b[:], ps[0][:, :], 1.0 / D, None,
                                      ALU.mult), waits=[st2a])
        vop(lambda v: v.tensor_tensor(zscr[:], m2b[:], m2b[:], ALU.mult))
        var2 = vop(lambda v: v.scalar_tensor_tensor(
            z2scr[:], ps[1][:, :], 1.0 / D, zscr[:], ALU.mult, ALU.subtract),
            waits=[st2b])
        sop(lambda s: s.activation(zscr[:], z2scr[:], AF.Ln,
                                   bias=constb[:, 0:1]), waits=[var2, cb_d])
        rstd2_d = sop(lambda s: s.activation(rstd2b[:], zscr[:], AF.Exp,
                                             scale=-0.5))
        psfree[0] = var2
        psfree[1] = var2

        hf_d = [None, None]
        mr_d = None
        mmh = [None, None]
        for kc in range(KC):
            pr = kc % 2
            ld2[pr] = ydma(s_x2r[pr], lambda sy, kc=kc, pr=pr: sy.dma_start(
                out=x2c[:, pr * NOWN:pr * NOWN + NOWN],
                in_=x2s[kc * 128:(kc + 1) * 128, :]),
                waits=[mm2a[pr], mm2b[pr], cast_d[pr], hf_d[pr]])
            u1 = vop(lambda v, pr=pr: v.tensor_tensor(
                zscr[:], x2c[:, pr * NOWN:pr * NOWN + NOWN], m2b[:],
                ALU.subtract),
                waits=[ld2[pr], rstd2_d, cbpl_d, (SP, SP.n)])
            vop(lambda v: v.tensor_tensor(z2scr[:], zscr[:], rstd2b[:],
                                          ALU.mult))
            hf_d[pr] = vop(lambda v, kc=kc: v.tensor_scalar(
                hf[:, kc * NOWN:(kc + 1) * NOWN], z2scr[:],
                cbpl1[:, kc:kc + 1], modfm[:, 48 + kc:49 + kc],
                ALU.mult, ALU.add))
            if kc == 0:
                vop(lambda v: v.tensor_copy(mrun[:, 0:NOWN], hf[:, 0:NOWN]))
                mr_d = vop(lambda v: v.tensor_copy(mrun[:, 2 * NOWN:3 * NOWN],
                                                   hf[:, 0:NOWN]))
            else:
                src = ((kc - 1) % 2) * NOWN
                dst = (kc % 2) * NOWN
                vop(lambda v, kc=kc, src=src, dst=dst: v.tensor_tensor(
                    mrun[:, dst:dst + NOWN], mrun[:, src:src + NOWN],
                    hf[:, kc * NOWN:(kc + 1) * NOWN], ALU.max))
                mr_d = vop(lambda v, kc=kc, src=src, dst=dst: v.tensor_tensor(
                    mrun[:, 2 * NOWN + dst:2 * NOWN + dst + NOWN],
                    mrun[:, 2 * NOWN + src:2 * NOWN + src + NOWN],
                    hf[:, kc * NOWN:(kc + 1) * NOWN], ALU.min))
            sqh = vop(lambda v, kc=kc, pr=pr: v.tensor_tensor(
                csq[:, pr * NOWN:pr * NOWN + NOWN],
                hf[:, kc * NOWN:(kc + 1) * NOWN],
                hf[:, kc * NOWN:(kc + 1) * NOWN], ALU.mult),
                waits=[mmh[pr]])
            for nb in range(2):
                mmh[pr] = mm(lambda t, kc=kc, pr=pr, nb=nb: t.matmul(
                    ps[2][:, nb * 512:nb * 512 + 512], ones128[:, :],
                    csq[:, pr * NOWN + nb * 512: pr * NOWN + nb * 512 + 512],
                    start=(kc == 0), stop=(kc == KC - 1)),
                    waits=([sqh] + pswait(2)) if nb == 0 else [],
                    last=(nb == 1))
        sumhf2 = mmh[(KC - 1) % 2]

        mr_off = ((KC - 1) % 2) * NOWN
        mrow_d = halve(lambda lo, c: mrun[lo:lo + c, mr_off:mr_off + NOWN],
                       (mha, mhb), ALU.max, row(0), [mr_d])

        r1_ = sop(lambda s: s.activation(row(1), ps[2][0:1, :], AF.Ln,
                                         scale=1.0 / D, bias=constb[0:1, 1:2]),
                  waits=[sumhf2, cb_d])
        r2_ = sop(lambda s: s.activation(row(2), row(1), AF.Exp, scale=-0.5))
        psfree[2] = r1_
        t1_ = vop(lambda v: v.tensor_single_scalar(row(1), row(0), 1e-30,
                                                   ALU.max),
                  waits=[mrow_d, r2_])
        t2_ = sop(lambda s: s.activation(row(3), row(1), AF.Ln), waits=[t1_])
        tx_d = sop(lambda s: s.activation(row(1), row(3), AF.Exp, scale=-1.0,
                                          bias=constb[0:1, 3:4]))
        s1_ = vop(lambda v: v.tensor_tensor(row(3), row(2), row(0), ALU.mult),
                  waits=[r2_, tx_d])
        s2_ = vop(lambda v: v.tensor_single_scalar(row(0), row(3), 1e-5,
                                                   ALU.max))
        sx_d = vop(lambda v: v.tensor_single_scalar(row(3), row(0),
                                                    1.0 / 127.0, ALU.mult))

        bc_d = []
        for rsrc, dst, dep in ((row(1), txb, tx_d), (row(3), sxb, sx_d)):
            m2_ = None
            for nb in range(2):
                m2_ = mm(lambda t, rsrc=rsrc, nb=nb: t.matmul(
                    ps[3][:, nb * 512:nb * 512 + 512], ones1f[:, :],
                    rsrc[0:1, nb * 512:nb * 512 + 512], start=True, stop=True),
                    waits=([dep, ones1_d] + pswait(3)) if nb == 0 else [],
                    last=(nb == 1))
            bcv = vop(lambda v, dst=dst: v.tensor_copy(dst[:], ps[3][:, :]),
                      waits=[m2_])
            psfree[3] = bcv
            bc_d.append(bcv)
        bc_tx, bc_sx = bc_d

        xq_d = None
        for kc in range(KC):
            vop(lambda v, kc=kc: v.tensor_tensor(
                zscr[:], hf[:, kc * NOWN:(kc + 1) * NOWN], txb[:], ALU.mult),
                waits=[bc_tx, bc_sx])
            vop(lambda v: v.tensor_copy(zint[:], zscr[:]))
            vop(lambda v: v.tensor_tensor(z2scr[:], zscr[:], zint[:],
                                          ALU.subtract))
            vop(lambda v: v.tensor_single_scalar(m2b[:], z2scr[:], 0.5,
                                                 ALU.is_ge))
            vop(lambda v: v.tensor_tensor(rstd2b[:], zint[:], m2b[:], ALU.add))
            vop(lambda v: v.scalar_tensor_tensor(
                zscr[:], z2scr[:], -0.5, rstd2b[:], ALU.is_le, ALU.subtract))
            xq_d = vop(lambda v, kc=kc: v.tensor_tensor(
                xq[:, kc * NOWN:(kc + 1) * NOWN], zscr[:], sxb[:], ALU.mult))
        xq_all = xq_d
        if debug:
            for kc in range(KC):
                ddump(dbg_hf[kc * 128:(kc + 1) * 128, :],
                      hf[:, kc * NOWN:(kc + 1) * NOWN], waits=[xq_all])
                ddump(dbg_xq[kc * 128:(kc + 1) * 128, :],
                      xq[:, kc * NOWN:(kc + 1) * NOWN], waits=[xq_all])
            ddump(dbg_rows[:, 0:NOWN], txb[:, :], waits=[xq_all])
            ddump(dbg_rows[:, NOWN:2 * NOWN], sxb[:, :], waits=[xq_all])

        E.flush()
        stF.close()

        stR = ExitStack()
        hff = sb(stR, "hff", [128, FC * NOWN], BF16, side="right")
        thb = sb(stR, "thb", [128, NOWN], F32, side="right")
        sxh3b = sb(stR, "sxh3b", [128, NOWN], F32, side="right")

        stG0 = ExitStack()
        mrun3 = sb(stG0, "mrun3", [128, 2 * NOWN], BF16)

        stG = ExitStack()
        stg = sb(stG, "stg", [128, 3 * NOWN], F32)
        tern = sb(stG, "tern", [128, 2 * D], BF16)
        sA = sb(stG, "sA", [128, 2 * NOWN], BF16)
        sB = sb(stG, "sB", [128, 2 * NOWN], BF16)
        slscr = sb(stG, "slscr", [128, NOWN], BF16)

        e3_start = [(VP, VP.n)]
        if debug:
            e3_start.append((s_dbg, s_dbg.n))
        slot_free = [list(e3_start) for _ in range(3)]
        e3_hs = [0]

        mm_b1 = [None]
        mm_b1h = [None]
        mm_b2 = [None]
        mm_b2h = [None]
        sil_d = [None]
        stt_d = [None]
        hsq_d = [None]
        mr3_d = [None]
        sq_mm = [None]
        q1h = [None, None]
        q2h = [None, None]

        comb_d = [None, None]

        def e3_load_quant(oc):
            for wsel, base, thcol, nthcol, mmh, qh, wt in (
                    (0, 0, 1, 2, (mm_b1h, mm_b1), q1h, w1t),
                    (1, D, 4, 5, (mm_b2h, mm_b2), q2h, w2t)):
                for hh2 in range(2):
                    hs = e3_hs[0]
                    e3_hs[0] = (hs + 1) % 3
                    pa = hh2 * NOWN
                    ssl = slice(hs * NOWN, hs * NOWN + NOWN)
                    tsl = slice(base + hh2 * NOWN, base + hh2 * NOWN + NOWN)
                    ldw = gdma(s_st[hs], lambda g, oc=oc, ssl=ssl, wt=wt,
                               hh2=hh2: g.dma_start(
                        out=stg[:, ssl],
                        in_=wt[oc][:, hh2 * NOWN:hh2 * NOWN + NOWN]),
                        waits=slot_free[hs] + [th1_d, th2_d, th3_d])
                    s1 = sop(lambda s, ssl=ssl, pa=pa, nthcol=nthcol:
                             s.activation(
                                 sA[:, pa:pa + NOWN], stg[:, ssl], AF.Sign,
                                 bias=thc[:, nthcol:nthcol + 1]),
                             waits=[ldw, comb_d[hh2]])
                    s2 = sop(lambda s, ssl=ssl, pa=pa, thcol=thcol:
                             s.activation(
                                 sB[:, pa:pa + NOWN], stg[:, ssl], AF.Sign,
                                 bias=thc[:, thcol:thcol + 1]))
                    q = vop(lambda v, tsl=tsl, pa=pa: v.tensor_tensor(
                        tern[:, tsl], sA[:, pa:pa + NOWN],
                        sB[:, pa:pa + NOWN], ALU.add),
                        waits=[s2, mmh[hh2][0]])
                    qh[hh2] = q
                    comb_d[hh2] = q
                    slot_free[hs] = [q]

        def e3_sumh2(oc):
            for nb in range(2):
                sq_mm[0] = mm(lambda t, oc=oc, nb=nb: t.matmul(
                    ps[2][:, nb * 512:nb * 512 + 512], ones128[:, :],
                    slscr[:, nb * 512:nb * 512 + 512],
                    start=(oc == 0), stop=(oc == FC - 1)),
                    waits=([hsq_d[0]] + pswait(2)) if nb == 0 else [],
                    last=(nb == 1))

        e3_load_quant(0)
        for oc in range(FC):
            qq1 = (q1h[0], q1h[1])
            qq2 = (q2h[0], q2h[1])
            for kc in range(KC):
                for nb in range(2):
                    e = mm(lambda t, kc=kc, nb=nb: t.matmul(
                        ps[0][:, nb * 512:nb * 512 + 512],
                        tern[:, kc * 128:(kc + 1) * 128],
                        xq[:, kc * NOWN + nb * 512: kc * NOWN + nb * 512 + 512],
                        start=(kc == 0), stop=(kc == KC - 1)),
                        waits=([qq1[0], xq_all] + pswait(0))
                        if (kc == 0 and nb == 0) else
                        ([qq1[1]] if (kc == KC // 2 and nb == 0) else []),
                        last=(nb == 1 and kc in (KC // 2 - 1, KC - 1)))
                    if kc == KC // 2 - 1 and nb == 1:
                        mm_b1h[0] = e
            mm_b1[0] = e
            for kc in range(KC):
                for nb in range(2):
                    e = mm(lambda t, kc=kc, nb=nb: t.matmul(
                        ps[1][:, nb * 512:nb * 512 + 512],
                        tern[:, D + kc * 128: D + kc * 128 + 128],
                        xq[:, kc * NOWN + nb * 512: kc * NOWN + nb * 512 + 512],
                        start=(kc == 0), stop=(kc == KC - 1)),
                        waits=([qq2[0]] + pswait(1))
                        if (kc == 0 and nb == 0) else
                        ([qq2[1]] if (kc == KC // 2 and nb == 0) else []),
                        last=(nb == 1 and kc in (KC // 2 - 1, KC - 1)))
                    if kc == KC // 2 - 1 and nb == 1:
                        mm_b2h[0] = e
            mm_b2[0] = e
            if oc > 0:
                e3_sumh2(oc - 1)
            if oc + 1 < FC:
                e3_load_quant(oc + 1)
            sil_d[0] = sop(lambda s: s.activation(slscr[:, :], ps[0][:, :],
                                                  AF.Silu, scale=thc[:, 0:1]),
                           waits=[mm_b1[0], th1_d, sq_mm[0], mr3_d[0]])
            stt_d[0] = vop(lambda v, oc=oc: v.scalar_tensor_tensor(
                hff[:, oc * NOWN:(oc + 1) * NOWN], ps[1][:, :], thc[:, 3:4],
                slscr[:, :], ALU.mult, ALU.mult), waits=[mm_b2[0], sil_d[0]])
            psfree[0] = sil_d[0]
            psfree[1] = stt_d[0]
            if debug:
                ddump(dbg_hff[oc * 128:(oc + 1) * 128, :],
                      hff[:, oc * NOWN:(oc + 1) * NOWN], waits=[stt_d[0]])
            hsq_d[0] = vop(lambda v, oc=oc: v.tensor_tensor(
                slscr[:, :], hff[:, oc * NOWN:(oc + 1) * NOWN],
                hff[:, oc * NOWN:(oc + 1) * NOWN], ALU.mult),
                waits=[sq_mm[0], sil_d[0], stt_d[0]])
            if oc == 0:
                mr3_d[0] = vop(lambda v: v.tensor_copy(mrun3[:, 0:NOWN],
                                                       slscr[:, :]))
            else:
                src = ((oc - 1) % 2) * NOWN
                dst = (oc % 2) * NOWN
                mr3_d[0] = vop(lambda v, src=src, dst=dst: v.tensor_tensor(
                    mrun3[:, dst:dst + NOWN], mrun3[:, src:src + NOWN],
                    slscr[:, :], ALU.max))
        e3_sumh2(FC - 1)
        sq_mm = sq_mm[0]
        mr3_d = mr3_d[0]

        E.flush()
        stG.close()

        stGt = ExitStack()
        mha3 = sb(stGt, "mha3", [64, NOWN], F32)
        mhb3 = sb(stGt, "mhb3", [64, NOWN], F32)
        rows3 = sb(stGt, "rows3", [1, 4 * NOWN], F32)

        def row3(i):
            return rows3[0:1, i * NOWN:(i + 1) * NOWN]

        mr3_off = ((FC - 1) % 2) * NOWN
        mrow3_d = halve(lambda lo, c: mrun3[lo:lo + c, mr3_off:mr3_off + NOWN],
                        (mha3, mhb3), ALU.max, row3(0), [mr3_d])
        sop(lambda s: s.activation(row3(1), ps[2][0:1, :], AF.Ln,
                                   scale=1.0 / DFF, bias=constb[0:1, 1:2]),
            waits=[sq_mm])
        rh2 = sop(lambda s: s.activation(row3(2), row3(1), AF.Exp, scale=-0.5))
        psfree[2] = (SP, SP.n)
        h1_ = vop(lambda v: v.tensor_single_scalar(row3(1), row3(0), 1e-30,
                                                   ALU.max),
                  waits=[mrow3_d, rh2])
        h2_ = sop(lambda s: s.activation(row3(3), row3(1), AF.Ln), waits=[h1_])
        th_d = sop(lambda s: s.activation(row3(1), row3(3), AF.Exp, scale=-0.5,
                                          bias=constb[0:1, 3:4]))
        m_d = sop(lambda s: s.activation(row3(0), row3(3), AF.Exp, scale=0.5))
        g1_ = vop(lambda v: v.tensor_tensor(row3(3), row3(2), row3(0),
                                            ALU.mult), waits=[rh2, th_d, m_d])
        g2_ = vop(lambda v: v.tensor_single_scalar(row3(0), row3(3), 1e-5,
                                                   ALU.max))
        g3_ = vop(lambda v: v.tensor_scalar(row3(3), row3(0), thc[0:1, 6:7],
                                            None, ALU.mult), waits=[th3_d])
        sxh_d = vop(lambda v: v.tensor_single_scalar(row3(0), row3(3),
                                                     1.0 / 127.0, ALU.mult))
        for rsrc, dst, dep in ((row3(1), thb, th_d), (row3(0), sxh3b, sxh_d)):
            m2_ = None
            for nb in range(2):
                m2_ = mm(lambda t, rsrc=rsrc, nb=nb: t.matmul(
                    ps[3][:, nb * 512:nb * 512 + 512], ones1f[:, :],
                    rsrc[0:1, nb * 512:nb * 512 + 512], start=True, stop=True),
                    waits=([dep, sxh_d] + pswait(3)) if nb == 0 else [],
                    last=(nb == 1))
            bcv = vop(lambda v, dst=dst: v.tensor_copy(dst[:], ps[3][:, :]),
                      waits=[m2_])
            psfree[3] = bcv
        bc_all = psfree[3]

        E.flush()
        stGt.close()
        stG0.close()
        stE.close()

        stH = ExitStack()
        zscr4 = sb(stH, "zscr4", [128, NOWN], F32)
        z2scr4 = sb(stH, "z2scr4", [128, 2 * NOWN], F32)
        zint4 = sb(stH, "zint4", [128, 2 * NOWN], I32)
        w3stg = sb(stH, "w3stg", [128, 2 * 1024], F32)
        tern3 = sb(stH, "tern3", [128, 2 * 4096], BF16)
        sA4 = sb(stH, "sA4", [128, 2048], BF16)
        sB4 = sb(stH, "sB4", [128, 2048], BF16)
        sA4w = sb(stH, "sA4w", [128, 1024], BF16)
        sB4w = sb(stH, "sB4w", [128, 1024], BF16)
        x2c4 = sb(stH, "x2c4", [128, NOWN], F32)
        outst = sb(stH, "outst", [128, NOWN], F32)

        mm_oh = [None, None]
        mm_o = [None, None]
        ev4 = [None, None]
        st4 = [None]
        tq3 = [None, None]
        ld3_free = [None, None]

        s4_d = [None]

        def e4_quant_half(oc, hhf):
            for qq in range(4):
                sl = qq % 2
                ld = gdma(s_w3[sl], lambda g, oc=oc, hhf=hhf, qq=qq, sl=sl:
                          g.dma_start(
                    out=w3stg[:, sl * 1024:sl * 1024 + 1024],
                    in_=w3t[oc][:, hhf * 4096 + qq * 1024:
                                hhf * 4096 + qq * 1024 + 1024]),
                    waits=[th3_d, ld3_free[sl], bc_all])
                sw1_ = sop(lambda s, sl=sl: s.activation(
                    sA4w[:, :], w3stg[:, sl * 1024:sl * 1024 + 1024],
                    AF.Sign, bias=thc[:, 8:9]), waits=[ld, s4_d[0]])
                sw2_ = sop(lambda s, sl=sl: s.activation(
                    sB4w[:, :], w3stg[:, sl * 1024:sl * 1024 + 1024],
                    AF.Sign, bias=thc[:, 7:8]))
                tq3[hhf] = vop(lambda v, hhf=hhf, qq=qq: v.tensor_tensor(
                    tern3[:, hhf * 4096 + qq * 1024:
                          hhf * 4096 + qq * 1024 + 1024],
                    sA4w[:, :], sB4w[:, :], ALU.add),
                    waits=[sw2_, mm_oh[hhf]])
                s4_d[0] = tq3[hhf]
                ld3_free[sl] = tq3[hhf]

        e4_quant_half(0, 0)
        e4_quant_half(0, 1)

        hq_d = None
        hq_w0 = [bc_all] + ([(s_dbg, s_dbg.n)] if debug else [])
        sh_q = [None] * FC

        def hq_front(oc):
            pr = oc % 2
            vop(lambda v, oc=oc: v.tensor_tensor(
                zscr4[:], hff[:, oc * NOWN:(oc + 1) * NOWN], thb[:],
                ALU.mult), waits=(hq_w0 if oc == 0 else []) +
                ([sh_q[oc - 2]] if oc >= 2 else []))
            vop(lambda v, pr=pr: v.tensor_copy(
                zint4[:, pr * NOWN:pr * NOWN + NOWN], zscr4[:]))
            rr = vop(lambda v, pr=pr: v.tensor_tensor(
                z2scr4[:, pr * NOWN:pr * NOWN + NOWN], zscr4[:],
                zint4[:, pr * NOWN:pr * NOWN + NOWN], ALU.subtract))
            sh1 = sop(lambda s, pr=pr: s.activation(
                sA4[:, pr * NOWN:pr * NOWN + NOWN],
                z2scr4[:, pr * NOWN:pr * NOWN + NOWN],
                AF.Sign, bias=constb[:, 4:5]), waits=[rr, cb_d])
            sh_q[oc] = sop(lambda s, pr=pr: s.activation(
                sB4[:, pr * NOWN:pr * NOWN + NOWN],
                z2scr4[:, pr * NOWN:pr * NOWN + NOWN],
                AF.Sign, bias=constb[:, 5:6]))

        hq_done = [None] * FC

        def hq_back(oc):
            nonlocal hq_d
            pr = oc % 2
            vop(lambda v, pr=pr: v.tensor_tensor(
                outst[:, :], sA4[:, pr * NOWN:pr * NOWN + NOWN],
                sB4[:, pr * NOWN:pr * NOWN + NOWN], ALU.add),
                waits=[sh_q[oc]])
            hq_d = vop(lambda v, oc=oc, pr=pr: v.scalar_tensor_tensor(
                hff[:, oc * NOWN:(oc + 1) * NOWN], outst[:, :], -0.5,
                zint4[:, pr * NOWN:pr * NOWN + NOWN],
                ALU.mult, ALU.subtract))
            hq_done[oc] = hq_d

        hq_front(0)
        for oc in range(FC):
            if oc + 1 < FC:
                hq_front(oc + 1)
            hq_back(oc)
        hq_all = hq_d
        if debug:
            for oc in range(FC):
                ddump(dbg_hq[oc * 128:(oc + 1) * 128, :],
                      hff[:, oc * NOWN:(oc + 1) * NOWN], waits=[hq_all])
            ddump(dbg_rows3[:, 0:NOWN], thb[:, :], waits=[hq_all])
            ddump(dbg_rows3[:, NOWN:2 * NOWN], sxh3b[:, :], waits=[hq_all])

        for oc in range(KC):
            pr = oc % 2
            x2ld = ydma(s_x2r[0], lambda sy, oc=oc: sy.dma_start(
                out=x2c4[:, :],
                in_=x2s[oc * 128:(oc + 1) * 128, :]),
                waits=[ev4[0], ev4[1]] + ([(VP, VP.n)] if oc < 2 else []))
            lastmm = None
            for hhf in range(2):
                tq_now = tq3[hhf]
                for kc2 in range(FC // 2):
                    kc = hhf * (FC // 2) + kc2
                    for nb in range(2):
                        lastmm = mm(lambda t, kc=kc, nb=nb, pr=pr, hhf=hhf:
                                    t.matmul(
                                        ps[pr][:, nb * 512:nb * 512 + 512],
                                        tern3[:, hhf * 4096 + kc2 * 128:
                                              hhf * 4096 + kc2 * 128 + 128]
                                        if False else
                                        tern3[:, (kc % (FC // 2)) * 128 +
                                              hhf * 4096:
                                              (kc % (FC // 2)) * 128 +
                                              hhf * 4096 + 128],
                                        hff[:, kc * NOWN + nb * 512:
                                            kc * NOWN + nb * 512 + 512],
                                        start=(kc == 0), stop=(kc == FC - 1)),
                                    waits=(([tq_now] + pswait(pr))
                                           if (kc2 == 0 and nb == 0) else []) +
                                    ([hq_done[kc]]
                                     if (oc == 0 and nb == 0) else []),
                                    last=(kc2 == FC // 2 - 1 and nb == 1))
                mm_oh[hhf] = lastmm
                if hhf == 0 and oc + 1 < KC:
                    e4_quant_half(oc + 1, 0)
            if oc + 1 < KC:
                e4_quant_half(oc + 1, 1)
            mm_o[pr] = lastmm
            u_d = vop(lambda v, oc=oc, pr=pr: v.scalar_tensor_tensor(
                zscr4[:], ps[pr][:, :], gb2[:, oc:oc + 1], sxh3b[:],
                ALU.mult, ALU.mult), waits=[lastmm, gb_done, x2ld, st4[0]])
            ev4[pr] = vop(lambda v, oc=oc: v.tensor_tensor(
                outst[:, :], zscr4[:], x2c4[:, :], ALU.add))
            psfree[pr] = ev4[pr]
            st4[0] = ydma(s_out[0], lambda sy, oc=oc: sy.dma_start(
                out=outT[oc * 128:(oc + 1) * 128, :],
                in_=outst[:, :]), waits=[ev4[pr]])

        def final_waits(sy):
            for sm in all_dma_sems:
                if sm.n > 0:
                    sy.wait_ge(sm.h, sm.n)
        E.plain("sync", final_waits)
        E.flush()
        stH.close()
        stR.close()

    return nc


_cache = {}


def _tile_oc(w, noc):
    dout, din = w.shape
    kc = din // 128
    a = w.reshape(noc, 128, kc, 128).transpose(0, 3, 2, 1)
    return np.ascontiguousarray(a.reshape(noc, 128, din))


def _rope_tables(own0):
    inv = 1.0 / (10000.0 ** (np.arange(0, HD, 2, dtype=np.float64) / HD))
    t = np.arange(T, dtype=np.float64)
    ang = np.outer(inv, t)
    c = np.cos(ang)
    s = np.sin(ang)
    s2 = np.concatenate([-s[0:32], s[32:64]], axis=0)
    perm = np.concatenate([np.arange(own0, T), np.arange(0, own0)])
    return (np.ascontiguousarray(c[:, perm]).astype(np.float32),
            np.ascontiguousarray(s2[:, perm]).astype(np.float32))


def prepare_in_maps(L0_attn_x, L0_attn_wq, L0_attn_wk, L0_attn_wv, L0_attn_wo,
                    L0_ffn_w1, L0_ffn_w2, L0_ffn_w3, L0_c, L0_wmod):
    wq = np.asarray(L0_attn_wq, np.float32)
    wk = np.asarray(L0_attn_wk, np.float32)
    wv = np.asarray(L0_attn_wv, np.float32)
    wo = np.asarray(L0_attn_wo, np.float32)
    w1 = np.asarray(L0_ffn_w1, np.float32)
    w2 = np.asarray(L0_ffn_w2, np.float32)
    w3 = np.asarray(L0_ffn_w3, np.float32)
    wm = np.asarray(L0_wmod, np.float32)
    x = np.asarray(L0_attn_x, np.float32)
    c = np.asarray(L0_c, np.float32)

    shared = {
        "wqt": _tile_oc(wq, KC),
        "wkt": _tile_oc(wk, KC),
        "wot": _tile_oc(wo, KC),
        "w1t": _tile_oc(w1, FC),
        "w2t": _tile_oc(w2, FC),
        "w3t": _tile_oc(w3, KC),
    }
    wvr = wv.reshape(2 * NG, 256, KC, 128).transpose(0, 3, 2, 1)
    shared["wvt"] = np.ascontiguousarray(wvr.reshape(2 * NG, 128, KC * 256))
    wmr = wm.reshape(KC, 128, 96, 128).transpose(2, 1, 0, 3)
    shared["wmt"] = np.ascontiguousarray(wmr.reshape(96, 128, D))

    in_maps = []
    for core in range(8):
        b, p = core // 2, core % 2
        own0 = p * NOWN
        xtb = x[b].T
        xrot = np.ascontiguousarray(
            np.concatenate([xtb[:, own0:], xtb[:, :own0]], axis=1))
        rc, rs = _rope_tables(own0)
        m = dict(shared)
        m["xt"] = xrot
        m["cct"] = np.ascontiguousarray(c[b].reshape(KC, 128).T)
        m["rct"] = rc
        m["rst"] = rs
        in_maps.append(m)
    return in_maps


def kernel(**inputs):
    if "nc" not in _cache:
        _cache["nc"] = build_nc(detect_races=True)
    nc = _cache["nc"]
    in_maps = prepare_in_maps(**inputs)
    res = run_bass_kernel_spmd(nc, in_maps, core_ids=list(range(8)))
    out = np.empty((B, T, D), np.float32)
    for core in range(8):
        b, p = core // 2, core % 2
        out[b, p * NOWN:(p + 1) * NOWN, :] = res.results[core]["outT"].T
    return out


# version 09 (final)
# speedup vs baseline: 1.1664x; 1.0786x over previous
import math
import numpy as np
from contextlib import ExitStack

import concourse.bass as bass
import concourse.mybir as mybir
from concourse.bass_utils import run_bass_kernel_spmd

F32 = mybir.dt.float32
BF16 = mybir.dt.bfloat16
I32 = mybir.dt.int32
AF = mybir.ActivationFunctionType
ALU = mybir.AluOpType
AX = mybir.AxisListType

B, T, D = 4, 2048, 2048
NH, HD = 16, 128
DFF = 8192
KC = D // 128
FC = DFF // 128
NOWN = 1024
NG, GH = 4, 4

INV_SQRT_HD = float(np.float32(1.0 / math.sqrt(HD)))
LN_SQRT_HD = float(np.float32(math.log(math.sqrt(HD))))
LN_127 = float(np.float32(math.log(127.0)))
LN_2_SQHD = float(np.float32(math.log(2.0) - math.log(math.sqrt(HD))))


class Sem:
    __slots__ = ("h", "n")

    def __init__(self, h):
        self.h = h
        self.n = 0


class Emitter:

    def __init__(self, block):
        self.block = block
        self.buf = {e: [] for e in ("tensor", "vector", "scalar", "gpsimd", "sync")}
        self.water = {e: {} for e in self.buf}
        self.prog = {}

    def _push(self, eng, fn, waits, upd):
        agg = {}
        for s, v in waits:
            if s is None:
                continue
            assert v <= s.n, (eng, v, s.n, "wait on not-yet-planned sem value")
            if agg.get(id(s), (None, -1))[1] < v:
                agg[id(s)] = (s, v)
        wl = []
        wm = self.water[eng]
        for s, v in agg.values():
            if wm.get(id(s), -1) < v:
                wm[id(s)] = v
                wl.append((s.h, v))

        def closure(e, wl=wl, fn=fn, upd=upd):
            for h, v in wl:
                e.wait_ge(h, v)
            inst = fn(e)
            if upd is not None:
                inst.then_inc(upd[0], upd[1])

        self.buf[eng].append(closure)

    def chained(self, eng, fn, waits=()):
        ps = self.prog[eng]
        waits = [w for w in waits if w is not None]
        if ps.n > 0:
            self.water[eng].pop(id(ps), None)
            waits.append((ps, ps.n))
        ps.n += 1
        self._push(eng, fn, waits, (ps.h, 1))
        return (ps, ps.n)

    def plain(self, eng, fn, waits=(), inc=None):
        waits = [w for w in waits if w is not None]
        upd = None
        ret = None
        if inc is not None:
            s, k = inc
            s.n += k
            upd = (s.h, k)
            ret = (s, s.n)
        self._push(eng, fn, waits, upd)
        return ret

    def flush(self):
        for eng, ops in self.buf.items():
            if not ops:
                continue
            body = list(ops)
            ops.clear()

            def run(e, body=body):
                for f in body:
                    f(e)

            getattr(self.block, eng)(run)


def build_nc(detect_races=True, debug=False):
    nc = bass.Bass("TRN2", target_bir_lowering=False,
                   detect_race_conditions=detect_races)

    xt = nc.dram_tensor("xt", [D, T], F32, kind="ExternalInput").ap()
    wqt = nc.dram_tensor("wqt", [KC, 128, D], F32, kind="ExternalInput").ap()
    wkt = nc.dram_tensor("wkt", [KC, 128, D], F32, kind="ExternalInput").ap()
    wvt = nc.dram_tensor("wvt", [2 * NG, 128, KC * 256], F32, kind="ExternalInput").ap()
    wot = nc.dram_tensor("wot", [KC, 128, D], F32, kind="ExternalInput").ap()
    w1t = nc.dram_tensor("w1t", [FC, 128, D], F32, kind="ExternalInput").ap()
    w2t = nc.dram_tensor("w2t", [FC, 128, D], F32, kind="ExternalInput").ap()
    w3t = nc.dram_tensor("w3t", [KC, 128, DFF], F32, kind="ExternalInput").ap()
    wmt = nc.dram_tensor("wmt", [96, 128, D], F32, kind="ExternalInput").ap()
    cct = nc.dram_tensor("cct", [128, KC], F32, kind="ExternalInput").ap()
    rct = nc.dram_tensor("rct", [64, T], F32, kind="ExternalInput").ap()
    rst = nc.dram_tensor("rst", [64, T], F32, kind="ExternalInput").ap()
    outT = nc.dram_tensor("outT", [D, NOWN], F32, kind="ExternalOutput").ap()
    x2s = nc.dram_tensor("x2s", [D, NOWN], F32, kind="Internal").ap()
    if debug:
        dbg_hf = nc.dram_tensor("dbg_hf", [D, NOWN], F32, kind="ExternalOutput").ap()
        dbg_xq = nc.dram_tensor("dbg_xq", [D, NOWN], F32, kind="ExternalOutput").ap()
        dbg_thc = nc.dram_tensor("dbg_thc", [128, 12], F32, kind="ExternalOutput").ap()
        dbg_rows = nc.dram_tensor("dbg_rows", [128, 2 * NOWN], F32, kind="ExternalOutput").ap()
        dbg_hff = nc.dram_tensor("dbg_hff", [DFF, NOWN], F32, kind="ExternalOutput").ap()
        dbg_hq = nc.dram_tensor("dbg_hq", [DFF, NOWN], F32, kind="ExternalOutput").ap()
        dbg_rows3 = nc.dram_tensor("dbg_rows3", [128, 2 * NOWN], F32, kind="ExternalOutput").ap()
        dbg_nb = nc.dram_tensor("dbg_nb", [128, 3 * NH], F32, kind="ExternalOutput").ap()

    st = ExitStack()
    with st:
        sems = {}
        for name in ("vp", "sp", "gp", "tp"):
            sems[name] = Sem(st.enter_context(nc.semaphore(name)))
        VP, SP, GP, TP = (sems[k] for k in ("vp", "sp", "gp", "tp"))

        all_dma_sems = []

        def dsem(name):
            sm = Sem(st.enter_context(nc.semaphore(name)))
            all_dma_sems.append(sm)
            return sm

        blk = st.enter_context(nc.Block())
        E = Emitter(blk)
        E.prog = {"vector": VP, "scalar": SP, "gpsimd": GP}

        vop = lambda fn, waits=(): E.chained("vector", fn, waits)
        sop = lambda fn, waits=(): E.chained("scalar", fn, waits)
        gop = lambda fn, waits=(): E.chained("gpsimd", fn, waits)
        gdma = lambda sm, fn, waits=(): E.plain("gpsimd", fn, waits, inc=(sm, 16))
        ydma = lambda sm, fn, waits=(): E.plain("sync", fn, waits, inc=(sm, 16))

        s_cc, s_rc, s_rs = dsem("s_cc"), dsem("s_rc"), dsem("s_rs")
        s_x = [dsem("s_x0"), dsem("s_x1")]
        s_wm = [dsem("s_wm0"), dsem("s_wm1")]
        s_wq = [dsem("s_wq0"), dsem("s_wq1")]
        s_wv = dsem("s_wv")
        s_rope = dsem("s_rope")
        s_hv = dsem("s_hv")
        s_wo = [dsem("s_wo0"), dsem("s_wo1")]
        s_xo = [dsem("s_xo0"), dsem("s_xo1")]
        s_x2w = [dsem("s_x2w0"), dsem("s_x2w1")]
        s_x2r = [dsem("s_x2r0"), dsem("s_x2r1")]
        s_st = [dsem("s_st0"), dsem("s_st1"), dsem("s_st2"), dsem("s_st3")]
        s_w3 = [dsem("s_w30"), dsem("s_w31")]
        s_out = [dsem("s_out0"), dsem("s_out1")]
        s_dbg = dsem("s_dbg")

        def ddump(dst_ap, src_ap, waits=()):
            if not debug:
                return None
            return gdma(s_dbg, lambda g, dst_ap=dst_ap, src_ap=src_ap:
                        g.dma_start(out=dst_ap, in_=src_ap),
                        waits=list(waits) + [(s_dbg, s_dbg.n)])

        def mm(fn, waits=(), last=False):
            return E.plain("tensor", fn, waits, inc=(TP, 1) if last else None)

        ps = [st.enter_context(nc.psum_tensor(f"ps{i}", [128, 1024], F32))
              for i in range(4)]
        psfree = [None, None, None, None]

        def pswait(i):
            return [psfree[i]] if psfree[i] is not None else []

        def sb(stack, name, shape, dtype, side=None):
            kw = {"side": side} if side else {}
            return stack.enter_context(nc.sbuf_tensor(name, shape, dtype, **kw))

        ones128 = sb(st, "ones128", [128, 128], BF16)
        ones1f = sb(st, "ones1f", [1, 128], F32)
        modfm = sb(st, "modfm", [128, 96], F32)
        capl1 = sb(st, "capl1", [128, KC], F32)
        cbpl1 = sb(st, "cbpl1", [128, KC], F32)
        ga2 = sb(st, "ga2", [128, KC], F32)
        gb2 = sb(st, "gb2", [128, KC], F32)

        qmax = sb(st, "qmax", [128, NH], F32)
        kmax = sb(st, "kmax", [128, NH], F32)
        negB = sb(st, "negB", [128, NH], F32)
        bcol = sb(st, "bcol", [128, 8], F32)
        thc = sb(st, "thc", [128, 12], F32)
        constb = sb(st, "constb", [128, 7], F32)

        ones_d = vop(lambda v: v.memset(ones128[:], 1.0))
        ones1_d = vop(lambda v: v.memset(ones1f[:], 1.0))
        vop(lambda v: v.memset(constb[:, 0:1], 1e-5))
        vop(lambda v: v.memset(constb[:, 1:2], 1e-6))
        vop(lambda v: v.memset(constb[:, 2:3], -LN_SQRT_HD))
        vop(lambda v: v.memset(constb[:, 3:4], LN_127))
        vop(lambda v: v.memset(constb[:, 4:5], -0.5))
        vop(lambda v: v.memset(constb[:, 5:6], 0.5))
        cb_d = vop(lambda v: v.memset(constb[:, 6:7], LN_2_SQHD))
        ld_cc = gdma(s_cc, lambda g: g.dma_start(out=ccolf[:], in_=cct[:, :]))
        ld_rc = gdma(s_rc, lambda g: g.dma_start(out=cbT[:], in_=rct[:, :]))
        ld_rs = gdma(s_rs, lambda g: g.dma_start(out=sbT[:], in_=rst[:, :]))
        sc_done = sop(lambda s: s.activation(scol[:], ccolf[:], AF.Silu),
                      waits=[ld_cc])

        def halve(cur_fn, bufs, op, final_out, waits, n=NOWN):
            b0 = lambda lo, c: bufs[0][lo:lo + c, 0:n]
            b1 = lambda lo, c: bufs[1][lo:lo + c, 0:n]
            regions = [cur_fn, b0, b1]
            done = list(waits)
            p, lvl = 64, 0
            while p >= 1:
                cur = regions[(2 * lvl) % 3]
                tmp = regions[(2 * lvl + 1) % 3]
                nxt = regions[(2 * lvl + 2) % 3]
                d = gdma(s_hv, lambda g, cur=cur, tmp=tmp, p=p: g.dma_start(
                    out=tmp(0, p), in_=cur(p, p)), waits=done)
                if p > 1:
                    t = vop(lambda v, cur=cur, tmp=tmp, nxt=nxt, p=p:
                            v.tensor_tensor(nxt(0, p), cur(0, p), tmp(0, p),
                                            op), waits=[d])
                else:
                    t = vop(lambda v, cur=cur, tmp=tmp: v.tensor_tensor(
                        final_out, cur(0, 1), tmp(0, 1), op), waits=[d])
                done = [t]
                lvl += 1
                p //= 2
            return done[0]

        stA = ExitStack()
        hbuf = sb(stA, "hbuf", [128, KC * T], BF16)
        ybuf = sb(stA, "ybuf", [128, NH * NOWN], BF16)
        cbT = sb(stA, "cbT", [64, T], BF16)
        sbT = sb(stA, "sbT", [64, T], BF16)

        stB = ExitStack()
        xstr = sb(stB, "xstr", [128, 2 * T], BF16)
        xsq = sb(stB, "xsq", [128, 2 * T], BF16)
        mst = sb(stB, "mst", [128, T], F32)
        rstd = sb(stB, "rstd", [128, T], F32)
        uscr = sb(stB, "uscr", [128, T], F32)
        u2scr = sb(stB, "u2scr", [128, T], F32)
        wms = sb(stB, "wms", [128, 2 * D], BF16)
        ccolf = sb(stB, "ccolf", [128, KC], F32)
        scol = sb(stB, "scol", [128, KC], BF16)
        gscr = sb(stB, "gscr", [128, 3 * KC], F32)

        xld = [None, None]
        sqd = [None, None]
        mm_x = [None, None]
        for kc in range(KC):
            pr = kc % 2
            xld[pr] = gdma(s_x[pr], lambda g, kc=kc, pr=pr: g.dma_start(
                out=xstr[:, pr * T:pr * T + T],
                in_=xt[kc * 128:(kc + 1) * 128, :]),
                waits=[mm_x[pr], sqd[pr]])
            sqd[pr] = vop(lambda v, pr=pr: v.tensor_tensor(
                xsq[:, pr * T:pr * T + T], xstr[:, pr * T:pr * T + T],
                xstr[:, pr * T:pr * T + T], ALU.mult), waits=[xld[pr]])
            for nb in range(4):
                pt, sl = ps[nb // 2], slice((nb % 2) * 512, (nb % 2) * 512 + 512)
                mm(lambda t, kc=kc, pr=pr, pt=pt, sl=sl, nb=nb: t.matmul(
                    pt[:, sl], ones128[:, :],
                    xstr[:, pr * T + nb * 512: pr * T + nb * 512 + 512],
                    start=(kc == 0), stop=(kc == KC - 1)),
                    waits=[xld[pr], ones_d] + (pswait(nb // 2) if kc == 0 else []))
            for nb in range(4):
                pt, sl = ps[2 + nb // 2], slice((nb % 2) * 512, (nb % 2) * 512 + 512)
                mm_x[pr] = mm(lambda t, kc=kc, pr=pr, pt=pt, sl=sl, nb=nb: t.matmul(
                    pt[:, sl], ones128[:, :],
                    xsq[:, pr * T + nb * 512: pr * T + nb * 512 + 512],
                    start=(kc == 0), stop=(kc == KC - 1)),
                    waits=[sqd[pr]] + (pswait(2 + nb // 2) if kc == 0 else []),
                    last=True)
        stats_done = mm_x[(KC - 1) % 2]

        for hh in range(2):
            vop(lambda v, hh=hh: v.tensor_scalar(
                mst[:, hh * 1024:hh * 1024 + 1024], ps[hh][:, :], 1.0 / D, None,
                ALU.mult), waits=[stats_done])
            vop(lambda v, hh=hh: v.tensor_tensor(
                uscr[:, hh * 1024:hh * 1024 + 1024],
                mst[:, hh * 1024:hh * 1024 + 1024],
                mst[:, hh * 1024:hh * 1024 + 1024], ALU.mult))
            vop(lambda v, hh=hh: v.scalar_tensor_tensor(
                u2scr[:, hh * 1024:hh * 1024 + 1024], ps[2 + hh][:, :], 1.0 / D,
                uscr[:, hh * 1024:hh * 1024 + 1024], ALU.mult, ALU.subtract))
        var_done = (VP, VP.n)
        lnv = sop(lambda s: s.activation(uscr[:, :], u2scr[:, :], AF.Ln,
                                         bias=constb[:, 0:1]),
                  waits=[var_done, cb_d])
        rstd_done = sop(lambda s: s.activation(rstd[:, :], uscr[:, :], AF.Exp,
                                               scale=-0.5))
        psfree[0] = psfree[1] = var_done
        psfree[2] = psfree[3] = var_done

        wld = [None, None]
        mm_wm = [None, None]
        first_mod = [True]
        for dc in range(96):
            pr = dc % 2
            wld[pr] = gdma(s_wm[pr], lambda g, dc=dc, pr=pr: g.dma_start(
                out=wms[:, pr * D:pr * D + D], in_=wmt[dc]),
                waits=[sc_done, mm_wm[pr]])
            for kc in range(KC):
                w2 = [wld[pr]] if kc == 0 else []
                if kc == 0 and first_mod[0]:
                    w2 += [sc_done] + pswait(2)
                    first_mod[0] = False
                mm_wm[pr] = mm(lambda t, dc=dc, pr=pr, kc=kc: t.matmul(
                    ps[2][:, dc:dc + 1],
                    wms[:, pr * D + kc * 128: pr * D + kc * 128 + 128],
                    scol[:, kc:kc + 1],
                    start=(kc == 0), stop=(kc == KC - 1)),
                    waits=w2, last=(kc == KC - 1))
        mod_mm = mm_wm[1]
        mod_ev = vop(lambda v: v.tensor_copy(modfm[:], ps[2][:, 0:96]),
                     waits=[mod_mm])
        psfree[2] = mod_ev
        capl_d = vop(lambda v: v.tensor_scalar(capl1[:], modfm[:, 16:32], 1.0,
                                               None, ALU.add))
        cbpl_d = vop(lambda v: v.tensor_scalar(cbpl1[:], modfm[:, 64:80], 1.0,
                                               None, ALU.add))
        gate_d = []
        for dst, c0 in ((ga2, 32), (gb2, 80)):
            vop(lambda v, c0=c0: v.tensor_single_scalar(
                gscr[:, 0:KC], modfm[:, c0:c0 + KC], 0.0, ALU.is_ge))
            vop(lambda v: v.tensor_scalar(
                gscr[:, KC:2 * KC], gscr[:, 0:KC], 2.0, -1.0, ALU.mult, ALU.add))
            ab = sop(lambda s, c0=c0: s.activation(
                gscr[:, 0:KC], modfm[:, c0:c0 + KC], AF.Abs),
                waits=[mod_ev, (VP, VP.n)])
            vop(lambda v: v.tensor_single_scalar(
                gscr[:, 2 * KC:3 * KC], gscr[:, 0:KC], 0.25, ALU.max),
                waits=[ab])
            gate_d.append(vop(lambda v, dst=dst: v.tensor_tensor(
                dst[:], gscr[:, KC:2 * KC], gscr[:, 2 * KC:3 * KC], ALU.mult)))
        ga_done, gb_done = gate_d

        h_ld = [None, None]
        h_done = [None, None]
        for kc in range(KC):
            pr = kc % 2
            h_ld[pr] = gdma(s_x[pr], lambda g, kc=kc, pr=pr: g.dma_start(
                out=xstr[:, pr * T:pr * T + T],
                in_=xt[kc * 128:(kc + 1) * 128, :]),
                waits=[mm_x[pr], sqd[pr], h_done[pr]])
            vop(lambda v, pr=pr: v.tensor_tensor(
                uscr[:, :], xstr[:, pr * T:pr * T + T], mst[:, :], ALU.subtract),
                waits=[h_ld[pr], rstd_done, capl_d, mod_ev])
            vop(lambda v: v.tensor_tensor(u2scr[:, :], uscr[:, :], rstd[:, :],
                                          ALU.mult))
            h_done[pr] = vop(lambda v, kc=kc: v.tensor_scalar(
                hbuf[:, kc * T:(kc + 1) * T], u2scr[:, :],
                capl1[:, kc:kc + 1], modfm[:, kc:kc + 1], ALU.mult, ALU.add))
        h_all = h_done[(KC - 1) % 2]

        E.flush()
        stB.close()

        stP = ExitStack()
        stgP = sb(stP, "stgP", [128, 3 * 512], BF16, side="right")
        wsum = sb(stP, "wsum", [128, 800], F32, side="right")

        pre_steps = []
        pre_slot_free = [None, None, None]
        pre_red_last = [None]

        def _pre_chunk(src_ap, col, slot, eng):
            def emit(src_ap=src_ap, col=col, slot=slot, eng=eng):
                ld = gdma(s_st[slot], lambda g: g.dma_start(
                    out=stgP[:, slot * 512:slot * 512 + 512], in_=src_ap),
                    waits=[pre_slot_free[slot]])
                if eng == "v":
                    red = vop(lambda v: v.reduce_sum(
                        wsum[:, col:col + 1],
                        stgP[:, slot * 512:slot * 512 + 512], AX.X,
                        apply_absolute_value=True), waits=[ld])
                else:
                    red = sop(lambda s: s.activation(
                        lnscr[:, 0:512], stgP[:, slot * 512:slot * 512 + 512],
                        AF.Abs, accum_out=wsum[:, col:col + 1]), waits=[ld])
                pre_slot_free[slot] = red
                pre_red_last[0] = red
            return emit

        ci = 0
        for oc in range(FC):
            for qq in range(4):
                pre_steps.append(_pre_chunk(
                    w1t[oc][:, qq * 512:(qq + 1) * 512], 0 + 4 * oc + qq,
                    ci % 3, "s" if ci % 3 == 2 else "v"))
                ci += 1
                pre_steps.append(_pre_chunk(
                    w2t[oc][:, qq * 512:(qq + 1) * 512], 256 + 4 * oc + qq,
                    ci % 3, "s" if ci % 3 == 2 else "v"))
                ci += 1
        for tt3 in range(KC):
            for qq in range(16):
                pre_steps.append(_pre_chunk(
                    w3t[tt3][:, qq * 512:(qq + 1) * 512], 512 + 16 * tt3 + qq,
                    ci % 3, "s" if ci % 3 == 2 else "v"))
                ci += 1
        pre_pos = [0]

        def pump_pre(n):
            k = 0
            while k < n and pre_pos[0] < len(pre_steps):
                pre_steps[pre_pos[0]]()
                pre_pos[0] += 1
                k += 1

        stC = ExitStack()
        Qs = sb(stC, "Qs", [128, NH * NOWN], BF16)
        Kg = sb(stC, "Kg", [128, 2 * T], BF16)
        Vg = sb(stC, "Vg", [128, KC * 512], BF16)
        PT = sb(stC, "PT", [128, 2 * NOWN], BF16)
        wstg = sb(stC, "wstg", [128, 2 * D], BF16)
        wstgv = sb(stC, "wstgv", [128, KC * 256], BF16)
        qf = sb(stC, "qf", [64, NOWN], F32)
        lnscr = sb(stC, "lnscr", [128, NOWN], F32)
        rcp = sb(stC, "rcp", [128, NOWN], F32)
        ra = sb(stC, "ra", [64, NOWN], F32)
        rb = sb(stC, "rb", [64, NOWN], F32)

        state = {"wstg": None, "qf": None, "ropetmp": None, "pt": None,
                 "vstg": None, "kg0": None, "kg1": None, "vg_mm": None,
                 "yv": None}

        def rope(dst_ap, psrc, tok0, ntok, waits):
            ev = sop(lambda s: s.copy(qf[:, 0:ntok], psrc),
                     waits=list(waits) + [state["qf"]])
            d1 = ydma(s_rope, lambda sy: sy.dma_start(out=rb[0:32, 0:ntok],
                                              in_=qf[32:64, 0:ntok]),
                      waits=[ev, state["ropetmp"]])
            d2 = ydma(s_rope, lambda sy: sy.dma_start(out=rb[32:64, 0:ntok],
                                              in_=qf[0:32, 0:ntok]),
                      waits=[ev, state["ropetmp"]])
            a = vop(lambda v: v.tensor_tensor(
                ra[:, 0:ntok], qf[:, 0:ntok],
                cbT[:, tok0:tok0 + ntok], ALU.mult),
                waits=[ev, ld_rc, d1, d2, state["ropetmp"]])
            b = vop(lambda v: v.tensor_tensor(
                qf[:, 0:ntok], rb[:, 0:ntok],
                sbT[:, tok0:tok0 + ntok], ALU.mult), waits=[ld_rs])
            add = vop(lambda v: v.tensor_tensor(
                dst_ap, ra[:, 0:ntok], qf[:, 0:ntok], ALU.add))
            state["qf"] = add
            state["ropetmp"] = add
            return add, ev

        q_done = [None, None]
        wfree = [None, None]
        for oc in range(NH):
            pr = oc % 2
            ld = gdma(s_wq[pr], lambda g, oc=oc, pr=pr: g.dma_start(
                out=wstg[:, pr * D:pr * D + D], in_=wqt[oc]),
                waits=[h_all, wfree[pr]])
            lastmm = None
            for kc in range(KC):
                for nb in range(2):
                    lastmm = mm(lambda t, oc=oc, kc=kc, nb=nb, pr=pr: t.matmul(
                        ps[pr][:, nb * 512:nb * 512 + 512],
                        wstg[:, pr * D + kc * 128: pr * D + kc * 128 + 128],
                        hbuf[:, kc * T + nb * 512: kc * T + nb * 512 + 512],
                        start=(kc == 0), stop=(kc == KC - 1)),
                        waits=([ld, h_all] + pswait(pr))
                        if (kc == 0 and nb == 0) else [],
                        last=(kc == KC - 1 and nb == 1))
            wfree[pr] = lastmm
            ev2 = sop(lambda s, oc=oc, pr=pr: s.copy(
                Qs[64:128, oc * NOWN:(oc + 1) * NOWN], ps[pr][64:128, :]),
                waits=[lastmm])
            state["qev2"] = ev2
            rp, q_ev = rope(Qs[0:64, oc * NOWN:(oc + 1) * NOWN],
                            ps[pr][0:64, :], 0, NOWN, [lastmm])
            q_done[pr] = rp
            psfree[pr] = q_ev
            sq = vop(lambda v, oc=oc: v.tensor_tensor(
                PT[:, 0:NOWN], Qs[:, oc * NOWN:(oc + 1) * NOWN],
                Qs[:, oc * NOWN:(oc + 1) * NOWN], ALU.mult),
                waits=[rp, ev2, state["pt"]])
            mqn = None
            for nb in range(2):
                mqn = mm(lambda t, nb=nb: t.matmul(
                    ps[2][:, nb * 512:nb * 512 + 512], ones128[:, :],
                    PT[:, nb * 512:nb * 512 + 512], start=True, stop=True),
                    waits=([sq] + pswait(2)) if nb == 0 else [], last=(nb == 1))
            red = vop(lambda v, oc=oc: v.reduce_max(
                qmax[:, oc:oc + 1], ps[2][:, :], AX.X), waits=[mqn])
            state["pt"] = red
            psfree[2] = red
            pump_pre(8)

        for g in range(NG):
            v_ev = None
            for sub in range(2):
                vld = gdma(s_wv, lambda g_, g2=g, sub=sub: g_.dma_start(
                    out=wstgv[:], in_=wvt[2 * g2 + sub]),
                    waits=[h_all, state["vstg"]])
                for tc in range(KC):
                    pr = tc % 2
                    lastmm = None
                    for kc in range(KC):
                        lastmm = mm(lambda t, tc=tc, kc=kc, pr=pr: t.matmul(
                            ps[pr][:, 0:256],
                            hbuf[:, kc * T + tc * 128: kc * T + tc * 128 + 128],
                            wstgv[:, kc * 256:(kc + 1) * 256],
                            start=(kc == 0), stop=(kc == KC - 1)),
                            waits=([vld] + pswait(pr) +
                                   ([state["vg_mm"]] if (tc == 0 and sub == 0)
                                    else []))
                            if kc == 0 else [], last=(kc == KC - 1))
                    v_ev = sop(lambda s, tc=tc, pr=pr, sub=sub: s.copy(
                        Vg[:, tc * 512 + sub * 256:tc * 512 + sub * 256 + 256],
                        ps[pr][:, 0:256]), waits=[lastmm])
                    psfree[pr] = v_ev
                state["vstg"] = v_ev
            vg_done = v_ev

            for pair in range(2):
                kg_ready = [[], []]
                nb_edge = [None, None]
                for j2 in range(2):
                    h = g * GH + pair * 2 + j2
                    jg = pair * 2 + j2
                    pp = j2 * 2
                    kpr = h % 2
                    ld = gdma(s_wq[kpr], lambda g_, h=h, kpr=kpr: g_.dma_start(
                        out=wstg[:, kpr * D:kpr * D + D], in_=wkt[h]),
                        waits=[wfree[kpr]])
                    lastmm = None
                    for kc in range(KC):
                        for nb in range(4):
                            pt = ps[pp + nb // 2]
                            sl = slice((nb % 2) * 512, (nb % 2) * 512 + 512)
                            lastmm = mm(lambda t, kc=kc, nb=nb, pt=pt, sl=sl,
                                        kpr=kpr: t.matmul(
                                            pt[:, sl],
                                            wstg[:, kpr * D + kc * 128:
                                                 kpr * D + kc * 128 + 128],
                                            hbuf[:, kc * T + nb * 512:
                                                 kc * T + nb * 512 + 512],
                                            start=(kc == 0),
                                            stop=(kc == KC - 1)),
                                        waits=([ld] + pswait(pp) +
                                               pswait(pp + 1) +
                                               [state["kg%d" % j2]])
                                        if (kc == 0 and nb == 0) else [],
                                        last=(kc == KC - 1 and nb == 3))
                    wfree[kpr] = lastmm
                    ev2a = sop(lambda s, j2=j2, pp=pp: s.copy(
                        Kg[64:128, j2 * T:j2 * T + 1024], ps[pp][64:128, :]),
                        waits=[lastmm, state["kg%d" % j2]])
                    ev2b = sop(lambda s, j2=j2, pp=pp: s.copy(
                        Kg[64:128, j2 * T + 1024:j2 * T + 2048],
                        ps[pp + 1][64:128, :]))
                    r1, k_ev1 = rope(Kg[0:64, j2 * T:j2 * T + 1024],
                                     ps[pp][0:64, :],
                                     0, 1024, [lastmm, state["kg%d" % j2]])
                    r2, k_ev2 = rope(Kg[0:64, j2 * T + 1024:j2 * T + 2048],
                                     ps[pp + 1][0:64, :], 1024, 1024, [])
                    psfree[pp] = k_ev1
                    psfree[pp + 1] = k_ev2
                    kg_ready[j2] = [r1, r2, ev2a, ev2b]
                    sq1 = vop(lambda v, j2=j2: v.tensor_tensor(
                        PT[:, 0:1024], Kg[:, j2 * T:j2 * T + 1024],
                        Kg[:, j2 * T:j2 * T + 1024], ALU.mult),
                        waits=[r1, ev2a, state["pt"]])
                    sq2 = vop(lambda v, j2=j2: v.tensor_tensor(
                        PT[:, 1024:2048], Kg[:, j2 * T + 1024:j2 * T + 2048],
                        Kg[:, j2 * T + 1024:j2 * T + 2048], ALU.mult),
                        waits=[r2, ev2b])
                    tg = 2 - pp
                    mk = None
                    for nb in range(4):
                        pt = ps[tg + nb // 2]
                        sl = slice((nb % 2) * 512, (nb % 2) * 512 + 512)
                        mk = mm(lambda t, nb=nb, pt=pt, sl=sl: t.matmul(
                            pt[:, sl], ones128[:, :],
                            PT[:, nb * 512:nb * 512 + 512],
                            start=True, stop=True),
                            waits=([sq1, sq2] + pswait(tg) + pswait(tg + 1))
                            if nb == 0 else [], last=(nb == 3))
                    state["ptmk"] = mk
                    rk1 = vop(lambda v, tg=tg: v.reduce_max(
                        bcol[:, 0:1], ps[tg][:, :], AX.X), waits=[mk])
                    rk2 = vop(lambda v, tg=tg: v.reduce_max(
                        bcol[:, 1:2], ps[tg + 1][:, :], AX.X))
                    rk3 = vop(lambda v, h=h: v.tensor_tensor(
                        kmax[:, h:h + 1], bcol[:, 0:1], bcol[:, 1:2],
                        ALU.max))
                    psfree[tg] = rk1
                    psfree[tg + 1] = rk2
                    state["pt"] = rk3
                    u = vop(lambda v, h=h: v.tensor_tensor(
                        bcol[:, 2:3], qmax[:, h:h + 1], kmax[:, h:h + 1],
                        ALU.mult))
                    l1 = sop(lambda s: s.activation(bcol[:, 3:4],
                                                    bcol[:, 2:3], AF.Ln),
                             waits=[u])
                    l2 = sop(lambda s: s.activation(bcol[:, 4:5],
                                                    bcol[:, 3:4], AF.Exp,
                                                    scale=0.5,
                                                    bias=constb[:, 2:3]),
                             waits=[cb_d])
                    nb_edge[j2] = vop(lambda v, h=h: v.tensor_scalar(
                        negB[:, h:h + 1], bcol[:, 4:5], -1.0, -1.0,
                        ALU.mult, ALU.add), waits=[l2])
                    pump_pre(12)

                for j2 in range(2):
                    h = g * GH + pair * 2 + j2
                    jg = pair * 2 + j2
                    head_w = kg_ready[j2] + [q_done[0], q_done[1],
                                             state["qev2"]]
                    exp_w0 = [nb_edge[j2], state["ptmk"], state["pt"]]
                    prev_av = None
                    prev_sig = None
                    exd = [[None, None], [None, None]]
                    head_smm = [None]

                    def st_mm(kc):
                        pr = kc % 2
                        w = (head_w if kc == 0 else []) + pswait(pr)
                        for nb in range(2):
                            smm = mm(lambda t, h=h, j2=j2, kc=kc, nb=nb, pr=pr:
                                     t.matmul(
                                         ps[pr][:, nb * 512:nb * 512 + 512],
                                         Kg[:, j2 * T + kc * 128:
                                            j2 * T + kc * 128 + 128],
                                         Qs[:, h * NOWN + nb * 512:
                                            h * NOWN + nb * 512 + 512],
                                         start=True, stop=True),
                                     waits=w if nb == 0 else [],
                                     last=True)
                            head_smm[0] = smm
                            ex = sop(lambda s, kc=kc, pr=pr, h=h, nb=nb:
                                     s.activation(
                                         PT[:, pr * NOWN + nb * 512:
                                            pr * NOWN + nb * 512 + 512],
                                         ps[pr][:, nb * 512:nb * 512 + 512],
                                         AF.Exp, bias=negB[:, h:h + 1],
                                         scale=INV_SQRT_HD),
                                     waits=[smm] +
                                     (exp_w0 if (kc == 0 and nb == 0) else []))
                            exd[pr][nb] = ex
                        psfree[pr] = exd[pr][1]

                    def av_sig(kc):
                        nonlocal prev_av, prev_sig
                        pr = kc % 2
                        for nb in range(2):
                            prev_av = mm(
                                lambda t, kc=kc, jg=jg, nb=nb, pr=pr: t.matmul(
                                    ps[2][:, nb * 512:nb * 512 + 512],
                                    Vg[:, kc * 512 + jg * 128:
                                       kc * 512 + jg * 128 + 128],
                                    PT[:, pr * NOWN + nb * 512:
                                       pr * NOWN + nb * 512 + 512],
                                    start=(kc == 0), stop=(kc == KC - 1)),
                                waits=([exd[pr][nb], vg_done] + pswait(2))
                                if (kc == 0 and nb == 0) else [exd[pr][nb]],
                                last=(kc == KC - 1 and nb == 1))
                            prev_sig = mm(
                                lambda t, kc=kc, nb=nb, pr=pr: t.matmul(
                                    ps[3][:, nb * 512:nb * 512 + 512],
                                    ones128[:, :],
                                    PT[:, pr * NOWN + nb * 512:
                                       pr * NOWN + nb * 512 + 512],
                                    start=(kc == 0), stop=(kc == KC - 1)),
                                waits=pswait(3)
                                if (kc == 0 and nb == 0) else [],
                                last=(kc == KC - 1 and nb == 1))

                    st_mm(0)
                    for kc in range(KC):
                        if kc + 1 < KC:
                            st_mm(kc + 1)
                        av_sig(kc)
                        pump_pre(2)
                    ln1 = sop(lambda s: s.activation(lnscr[:, :], ps[3][:, :],
                                                     AF.Ln),
                              waits=[prev_sig])
                    ln2 = sop(lambda s: s.activation(rcp[:, :], lnscr[:, :],
                                                     AF.Exp, scale=-1.0),
                              waits=[state["yv"]])
                    yv = vop(lambda v, h=h: v.tensor_tensor(
                        ybuf[:, h * NOWN:(h + 1) * NOWN], ps[2][:, :],
                        rcp[:, :], ALU.mult), waits=[ln2, prev_av])
                    psfree[2] = yv
                    psfree[3] = ln1
                    state["yv"] = yv
                    state["kg%d" % j2] = head_smm[0]
                    state["vg_mm"] = prev_av
        y_all = state["yv"]

        if debug:
            ddump(dbg_nb[:, 0:NH], negB[:, :], waits=[(VP, VP.n), (SP, SP.n)])
            ddump(dbg_nb[:, NH:2 * NH], qmax[:, :], waits=[])
            ddump(dbg_nb[:, 2 * NH:3 * NH], kmax[:, :], waits=[])
        pump_pre(10 ** 9)

        def wscale(col0, ncols, nelem, c_sw, c_th, c_nth, swmul, waits):
            red = vop(lambda v: v.reduce_sum(
                bcol[:, 5:6], wsum[:, col0:col0 + ncols], AX.X), waits=waits)
            sm = halve(lambda lo, c: bcol[lo:lo + c, 5:6],
                       (bcol[:, 6:7], bcol[:, 7:8]),
                       ALU.add, bcol[0:1, 0:1], [red], n=1)
            vop(lambda v, nelem=nelem: v.tensor_scalar(
                bcol[0:1, 1:2], bcol[0:1, 0:1], 1.0 / nelem, None,
                ALU.mult), waits=[sm])
            u2 = vop(lambda v: v.tensor_single_scalar(
                bcol[0:1, 2:3], bcol[0:1, 1:2], 1e-5, ALU.max))
            m1 = mm(lambda t: t.matmul(
                ps[3][:, 0:1], ones1f[:, :],
                bcol[0:1, 2:3], start=True, stop=True),
                waits=[u2, ones1_d] + pswait(3), last=True)
            vop(lambda v, c_sw=c_sw, swmul=swmul: v.tensor_scalar(
                thc[:, c_sw:c_sw + 1], ps[3][:, 0:1], swmul, None, ALU.mult),
                waits=[m1])
            vop(lambda v, c_th=c_th: v.tensor_scalar(
                thc[:, c_th:c_th + 1], ps[3][:, 0:1], 0.5, None, ALU.mult))
            c3 = vop(lambda v, c_nth=c_nth: v.tensor_scalar(
                thc[:, c_nth:c_nth + 1], ps[3][:, 0:1], -0.5, None, ALU.mult))
            psfree[3] = c3
            return c3

        th1_d = wscale(0, 256, DFF * D, 0, 1, 2, -0.5,
                       [pre_red_last[0]] + [x for x in pre_slot_free])
        th2_d = wscale(256, 256, DFF * D, 3, 4, 5, -0.5, [])
        th3_d = wscale(512, 256, D * DFF, 6, 7, 8, -0.5, [])
        if debug:
            ddump(dbg_thc[:, 0:9], thc[:, 0:9], waits=[th3_d])

        E.flush()
        stC.close()
        stP.close()

        stD = ExitStack()
        wstgo = sb(stD, "wstgo", [128, 2 * D], BF16)
        xown = sb(stD, "xown", [128, 2 * NOWN], F32)
        x2stg = sb(stD, "x2stg", [128, 2 * NOWN], F32)

        oc_st = [None, None]
        ev_d = [None, None]
        wo_mm = [None, None]
        for oc in range(KC):
            pr = oc % 2
            ld = gdma(s_wo[pr], lambda g, oc=oc, pr=pr: g.dma_start(
                out=wstgo[:, pr * D:pr * D + D], in_=wot[oc]),
                waits=[y_all, wo_mm[pr]])
            xo = ydma(s_xo[pr], lambda sy, oc=oc, pr=pr: sy.dma_start(
                out=xown[:, pr * NOWN:pr * NOWN + NOWN],
                in_=xt[oc * 128:(oc + 1) * 128, 0:NOWN]),
                waits=[ev_d[pr]] + ([y_all] if oc < 2 else []))
            lastmm = None
            for kc in range(KC):
                for nb in range(2):
                    lastmm = mm(lambda t, oc=oc, kc=kc, nb=nb, pr=pr: t.matmul(
                        ps[pr][:, nb * 512:nb * 512 + 512],
                        wstgo[:, pr * D + kc * 128: pr * D + kc * 128 + 128],
                        ybuf[:, kc * NOWN + nb * 512: kc * NOWN + nb * 512 + 512],
                        start=(kc == 0), stop=(kc == KC - 1)),
                        waits=([ld, y_all] + pswait(pr))
                        if (kc == 0 and nb == 0) else [],
                        last=(kc == KC - 1 and nb == 1))
            wo_mm[pr] = lastmm
            ev_d[pr] = vop(lambda v, oc=oc, pr=pr: v.scalar_tensor_tensor(
                x2stg[:, pr * NOWN:pr * NOWN + NOWN], ps[pr][:, :],
                ga2[:, oc:oc + 1], xown[:, pr * NOWN:pr * NOWN + NOWN],
                ALU.mult, ALU.add), waits=[lastmm, xo, ga_done, oc_st[pr]])
            psfree[pr] = ev_d[pr]
            oc_st[pr] = ydma(s_x2w[pr], lambda sy, oc=oc, pr=pr: sy.dma_start(
                out=x2s[oc * 128:(oc + 1) * 128, :],
                in_=x2stg[:, pr * NOWN:pr * NOWN + NOWN]), waits=[ev_d[pr]])
        x2_written = [oc_st[0], oc_st[1]]

        E.flush()
        stD.close()
        stA.close()

        stE = ExitStack()
        xq = sb(stE, "xq", [128, KC * NOWN], BF16)

        stF = ExitStack()
        hf = sb(stF, "hf", [128, KC * NOWN], F32)
        x2c = sb(stF, "x2c", [128, 2 * NOWN], F32)
        cbf = sb(stF, "cbf", [128, 2 * NOWN], BF16)
        csq = sb(stF, "csq", [128, 2 * NOWN], BF16)
        m2b = sb(stF, "m2b", [128, NOWN], F32)
        rstd2b = sb(stF, "rstd2b", [128, NOWN], F32)
        zscr = sb(stF, "zscr", [128, NOWN], F32)
        z2scr = sb(stF, "z2scr", [128, NOWN], F32)
        zint = sb(stF, "zint", [128, NOWN], I32)
        mrun = sb(stF, "mrun", [128, 4 * NOWN], F32)
        mha = sb(stF, "mha", [64, NOWN], F32)
        mhb = sb(stF, "mhb", [64, NOWN], F32)
        rows = sb(stF, "rows", [1, 4 * NOWN], F32)
        txb = sb(stF, "txb", [128, NOWN], F32)
        sxb = sb(stF, "sxb", [128, NOWN], F32)

        def row(i):
            return rows[0:1, i * NOWN:(i + 1) * NOWN]

        ld2 = [None, None]
        cast_d = [None, None]
        mm2a = [None, None]
        mm2b = [None, None]
        for kc in range(KC):
            pr = kc % 2
            ld2[pr] = ydma(s_x2r[pr], lambda sy, kc=kc, pr=pr: sy.dma_start(
                out=x2c[:, pr * NOWN:pr * NOWN + NOWN],
                in_=x2s[kc * 128:(kc + 1) * 128, :]),
                waits=x2_written + [cast_d[pr]])
            c1 = vop(lambda v, pr=pr: v.tensor_copy(
                cbf[:, pr * NOWN:pr * NOWN + NOWN],
                x2c[:, pr * NOWN:pr * NOWN + NOWN]), waits=[ld2[pr], mm2a[pr]])
            cast_d[pr] = c1
            c2 = vop(lambda v, pr=pr: v.tensor_tensor(
                csq[:, pr * NOWN:pr * NOWN + NOWN],
                cbf[:, pr * NOWN:pr * NOWN + NOWN],
                cbf[:, pr * NOWN:pr * NOWN + NOWN], ALU.mult),
                waits=[mm2b[pr]])
            for nb in range(2):
                mm2a[pr] = mm(lambda t, kc=kc, pr=pr, nb=nb: t.matmul(
                    ps[0][:, nb * 512:nb * 512 + 512], ones128[:, :],
                    cbf[:, pr * NOWN + nb * 512: pr * NOWN + nb * 512 + 512],
                    start=(kc == 0), stop=(kc == KC - 1)),
                    waits=([c1] + pswait(0)) if nb == 0 else [],
                    last=(nb == 1))
            for nb in range(2):
                mm2b[pr] = mm(lambda t, kc=kc, pr=pr, nb=nb: t.matmul(
                    ps[1][:, nb * 512:nb * 512 + 512], ones128[:, :],
                    csq[:, pr * NOWN + nb * 512: pr * NOWN + nb * 512 + 512],
                    start=(kc == 0), stop=(kc == KC - 1)),
                    waits=([c2] + pswait(1)) if nb == 0 else [],
                    last=(nb == 1))
        st2a = mm2a[(KC - 1) % 2]
        st2b = mm2b[(KC - 1) % 2]
        vop(lambda v: v.tensor_scalar(m2b[:], ps[0][:, :], 1.0 / D, None,
                                      ALU.mult), waits=[st2a])
        vop(lambda v: v.tensor_tensor(zscr[:], m2b[:], m2b[:], ALU.mult))
        var2 = vop(lambda v: v.scalar_tensor_tensor(
            z2scr[:], ps[1][:, :], 1.0 / D, zscr[:], ALU.mult, ALU.subtract),
            waits=[st2b])
        sop(lambda s: s.activation(zscr[:], z2scr[:], AF.Ln,
                                   bias=constb[:, 0:1]), waits=[var2, cb_d])
        rstd2_d = sop(lambda s: s.activation(rstd2b[:], zscr[:], AF.Exp,
                                             scale=-0.5))
        psfree[0] = var2
        psfree[1] = var2

        hf_d = [None, None]
        mr_d = None
        mmh = [None, None]
        for kc in range(KC):
            pr = kc % 2
            ld2[pr] = ydma(s_x2r[pr], lambda sy, kc=kc, pr=pr: sy.dma_start(
                out=x2c[:, pr * NOWN:pr * NOWN + NOWN],
                in_=x2s[kc * 128:(kc + 1) * 128, :]),
                waits=[mm2a[pr], mm2b[pr], cast_d[pr], hf_d[pr]])
            u1 = vop(lambda v, pr=pr: v.tensor_tensor(
                zscr[:], x2c[:, pr * NOWN:pr * NOWN + NOWN], m2b[:],
                ALU.subtract),
                waits=[ld2[pr], rstd2_d, cbpl_d, (SP, SP.n)])
            vop(lambda v: v.tensor_tensor(z2scr[:], zscr[:], rstd2b[:],
                                          ALU.mult))
            hf_d[pr] = vop(lambda v, kc=kc: v.tensor_scalar(
                hf[:, kc * NOWN:(kc + 1) * NOWN], z2scr[:],
                cbpl1[:, kc:kc + 1], modfm[:, 48 + kc:49 + kc],
                ALU.mult, ALU.add))
            if kc == 0:
                vop(lambda v: v.tensor_copy(mrun[:, 0:NOWN], hf[:, 0:NOWN]))
                mr_d = vop(lambda v: v.tensor_copy(mrun[:, 2 * NOWN:3 * NOWN],
                                                   hf[:, 0:NOWN]))
            else:
                src = ((kc - 1) % 2) * NOWN
                dst = (kc % 2) * NOWN
                vop(lambda v, kc=kc, src=src, dst=dst: v.tensor_tensor(
                    mrun[:, dst:dst + NOWN], mrun[:, src:src + NOWN],
                    hf[:, kc * NOWN:(kc + 1) * NOWN], ALU.max))
                mr_d = vop(lambda v, kc=kc, src=src, dst=dst: v.tensor_tensor(
                    mrun[:, 2 * NOWN + dst:2 * NOWN + dst + NOWN],
                    mrun[:, 2 * NOWN + src:2 * NOWN + src + NOWN],
                    hf[:, kc * NOWN:(kc + 1) * NOWN], ALU.min))
            sqh = vop(lambda v, kc=kc, pr=pr: v.tensor_tensor(
                csq[:, pr * NOWN:pr * NOWN + NOWN],
                hf[:, kc * NOWN:(kc + 1) * NOWN],
                hf[:, kc * NOWN:(kc + 1) * NOWN], ALU.mult),
                waits=[mmh[pr]])
            for nb in range(2):
                mmh[pr] = mm(lambda t, kc=kc, pr=pr, nb=nb: t.matmul(
                    ps[2][:, nb * 512:nb * 512 + 512], ones128[:, :],
                    csq[:, pr * NOWN + nb * 512: pr * NOWN + nb * 512 + 512],
                    start=(kc == 0), stop=(kc == KC - 1)),
                    waits=([sqh] + pswait(2)) if nb == 0 else [],
                    last=(nb == 1))
        sumhf2 = mmh[(KC - 1) % 2]

        mr_off = ((KC - 1) % 2) * NOWN
        mrow_d = halve(lambda lo, c: mrun[lo:lo + c, mr_off:mr_off + NOWN],
                       (mha, mhb), ALU.max, row(0), [mr_d])

        r1_ = sop(lambda s: s.activation(row(1), ps[2][0:1, :], AF.Ln,
                                         scale=1.0 / D, bias=constb[0:1, 1:2]),
                  waits=[sumhf2, cb_d])
        r2_ = sop(lambda s: s.activation(row(2), row(1), AF.Exp, scale=-0.5))
        psfree[2] = r1_
        t1_ = vop(lambda v: v.tensor_single_scalar(row(1), row(0), 1e-30,
                                                   ALU.max),
                  waits=[mrow_d, r2_])
        t2_ = sop(lambda s: s.activation(row(3), row(1), AF.Ln), waits=[t1_])
        tx_d = sop(lambda s: s.activation(row(1), row(3), AF.Exp, scale=-1.0,
                                          bias=constb[0:1, 3:4]))
        s1_ = vop(lambda v: v.tensor_tensor(row(3), row(2), row(0), ALU.mult),
                  waits=[r2_, tx_d])
        s2_ = vop(lambda v: v.tensor_single_scalar(row(0), row(3), 1e-5,
                                                   ALU.max))
        sx_d = vop(lambda v: v.tensor_single_scalar(row(3), row(0),
                                                    1.0 / 127.0, ALU.mult))

        bc_d = []
        for rsrc, dst, dep in ((row(1), txb, tx_d), (row(3), sxb, sx_d)):
            m2_ = None
            for nb in range(2):
                m2_ = mm(lambda t, rsrc=rsrc, nb=nb: t.matmul(
                    ps[3][:, nb * 512:nb * 512 + 512], ones1f[:, :],
                    rsrc[0:1, nb * 512:nb * 512 + 512], start=True, stop=True),
                    waits=([dep, ones1_d] + pswait(3)) if nb == 0 else [],
                    last=(nb == 1))
            bcv = vop(lambda v, dst=dst: v.tensor_copy(dst[:], ps[3][:, :]),
                      waits=[m2_])
            psfree[3] = bcv
            bc_d.append(bcv)
        bc_tx, bc_sx = bc_d

        xq_d = None
        for kc in range(KC):
            vop(lambda v, kc=kc: v.tensor_tensor(
                zscr[:], hf[:, kc * NOWN:(kc + 1) * NOWN], txb[:], ALU.mult),
                waits=[bc_tx, bc_sx])
            vop(lambda v: v.tensor_copy(zint[:], zscr[:]))
            vop(lambda v: v.tensor_tensor(z2scr[:], zscr[:], zint[:],
                                          ALU.subtract))
            vop(lambda v: v.tensor_single_scalar(m2b[:], z2scr[:], 0.5,
                                                 ALU.is_ge))
            vop(lambda v: v.tensor_tensor(rstd2b[:], zint[:], m2b[:], ALU.add))
            vop(lambda v: v.scalar_tensor_tensor(
                zscr[:], z2scr[:], -0.5, rstd2b[:], ALU.is_le, ALU.subtract))
            xq_d = vop(lambda v, kc=kc: v.tensor_tensor(
                xq[:, kc * NOWN:(kc + 1) * NOWN], zscr[:], sxb[:], ALU.mult))
        xq_all = xq_d
        if debug:
            for kc in range(KC):
                ddump(dbg_hf[kc * 128:(kc + 1) * 128, :],
                      hf[:, kc * NOWN:(kc + 1) * NOWN], waits=[xq_all])
                ddump(dbg_xq[kc * 128:(kc + 1) * 128, :],
                      xq[:, kc * NOWN:(kc + 1) * NOWN], waits=[xq_all])
            ddump(dbg_rows[:, 0:NOWN], txb[:, :], waits=[xq_all])
            ddump(dbg_rows[:, NOWN:2 * NOWN], sxb[:, :], waits=[xq_all])

        E.flush()
        stF.close()

        stR = ExitStack()
        hff = sb(stR, "hff", [128, FC * NOWN], BF16, side="right")
        thb = sb(stR, "thb", [128, NOWN], F32, side="right")
        sxh3b = sb(stR, "sxh3b", [128, NOWN], F32, side="right")

        stG0 = ExitStack()
        mrun3 = sb(stG0, "mrun3", [128, 2 * NOWN], BF16)

        stG = ExitStack()
        stg = sb(stG, "stg", [128, 3 * NOWN], F32)
        tern = sb(stG, "tern", [128, 2 * D], BF16)
        sA = sb(stG, "sA", [128, 2 * NOWN], BF16)
        sB = sb(stG, "sB", [128, 2 * NOWN], BF16)
        slscr = sb(stG, "slscr", [128, NOWN], BF16)

        e3_start = [(VP, VP.n)]
        if debug:
            e3_start.append((s_dbg, s_dbg.n))
        slot_free = [list(e3_start) for _ in range(3)]
        e3_hs = [0]

        mm_b1 = [None]
        mm_b1h = [None]
        mm_b2 = [None]
        mm_b2h = [None]
        sil_d = [None]
        stt_d = [None]
        hsq_d = [None]
        mr3_d = [None]
        sq_mm = [None]
        q1h = [None, None]
        q2h = [None, None]

        comb_d = [None, None]

        def e3_load_quant(oc):
            for wsel, base, thcol, nthcol, mmh, qh, wt in (
                    (0, 0, 1, 2, (mm_b1h, mm_b1), q1h, w1t),
                    (1, D, 4, 5, (mm_b2h, mm_b2), q2h, w2t)):
                for hh2 in range(2):
                    hs = e3_hs[0]
                    e3_hs[0] = (hs + 1) % 3
                    pa = hh2 * NOWN
                    ssl = slice(hs * NOWN, hs * NOWN + NOWN)
                    tsl = slice(base + hh2 * NOWN, base + hh2 * NOWN + NOWN)
                    ldw = gdma(s_st[hs], lambda g, oc=oc, ssl=ssl, wt=wt,
                               hh2=hh2: g.dma_start(
                        out=stg[:, ssl],
                        in_=wt[oc][:, hh2 * NOWN:hh2 * NOWN + NOWN]),
                        waits=slot_free[hs] + [th1_d, th2_d, th3_d])
                    s1 = sop(lambda s, ssl=ssl, pa=pa, nthcol=nthcol:
                             s.activation(
                                 sA[:, pa:pa + NOWN], stg[:, ssl], AF.Sign,
                                 bias=thc[:, nthcol:nthcol + 1]),
                             waits=[ldw, comb_d[hh2]])
                    s2 = sop(lambda s, ssl=ssl, pa=pa, thcol=thcol:
                             s.activation(
                                 sB[:, pa:pa + NOWN], stg[:, ssl], AF.Sign,
                                 bias=thc[:, thcol:thcol + 1]))
                    q = vop(lambda v, tsl=tsl, pa=pa: v.tensor_tensor(
                        tern[:, tsl], sA[:, pa:pa + NOWN],
                        sB[:, pa:pa + NOWN], ALU.add),
                        waits=[s2, mmh[hh2][0]])
                    qh[hh2] = q
                    comb_d[hh2] = q
                    slot_free[hs] = [q]

        def e3_sumh2(oc):
            for nb in range(2):
                sq_mm[0] = mm(lambda t, oc=oc, nb=nb: t.matmul(
                    ps[2][:, nb * 512:nb * 512 + 512], ones128[:, :],
                    slscr[:, nb * 512:nb * 512 + 512],
                    start=(oc == 0), stop=(oc == FC - 1)),
                    waits=([hsq_d[0]] + pswait(2)) if nb == 0 else [],
                    last=(nb == 1))

        e3_load_quant(0)
        for oc in range(FC):
            qq1 = (q1h[0], q1h[1])
            qq2 = (q2h[0], q2h[1])
            for kc in range(KC):
                for nb in range(2):
                    e = mm(lambda t, kc=kc, nb=nb: t.matmul(
                        ps[0][:, nb * 512:nb * 512 + 512],
                        tern[:, kc * 128:(kc + 1) * 128],
                        xq[:, kc * NOWN + nb * 512: kc * NOWN + nb * 512 + 512],
                        start=(kc == 0), stop=(kc == KC - 1)),
                        waits=([qq1[0], xq_all] + pswait(0))
                        if (kc == 0 and nb == 0) else
                        ([qq1[1]] if (kc == KC // 2 and nb == 0) else []),
                        last=(nb == 1 and kc in (KC // 2 - 1, KC - 1)))
                    if kc == KC // 2 - 1 and nb == 1:
                        mm_b1h[0] = e
            mm_b1[0] = e
            for kc in range(KC):
                for nb in range(2):
                    e = mm(lambda t, kc=kc, nb=nb: t.matmul(
                        ps[1][:, nb * 512:nb * 512 + 512],
                        tern[:, D + kc * 128: D + kc * 128 + 128],
                        xq[:, kc * NOWN + nb * 512: kc * NOWN + nb * 512 + 512],
                        start=(kc == 0), stop=(kc == KC - 1)),
                        waits=([qq2[0]] + pswait(1))
                        if (kc == 0 and nb == 0) else
                        ([qq2[1]] if (kc == KC // 2 and nb == 0) else []),
                        last=(nb == 1 and kc in (KC // 2 - 1, KC - 1)))
                    if kc == KC // 2 - 1 and nb == 1:
                        mm_b2h[0] = e
            mm_b2[0] = e
            if oc > 0:
                e3_sumh2(oc - 1)
            if oc + 1 < FC:
                e3_load_quant(oc + 1)
            sil_d[0] = sop(lambda s: s.activation(slscr[:, :], ps[0][:, :],
                                                  AF.Silu, scale=thc[:, 0:1]),
                           waits=[mm_b1[0], th1_d, sq_mm[0], mr3_d[0]])
            stt_d[0] = vop(lambda v, oc=oc: v.scalar_tensor_tensor(
                hff[:, oc * NOWN:(oc + 1) * NOWN], ps[1][:, :], thc[:, 3:4],
                slscr[:, :], ALU.mult, ALU.mult), waits=[mm_b2[0], sil_d[0]])
            psfree[0] = sil_d[0]
            psfree[1] = stt_d[0]
            if debug:
                ddump(dbg_hff[oc * 128:(oc + 1) * 128, :],
                      hff[:, oc * NOWN:(oc + 1) * NOWN], waits=[stt_d[0]])
            hsq_d[0] = vop(lambda v, oc=oc: v.tensor_tensor(
                slscr[:, :], hff[:, oc * NOWN:(oc + 1) * NOWN],
                hff[:, oc * NOWN:(oc + 1) * NOWN], ALU.mult),
                waits=[sq_mm[0], sil_d[0], stt_d[0]])
            if oc == 0:
                mr3_d[0] = vop(lambda v: v.tensor_copy(mrun3[:, 0:NOWN],
                                                       slscr[:, :]))
            else:
                src = ((oc - 1) % 2) * NOWN
                dst = (oc % 2) * NOWN
                mr3_d[0] = vop(lambda v, src=src, dst=dst: v.tensor_tensor(
                    mrun3[:, dst:dst + NOWN], mrun3[:, src:src + NOWN],
                    slscr[:, :], ALU.max))
        e3_sumh2(FC - 1)
        sq_mm = sq_mm[0]
        mr3_d = mr3_d[0]

        E.flush()
        stG.close()

        stGt = ExitStack()
        mha3 = sb(stGt, "mha3", [64, NOWN], F32)
        mhb3 = sb(stGt, "mhb3", [64, NOWN], F32)
        rows3 = sb(stGt, "rows3", [1, 4 * NOWN], F32)

        def row3(i):
            return rows3[0:1, i * NOWN:(i + 1) * NOWN]

        mr3_off = ((FC - 1) % 2) * NOWN
        mrow3_d = halve(lambda lo, c: mrun3[lo:lo + c, mr3_off:mr3_off + NOWN],
                        (mha3, mhb3), ALU.max, row3(0), [mr3_d])
        sop(lambda s: s.activation(row3(1), ps[2][0:1, :], AF.Ln,
                                   scale=1.0 / DFF, bias=constb[0:1, 1:2]),
            waits=[sq_mm])
        rh2 = sop(lambda s: s.activation(row3(2), row3(1), AF.Exp, scale=-0.5))
        psfree[2] = (SP, SP.n)
        h1_ = vop(lambda v: v.tensor_single_scalar(row3(1), row3(0), 1e-30,
                                                   ALU.max),
                  waits=[mrow3_d, rh2])
        h2_ = sop(lambda s: s.activation(row3(3), row3(1), AF.Ln), waits=[h1_])
        th_d = sop(lambda s: s.activation(row3(1), row3(3), AF.Exp, scale=-0.5,
                                          bias=constb[0:1, 3:4]))
        m_d = sop(lambda s: s.activation(row3(0), row3(3), AF.Exp, scale=0.5))
        g1_ = vop(lambda v: v.tensor_tensor(row3(3), row3(2), row3(0),
                                            ALU.mult), waits=[rh2, th_d, m_d])
        g2_ = vop(lambda v: v.tensor_single_scalar(row3(0), row3(3), 1e-5,
                                                   ALU.max))
        g3_ = vop(lambda v: v.tensor_scalar(row3(3), row3(0), thc[0:1, 6:7],
                                            None, ALU.mult), waits=[th3_d])
        sxh_d = vop(lambda v: v.tensor_single_scalar(row3(0), row3(3),
                                                     1.0 / 127.0, ALU.mult))
        for rsrc, dst, dep in ((row3(1), thb, th_d), (row3(0), sxh3b, sxh_d)):
            m2_ = None
            for nb in range(2):
                m2_ = mm(lambda t, rsrc=rsrc, nb=nb: t.matmul(
                    ps[3][:, nb * 512:nb * 512 + 512], ones1f[:, :],
                    rsrc[0:1, nb * 512:nb * 512 + 512], start=True, stop=True),
                    waits=([dep, sxh_d] + pswait(3)) if nb == 0 else [],
                    last=(nb == 1))
            bcv = vop(lambda v, dst=dst: v.tensor_copy(dst[:], ps[3][:, :]),
                      waits=[m2_])
            psfree[3] = bcv
        bc_all = psfree[3]

        E.flush()
        stGt.close()
        stG0.close()
        stE.close()

        stH = ExitStack()
        zscr4 = sb(stH, "zscr4", [128, NOWN], F32)
        z2scr4 = sb(stH, "z2scr4", [128, 2 * NOWN], F32)
        zint4 = sb(stH, "zint4", [128, 2 * NOWN], I32)
        w3stg = sb(stH, "w3stg", [128, 2 * 1024], F32)
        tern3 = sb(stH, "tern3", [128, 2 * 4096], BF16)
        sA4 = sb(stH, "sA4", [128, 2048], BF16)
        sB4 = sb(stH, "sB4", [128, 2048], BF16)
        sA4w = sb(stH, "sA4w", [128, 1024], BF16)
        sB4w = sb(stH, "sB4w", [128, 1024], BF16)
        x2c4 = sb(stH, "x2c4", [128, NOWN], F32)
        outst = sb(stH, "outst", [128, NOWN], F32)

        mm_oh = [None, None]
        mm_o = [None, None]
        ev4 = [None, None]
        st4 = [None]
        tq3 = [None, None]
        ld3_free = [None, None]

        s4_d = [None]

        def e4_quant_half(oc, hhf):
            for qq in range(4):
                sl = qq % 2
                ld = gdma(s_w3[sl], lambda g, oc=oc, hhf=hhf, qq=qq, sl=sl:
                          g.dma_start(
                    out=w3stg[:, sl * 1024:sl * 1024 + 1024],
                    in_=w3t[oc][:, hhf * 4096 + qq * 1024:
                                hhf * 4096 + qq * 1024 + 1024]),
                    waits=[th3_d, ld3_free[sl], bc_all])
                sw1_ = sop(lambda s, sl=sl: s.activation(
                    sA4w[:, :], w3stg[:, sl * 1024:sl * 1024 + 1024],
                    AF.Sign, bias=thc[:, 8:9]), waits=[ld, s4_d[0]])
                sw2_ = sop(lambda s, sl=sl: s.activation(
                    sB4w[:, :], w3stg[:, sl * 1024:sl * 1024 + 1024],
                    AF.Sign, bias=thc[:, 7:8]))
                tq3[hhf] = vop(lambda v, hhf=hhf, qq=qq: v.tensor_tensor(
                    tern3[:, hhf * 4096 + qq * 1024:
                          hhf * 4096 + qq * 1024 + 1024],
                    sA4w[:, :], sB4w[:, :], ALU.add),
                    waits=[sw2_, mm_oh[hhf]])
                s4_d[0] = tq3[hhf]
                ld3_free[sl] = tq3[hhf]

        e4_quant_half(0, 0)
        e4_quant_half(0, 1)

        hq_d = None
        hq_w0 = [bc_all] + ([(s_dbg, s_dbg.n)] if debug else [])
        sh_q = [None] * FC

        def hq_front(oc):
            pr = oc % 2
            vop(lambda v, oc=oc: v.tensor_tensor(
                zscr4[:], hff[:, oc * NOWN:(oc + 1) * NOWN], thb[:],
                ALU.mult), waits=(hq_w0 if oc == 0 else []) +
                ([sh_q[oc - 2]] if oc >= 2 else []))
            vop(lambda v, pr=pr: v.tensor_copy(
                zint4[:, pr * NOWN:pr * NOWN + NOWN], zscr4[:]))
            rr = vop(lambda v, pr=pr: v.tensor_tensor(
                z2scr4[:, pr * NOWN:pr * NOWN + NOWN], zscr4[:],
                zint4[:, pr * NOWN:pr * NOWN + NOWN], ALU.subtract))
            sh1 = sop(lambda s, pr=pr: s.activation(
                sA4[:, pr * NOWN:pr * NOWN + NOWN],
                z2scr4[:, pr * NOWN:pr * NOWN + NOWN],
                AF.Sign, bias=constb[:, 4:5]), waits=[rr, cb_d])
            sh_q[oc] = sop(lambda s, pr=pr: s.activation(
                sB4[:, pr * NOWN:pr * NOWN + NOWN],
                z2scr4[:, pr * NOWN:pr * NOWN + NOWN],
                AF.Sign, bias=constb[:, 5:6]))

        hq_done = [None] * FC

        def hq_back(oc):
            nonlocal hq_d
            pr = oc % 2
            vop(lambda v, pr=pr: v.tensor_tensor(
                outst[:, :], sA4[:, pr * NOWN:pr * NOWN + NOWN],
                sB4[:, pr * NOWN:pr * NOWN + NOWN], ALU.add),
                waits=[sh_q[oc]])
            hq_d = vop(lambda v, oc=oc, pr=pr: v.scalar_tensor_tensor(
                hff[:, oc * NOWN:(oc + 1) * NOWN], outst[:, :], -0.5,
                zint4[:, pr * NOWN:pr * NOWN + NOWN],
                ALU.mult, ALU.subtract))
            hq_done[oc] = hq_d

        hq_front(0)
        for oc in range(FC):
            if oc + 1 < FC:
                hq_front(oc + 1)
            hq_back(oc)
        hq_all = hq_d
        if debug:
            for oc in range(FC):
                ddump(dbg_hq[oc * 128:(oc + 1) * 128, :],
                      hff[:, oc * NOWN:(oc + 1) * NOWN], waits=[hq_all])
            ddump(dbg_rows3[:, 0:NOWN], thb[:, :], waits=[hq_all])
            ddump(dbg_rows3[:, NOWN:2 * NOWN], sxh3b[:, :], waits=[hq_all])

        for oc in range(KC):
            pr = oc % 2
            x2ld = ydma(s_x2r[0], lambda sy, oc=oc: sy.dma_start(
                out=x2c4[:, :],
                in_=x2s[oc * 128:(oc + 1) * 128, :]),
                waits=[ev4[0], ev4[1]] + ([(VP, VP.n)] if oc < 2 else []))
            lastmm = None
            for hhf in range(2):
                tq_now = tq3[hhf]
                for kc2 in range(FC // 2):
                    kc = hhf * (FC // 2) + kc2
                    for nb in range(2):
                        lastmm = mm(lambda t, kc=kc, nb=nb, pr=pr, hhf=hhf:
                                    t.matmul(
                                        ps[pr][:, nb * 512:nb * 512 + 512],
                                        tern3[:, hhf * 4096 + kc2 * 128:
                                              hhf * 4096 + kc2 * 128 + 128]
                                        if False else
                                        tern3[:, (kc % (FC // 2)) * 128 +
                                              hhf * 4096:
                                              (kc % (FC // 2)) * 128 +
                                              hhf * 4096 + 128],
                                        hff[:, kc * NOWN + nb * 512:
                                            kc * NOWN + nb * 512 + 512],
                                        start=(kc == 0), stop=(kc == FC - 1)),
                                    waits=(([tq_now] + pswait(pr))
                                           if (kc2 == 0 and nb == 0) else []) +
                                    ([hq_done[kc]]
                                     if (oc == 0 and nb == 0) else []),
                                    last=(kc2 == FC // 2 - 1 and nb == 1))
                mm_oh[hhf] = lastmm
                if hhf == 0 and oc + 1 < KC:
                    e4_quant_half(oc + 1, 0)
            if oc + 1 < KC:
                e4_quant_half(oc + 1, 1)
            mm_o[pr] = lastmm
            u_d = vop(lambda v, oc=oc, pr=pr: v.scalar_tensor_tensor(
                zscr4[:], ps[pr][:, :], gb2[:, oc:oc + 1], sxh3b[:],
                ALU.mult, ALU.mult), waits=[lastmm, gb_done, x2ld, st4[0]])
            ev4[pr] = vop(lambda v, oc=oc: v.tensor_tensor(
                outst[:, :], zscr4[:], x2c4[:, :], ALU.add))
            psfree[pr] = ev4[pr]
            st4[0] = ydma(s_out[0], lambda sy, oc=oc: sy.dma_start(
                out=outT[oc * 128:(oc + 1) * 128, :],
                in_=outst[:, :]), waits=[ev4[pr]])

        def final_waits(sy):
            for sm in all_dma_sems:
                if sm.n > 0:
                    sy.wait_ge(sm.h, sm.n)
        E.plain("sync", final_waits)
        E.flush()
        stH.close()
        stR.close()

    return nc


_cache = {}


def _tile_oc(w, noc):
    dout, din = w.shape
    kc = din // 128
    a = w.reshape(noc, 128, kc, 128).transpose(0, 3, 2, 1)
    return np.ascontiguousarray(a.reshape(noc, 128, din))


def _rope_tables(own0):
    inv = 1.0 / (10000.0 ** (np.arange(0, HD, 2, dtype=np.float64) / HD))
    t = np.arange(T, dtype=np.float64)
    ang = np.outer(inv, t)
    c = np.cos(ang)
    s = np.sin(ang)
    s2 = np.concatenate([-s[0:32], s[32:64]], axis=0)
    perm = np.concatenate([np.arange(own0, T), np.arange(0, own0)])
    return (np.ascontiguousarray(c[:, perm]).astype(np.float32),
            np.ascontiguousarray(s2[:, perm]).astype(np.float32))


def prepare_in_maps(L0_attn_x, L0_attn_wq, L0_attn_wk, L0_attn_wv, L0_attn_wo,
                    L0_ffn_w1, L0_ffn_w2, L0_ffn_w3, L0_c, L0_wmod):
    wq = np.asarray(L0_attn_wq, np.float32)
    wk = np.asarray(L0_attn_wk, np.float32)
    wv = np.asarray(L0_attn_wv, np.float32)
    wo = np.asarray(L0_attn_wo, np.float32)
    w1 = np.asarray(L0_ffn_w1, np.float32)
    w2 = np.asarray(L0_ffn_w2, np.float32)
    w3 = np.asarray(L0_ffn_w3, np.float32)
    wm = np.asarray(L0_wmod, np.float32)
    x = np.asarray(L0_attn_x, np.float32)
    c = np.asarray(L0_c, np.float32)

    shared = {
        "wqt": _tile_oc(wq, KC),
        "wkt": _tile_oc(wk, KC),
        "wot": _tile_oc(wo, KC),
        "w1t": _tile_oc(w1, FC),
        "w2t": _tile_oc(w2, FC),
        "w3t": _tile_oc(w3, KC),
    }
    wvr = wv.reshape(2 * NG, 256, KC, 128).transpose(0, 3, 2, 1)
    shared["wvt"] = np.ascontiguousarray(wvr.reshape(2 * NG, 128, KC * 256))
    wmr = wm.reshape(KC, 128, 96, 128).transpose(2, 1, 0, 3)
    shared["wmt"] = np.ascontiguousarray(wmr.reshape(96, 128, D))

    in_maps = []
    for core in range(8):
        b, p = core // 2, core % 2
        own0 = p * NOWN
        xtb = x[b].T
        xrot = np.ascontiguousarray(
            np.concatenate([xtb[:, own0:], xtb[:, :own0]], axis=1))
        rc, rs = _rope_tables(own0)
        m = dict(shared)
        m["xt"] = xrot
        m["cct"] = np.ascontiguousarray(c[b].reshape(KC, 128).T)
        m["rct"] = rc
        m["rst"] = rs
        in_maps.append(m)
    return in_maps


def kernel(**inputs):
    if "nc" not in _cache:
        _cache["nc"] = build_nc(detect_races=True)
    nc = _cache["nc"]
    in_maps = prepare_in_maps(**inputs)
    res = run_bass_kernel_spmd(nc, in_maps, core_ids=list(range(8)))
    out = np.empty((B, T, D), np.float32)
    for core in range(8):
        b, p = core // 2, core % 2
        out[b, p * NOWN:(p + 1) * NOWN, :] = res.results[core]["outT"].T
    return out
